# version 10
# baseline (speedup 1.0000x reference)
import numpy as np
from contextlib import ExitStack
import concourse.bass as bass
import concourse.mybir as mybir
from concourse.bass_utils import run_bass_kernel_spmd

F32 = mybir.dt.float32
ALU = mybir.AluOpType
AF = mybir.ActivationFunctionType
AX = mybir.AxisListType

B, T, D, NH, HD, II = 8, 2048, 1024, 16, 64, 3072
NTC = T // 128
NDC = D // 128
NIC = II // 128
MAGIC = float(1.5 * 2 ** 23)
QMAXR = float(np.float32(1.0) / np.float32(32767.0))
R7 = float(np.float32(1.0) / np.float32(7.0))
EPS = 1e-05
C16 = float(np.float32(1.0 / 16.0) * np.float32(1.00001))


class S:
    def __init__(self, nc, ctx, name):
        self.h = ctx.enter_context(nc.semaphore(name))
        self.n = 0
    def inc(self, ins, k=1):
        ins.then_inc(self.h, k)
        self.n += k
        return self.n


class Ring:
    def __init__(self):
        self.hist = []
    def push(self, v):
        self.hist.append(v)
    def old(self, back):
        return self.hist[-back] if len(self.hist) >= back else None


class KB:
    def __init__(self, dbg=()):
        self.dbg = set(dbg)
        nc = self.nc = bass.Bass("TRN2", target_bir_lowering=False)
        self.ctx = ExitStack()
        d = self.dram = {}
        def din(name, shape):
            d[name] = nc.dram_tensor(name, shape, F32, kind="ExternalInput").ap()
        def dint(name, shape):
            d[name] = nc.dram_tensor(name, shape, F32).ap()
        din("xin", [T, D])
        for w in ("wqT", "wkT", "wvT", "woT"):
            din(w, [D, D])
        din("bqc", [128, NDC]); din("bkc", [128, NDC]); din("bvc", [128, NDC])
        din("bob", [128, D]); din("lnwb", [128, D]); din("lnbb", [128, D])
        din("c2t", [128, T]); din("s2t", [128, T])
        din("idn", [128, 128]); din("onesr", [1, 128]); din("ones2", [128, 2])
        din("wup", [II, D]); din("wgate", [II, D]); din("wdown", [D, II])
        dint("v_h", [T, D]); dint("avT_h", [D, T]); dint("x2_h", [T, D])
        dint("up_h", [II, T]); dint("gate_h", [II, T]); dint("g2_h", [II, T]); dint("o_h", [II, T])
        dint("wupT_h", [D, II]); dint("wgateT_h", [D, II]); dint("wdT_h", [II, D])
        dint("wqoi_h", [72 * 128, 1024])
        for i in range(4):
            dint(f"ari{i}", [1, 2]); dint(f"aro{i}", [1, 2])
        d["out"] = nc.dram_tensor("out", [T, D], F32, kind="ExternalOutput").ap()
        self.tapd = {}

        ctx = self.ctx
        sb = lambda *a: ctx.enter_context(nc.sbuf_tensor(*a))
        self.BIGA = sb("BIGA", [128, NDC * T], F32)
        self.CHK = sb("CHK", [128, 2 * D], F32)
        self.IDN = sb("IDNs", [128, 128], F32)
        self.ONESRt = sb("ONESRs", [128, 128], F32); self.ONESR = self.ONESRt[0:1, :]; self.ONESR64 = self.ONESRt[64:65, :]
        self.ONES2 = sb("ONES2s", [128, 2], F32)
        self.GMX = sb("GMX", [128, 8], F32)
        self.MCOL = sb("MCOL", [128, 2], F32)
        self.MAGC = sb("MAGC", [128, 2], F32)
        self.TROWt = sb("TROW", [128, 128], F32); self.TROW = self.TROWt[0:2, :]
        self.SCLt = sb("SCL", [128, 16], F32); self.SCL = self.SCLt[0:1, :]
        self.SCLB = sb("SCLB", [128, 12], F32)
        self.taps = {}
        for name, shape in (("xn", [T, D]), ("xnT", [128, NDC * T]),
                            ("vtok", [T, D]), ("qrope", [128, T]), ("krope", [128, T]),
                            ("avT", [D, T]), ("x2", [T, D]), ("x2qT", [128, NDC * T]),
                            ("wupTq", [D, II]), ("wdTq", [II, D]),
                            ("up", [II, T]), ("gate", [II, T]), ("g2", [II, T]), ("o", [II, T]),
                            ("sup", [1, 16]), ("att0", [128, 4 * T])):
            if name in self.dbg:
                self.taps[name] = nc.dram_tensor("tap_" + name, shape, F32, kind="ExternalOutput").ap()

        mk = lambda name: S(nc, ctx, name)
        self.ld = mk("ld")
        self.gld = mk("gld")
        self.stc = mk("stc")
        self.vec = mk("vec")
        self.scl = mk("scl")
        self.pe = mk("pe")
        self.gp = mk("gp")
        self.bar = mk("bar")
        self.cc = mk("cc")

        self._sems = {}
        self._sem_pool = [S(nc, ctx, f"aux{i}") for i in range(48)]
        self.block = ctx.enter_context(nc.Block())
        self._worked = {"vector": False, "scalar": False, "gpsimd": False}

    def sem(self, name):
        if name not in self._sems:
            self._sems[name] = self._sem_pool.pop()
        return self._sems[name]

    def ssem(self, name):
        st = self.sem(name)
        if not hasattr(self, "_store_sems"):
            self._store_sems = set()
        self._store_sems.add(st)
        return st

    def w(self, eng):
        self._worked[eng] = True

    def barrier(self):
        block, nc = self.block, self.nc
        bar, stc = self.bar, self.stc
        n_inc = 0
        if self._worked["vector"]:
            @block.vector
            def _(vector):
                bar.inc(vector.tensor_copy(self.MCOL[:, 1:2], self.MCOL[:, 0:1]))
            n_inc += 1
        if self._worked["scalar"]:
            @block.scalar
            def _(scalar):
                bar.inc(scalar.copy(self.MCOL[0:1, 1:2], self.MCOL[0:1, 0:1]))
            n_inc += 1
        if self._worked["gpsimd"]:
            stn = stc.n
            store_waits = [(ss.h, ss.n) for ss in getattr(self, "_store_sems", set()) if ss.n]
            @block.gpsimd
            def _(gpsimd):
                if stn:
                    gpsimd.wait_ge(stc.h, stn)
                for h, n in store_waits:
                    gpsimd.wait_ge(h, n)
                bar.inc(gpsimd.tensor_copy(self.MCOL[:, 1:2], self.MCOL[:, 0:1]))
            n_inc += 1
        target = bar.n
        @block.sync
        def _(sync):
            sync.wait_ge(bar.h, target)
        @block.tensor
        def _(tensor):
            tensor.wait_ge(bar.h, target)
        @block.vector
        def _(vector):
            vector.wait_ge(bar.h, target)
        @block.scalar
        def _(scalar):
            scalar.wait_ge(bar.h, target)
        @block.gpsimd
        def _(gpsimd):
            gpsimd.wait_ge(bar.h, target)
        self._worked = {"vector": False, "scalar": False, "gpsimd": False}

    def p0_consts(self):
        block, d = self.block, self.dram
        cs = self.sem("const")
        @block.sync
        def _(sync):
            cs.inc(sync.dma_start(out=self.IDN[:], in_=d["idn"]), 16)
            cs.inc(sync.dma_start(out=self.ONESR, in_=d["onesr"]), 16)
            cs.inc(sync.dma_start(out=self.ONESR64, in_=d["onesr"]), 16)
            cs.inc(sync.dma_start(out=self.ONES2[:], in_=d["ones2"]), 16)
        self.ld_consts = cs.n
        @block.vector
        def _(vector):
            vector.memset(self.GMX[:], 0.0)
            self.vec.inc(vector.memset(self.MAGC[:], MAGIC))
        self.p0_vec = self.vec.n
        self.w("vector")

    def p1_ln1(self):
        nc, block, d = self.nc, self.block, self.dram
        ld, vec, scl, pe = self.ld, self.vec, self.scl, self.pe
        ctx = ExitStack()
        sb = lambda *a: ctx.enter_context(nc.sbuf_tensor(*a))
        LNWB = sb("LNWB", [128, D], F32)
        LNBB = sb("LNBB", [128, D], F32)
        DB = sb("DB", [128, D], F32)
        JUNK = sb("JUNK", [128, D], F32)
        COLS = sb("COLS", [128, 8], F32)
        XN = sb("XN", [128, D], F32)
        PT = ctx.enter_context(nc.psum_tensor("PT1", [128, 2 * D], F32))

        cs = self.sem("const")
        @block.sync
        def _(sync):
            cs.inc(sync.dma_start(out=LNWB[:], in_=d["lnwb"]), 16)
            cs.inc(sync.dma_start(out=LNBB[:], in_=d["lnbb"]), 16)
        ld_tabs = cs.n

        for tc in range(NTC):
            par = tc % 2
            xc = self.CHK[:, par * D:(par + 1) * D]
            vec_need = self.vec.n if tc >= 2 else None
            chks = self.sem(f"chk{par}")
            @block.sync
            def _(sync):
                if vec_need is not None:
                    sync.wait_ge(vec.h, vec_need)
                chks.inc(sync.dma_start(out=xc, in_=d["xin"][tc * 128:(tc + 1) * 128, :]), 16)
            ldn = chks.n

            xn_tap_wait = getattr(self, "_xn_tap_stc", None)
            @block.vector
            def _(vector):
                vector.wait_ge(chks.h, ldn)
                vector.wait_ge(self.sem("const").h, ld_tabs)
                if xn_tap_wait is not None:
                    vector.wait_ge(self.stc.h, xn_tap_wait)
                vector.tensor_reduce(COLS[:, 0:1], xc, AX.X, ALU.add)
                vector.drain()
                vector.tensor_scalar_mul(COLS[:, 1:2], COLS[:, 0:1], float(1.0 / D))
                vector.drain()
                vec.inc(vector.tensor_scalar_sub(DB[:], xc, COLS[:, 1:2]))
            vecn1 = vec.n
            @block.scalar
            def _(scalar):
                scalar.wait_ge(vec.h, vecn1)
                scl.inc(scalar.activation(JUNK[:], DB[:], AF.Square, accum_out=COLS[:, 2:3]))
            scln1 = scl.n
            @block.vector
            def _(vector):
                vector.wait_ge(scl.h, scln1)
                vec.inc(vector.tensor_scalar(COLS[:, 3:4], COLS[:, 2:3], float(1.0 / D), float(EPS), ALU.mult, ALU.add))
            vecn2 = vec.n
            @block.scalar
            def _(scalar):
                scalar.wait_ge(vec.h, vecn2)
                scl.inc(scalar.sqrt(COLS[:, 4:5], COLS[:, 3:4]))
            scln2 = scl.n
            @block.vector
            def _(vector):
                vector.wait_ge(scl.h, scln2)
                vector.reciprocal(COLS[:, 5:6], COLS[:, 4:5])
                vector.drain()
                vector.scalar_tensor_tensor(XN[:], DB[:], COLS[:, 5:6], LNWB[:], ALU.mult, ALU.mult)
                vector.drain()
                vec.inc(vector.tensor_tensor(XN[:], XN[:], LNBB[:], ALU.add))
            vecn3 = vec.n
            ppar = tc % 2
            pslice = PT[:, ppar * D:(ppar + 1) * D]
            pe_prev = pe.n
            @block.tensor
            def _(tensor):
                tensor.wait_ge(vec.h, vecn3)
                ins = None
                for dc in range(NDC):
                    ins = tensor.transpose(pslice[:, dc * 128:(dc + 1) * 128],
                                           XN[:, dc * 128:(dc + 1) * 128], self.IDN[:])
                pe.inc(ins)
            pen = pe.n
            @block.vector
            def _(vector):
                vector.wait_ge(pe.h, pen)
                dst = self.BIGA[:].rearrange("p (c t) -> p c t", c=NDC)[:, :, tc * 128:(tc + 1) * 128]
                src = pslice.rearrange("p (c t) -> p c t", c=NDC)
                vec.inc(vector.tensor_copy(dst, src))
            if "xn" in self.taps:
                vecn4 = vec.n
                @block.gpsimd
                def _(gpsimd):
                    gpsimd.wait_ge(vec.h, vecn4)
                    self.stc.inc(gpsimd.dma_start(out=self.taps["xn"][tc * 128:(tc + 1) * 128, :], in_=XN[:]), 16)
                self._xn_tap_stc = self.stc.n
                self.w("gpsimd")
        self.w("vector"); self.w("scalar")
        if "xnT" in self.taps:
            vecn = vec.n
            @block.gpsimd
            def _(gpsimd):
                gpsimd.wait_ge(vec.h, vecn)
                self.stc.inc(gpsimd.dma_start(out=self.taps["xnT"], in_=self.BIGA[:]), 16)
            self.w("gpsimd")
        self.barrier()
        ctx.close()

    def p2_v(self):
        nc, block, d = self.nc, self.block, self.dram
        ld, vec, pe, stc = self.ld, self.vec, self.pe, self.stc
        ctx = ExitStack()
        sb = lambda *a: ctx.enter_context(nc.sbuf_tensor(*a))
        WVS = sb("WVS", [128, 2 * D], F32)
        BVC = sb("BVC", [128, NDC], F32)
        VF = sb("VF", [128, T], F32)
        VTOK = sb("VTOK", [128, 256], F32)
        VP = ctx.enter_context(nc.psum_tensor("VP2", [128, 1024], F32))
        TP = ctx.enter_context(nc.psum_tensor("TP2", [128, 1024], F32))

        cs = self.sem("const")
        @block.sync
        def _(sync):
            cs.inc(sync.dma_start(out=BVC[:], in_=d["bvc"]), 16)
        ld_bvc = cs.n
        pe_after_proj = {}
        vec_after_evac = {}
        stc_after_store = {}
        tr_guard = 0
        for oc in range(NDC):
            par = oc % 2
            ws = WVS[:, par * D:(par + 1) * D]
            pe_need = pe_after_proj.get(oc - 2)
            wvss = self.sem(f"wvs{par}")
            @block.sync
            def _(sync):
                if pe_need is not None:
                    sync.wait_ge(pe.h, pe_need)
                for dc in range(NDC):
                    wvss.inc(sync.dma_start(
                        out=ws[:, dc * 128:(dc + 1) * 128],
                        in_=d["wvT"][dc * 128:(dc + 1) * 128, oc * 128:(oc + 1) * 128]), 16)
            ldn = wvss.n
            for tq in range(4):
                vpar = tq % 2
                pslice = VP[:, vpar * 512:(vpar + 1) * 512]
                vec_need = vec_after_evac.get((oc, tq - 2)) or vec_after_evac.get((oc - 1, tq + 2))
                @block.tensor
                def _(tensor):
                    tensor.wait_ge(wvss.h, ldn)
                    if vec_need is not None:
                        tensor.wait_ge(vec.h, vec_need)
                    ins = None
                    for dc in range(NDC):
                        ins = tensor.matmul(pslice, ws[:, dc * 128:(dc + 1) * 128],
                                            self.BIGA[:, dc * T + tq * 512: dc * T + (tq + 1) * 512],
                                            start=(dc == 0), stop=(dc == NDC - 1))
                    pe.inc(ins)
                pen = pe.n
                tg = tr_guard
                @block.vector
                def _(vector):
                    vector.wait_ge(pe.h, max(pen, tg))
                    vector.wait_ge(self.sem("const").h, ld_bvc)
                    vec.inc(vector.tensor_scalar(VF[:, tq * 512:(tq + 1) * 512], pslice,
                                                 BVC[:, oc:oc + 1], None, ALU.add))
                vec_after_evac[(oc, tq)] = vec.n
            pe_after_proj[oc] = pe.n
            for tc in range(NTC):
                tpar = tc % 2
                tps = TP[:, tpar * 512:tpar * 512 + 128]
                vtk = VTOK[:, tpar * 128:(tpar + 1) * 128]
                vec_need = vec_after_evac[(oc, 3)]
                evac_need = vec_after_evac.get(("t", oc, tc - 2))
                @block.tensor
                def _(tensor):
                    tensor.wait_ge(vec.h, max(vec_need, evac_need or 0))
                    pe.inc(tensor.transpose(tps, VF[:, tc * 128:(tc + 1) * 128], self.IDN[:]))
                pen = pe.n
                st_need = stc_after_store.get((oc, tc - 2))
                @block.vector
                def _(vector):
                    vector.wait_ge(pe.h, pen)
                    if st_need is not None:
                        vector.wait_ge(self.ssem("st_v").h, st_need)
                    vec.inc(vector.tensor_copy(vtk, tps))
                vec_after_evac[("t", oc, tc)] = vec.n
                vecn = vec.n
                stv = self.ssem("st_v")
                @block.gpsimd
                def _(gpsimd):
                    gpsimd.wait_ge(vec.h, vecn)
                    stv.inc(gpsimd.dma_start(
                        out=d["v_h"][tc * 128:(tc + 1) * 128, oc * 128:(oc + 1) * 128], in_=vtk), 16)
                stc_after_store[(oc, tc)] = stv.n
            tr_guard = pe.n
        self.w("vector"); self.w("gpsimd")
        self.barrier()
        if "vtok" in self.taps:
            @block.gpsimd
            def _(gpsimd):
                self.stc.inc(gpsimd.dma_start(out=self.taps["vtok"], in_=d["v_h"]), 16)
            self.w("gpsimd")
            self.barrier()
        ctx.close()

    def p3_attn(self, stage=7, npairs=8, nheads=2, nkc=NTC):
        nc, block, d = self.nc, self.block, self.dram
        ld, vec, scl, pe, stc = self.ld, self.vec, self.scl, self.pe, self.stc
        ctx = ExitStack()
        sb = lambda *a: ctx.enter_context(nc.sbuf_tensor(*a))
        WQS = sb("WQS", [128, 2 * D], F32)
        WKS = sb("WKS", [128, 2 * D], F32)
        BQC = sb("BQC", [128, NDC], F32)
        BKC = sb("BKC", [128, NDC], F32)
        C2 = sb("C2", [128, T], F32)
        S2 = sb("S2", [128, T], F32)
        SC1 = sb("SC1", [128, T], F32)
        SC2 = sb("SC2", [128, T], F32)
        SC3 = sb("SC3", [128, T], F32)
        QRB = sb("QRB", [128, T], F32)
        P1 = sb("P1", [128, T], F32)
        STATR = sb("STATR", [128, 3 * T], F32)
        QSS = STATR[0:2, 0:T]
        KSS = STATR[0:2, T:2 * T]
        BSR = STATR[0:2, 2 * T:3 * T]
        MQt = sb("MQ", [128, 2], F32)
        MQ = MQt[0:2, :]
        BIASP = sb("BIASP", [128, 32], F32)
        ALPHP = sb("ALPHP", [128, 32], F32)
        VLD = sb("VLD", [128, 256], F32)
        VTT = sb("VTT", [128, 130], F32)
        RBt = sb("RBt", [128, 1024], F32)
        self.taps_ops_buf = sb("TOB", [128, 512], F32)
        RROW = RBt[64:65, 0:512]
        BCS = RBt[0:64, 512:1024]
        AVEt = sb("AVE", [128, 1024], F32)
        AVE = AVEt[0:64, :]
        SCP = ctx.enter_context(nc.psum_tensor("SCP3", [128, T], F32))
        OPS = [ctx.enter_context(nc.psum_tensor(f"OPS{i}", [65, 512], F32)) for i in range(4)]

        cs = self.sem("const")
        @block.sync
        def _(sync):
            cs.inc(sync.dma_start(out=BQC[:], in_=d["bqc"]), 16)
            cs.inc(sync.dma_start(out=BKC[:], in_=d["bkc"]), 16)
            cs.inc(sync.dma_start(out=C2[:], in_=d["c2t"]), 16)
            cs.inc(sync.dma_start(out=S2[:], in_=d["s2t"]), 16)
        ld_tabs = cs.n

        bcs_guard = 0
        vld_ring = Ring()
        ave_ring = Ring()
        exp_last = {0: 0, 1: 0}
        p1_pe = {0: 0, 1: 0}
        scp_vec_guard = 0
        scp_scl_guard = 0
        pe_pair_w = Ring()

        for p in range(npairs):
            if getattr(self, "_p3_interleave_dequant", False):
                self._p6_emit(9)
            wpar = p % 2
            wq = WQS[:, wpar * D:(wpar + 1) * D]
            wk = WKS[:, wpar * D:(wpar + 1) * D]
            pe_need = pe_pair_w.old(2)
            wqss = self.sem(f"wqs{wpar}")
            @block.sync
            def _(sync):
                if pe_need is not None:
                    sync.wait_ge(pe.h, pe_need)
                for dc in range(NDC):
                    wqss.inc(sync.dma_start(
                        out=wq[:, dc * 128:(dc + 1) * 128],
                        in_=d["wqT"][dc * 128:(dc + 1) * 128, p * 128:(p + 1) * 128]), 16)
                    wqss.inc(sync.dma_start(
                        out=wk[:, dc * 128:(dc + 1) * 128],
                        in_=d["wkT"][dc * 128:(dc + 1) * 128, p * 128:(p + 1) * 128]), 16)
            ldn_w = wqss.n

            vg, sg = scp_vec_guard, scp_scl_guard
            evac_n = {}
            for wi, (ws, dst, bcol) in enumerate(((wq, SC1, BQC), (wk, SC2, BKC))):
                for tq in range(4):
                    pslice = SCP[:, tq * 512:(tq + 1) * 512]
                    bank_guard = evac_n.get(tq)
                    @block.tensor
                    def _(tensor):
                        tensor.wait_ge(wqss.h, ldn_w)
                        tensor.wait_ge(self.sem("const").h, ld_tabs)
                        if wi == 0 and tq == 0:
                            if vg:
                                tensor.wait_ge(vec.h, vg)
                            if sg:
                                tensor.wait_ge(scl.h, sg)
                        if bank_guard is not None:
                            tensor.wait_ge(vec.h, bank_guard)
                        ins = None
                        for dc in range(NDC):
                            ins = tensor.matmul(pslice, ws[:, dc * 128:(dc + 1) * 128],
                                                self.BIGA[:, dc * T + tq * 512: dc * T + (tq + 1) * 512],
                                                start=(dc == 0), stop=(dc == NDC - 1))
                        pe.inc(ins)
                    pen = pe.n
                    @block.vector
                    def _(vector):
                        vector.wait_ge(pe.h, pen)
                        vec.inc(vector.tensor_scalar(dst[:, tq * 512:(tq + 1) * 512], pslice,
                                                     bcol[:, p:p + 1], None, ALU.add))
                    evac_n[tq] = vec.n
            vec_proj = vec.n

            if stage < 2:
                if p == 0 and "qrope" in self.taps:
                    vpn = vec.n
                    @block.gpsimd
                    def _(gpsimd):
                        gpsimd.wait_ge(vec.h, vpn)
                        stc.inc(gpsimd.dma_start(out=self.taps["qrope"], in_=SC1[:]), 16)
                        stc.inc(gpsimd.dma_start(out=self.taps["krope"], in_=SC2[:]), 16)
                scp_vec_guard = vec.n
                scp_scl_guard = scl.n
                pe_pair_w.push(pe.n)
                continue
            sq_pe = 0
            for (src, ss) in ((SC1, QSS), (SC2, KSS)):
                vecn = vec.n
                @block.scalar
                def _(scalar):
                    scalar.wait_ge(vec.h, vecn)
                    if sq_pe:
                        scalar.wait_ge(pe.h, sq_pe)
                    scl.inc(scalar.activation(SC3[:], src[:], AF.Square))
                scln = scl.n
                @block.tensor
                def _(tensor):
                    tensor.wait_ge(scl.h, scln)
                    tensor.wait_ge(vec.h, vec_proj)
                    ins = None
                    for tq in range(4):
                        ins = tensor.matmul(SCP[0:2, tq * 512:(tq + 1) * 512], self.ONES2[:],
                                            SC3[:, tq * 512:(tq + 1) * 512], start=True, stop=True)
                    pe.inc(ins)
                sq_pe = pe.n
                pen = pe.n
                @block.vector
                def _(vector):
                    vector.wait_ge(pe.h, pen)
                    vec.inc(vector.tensor_copy(ss, SCP[0:2, :]))

            @block.vector
            def _(vector):
                vector.tensor_reduce(MQ[:, 0:1], QSS, AX.X, ALU.max)
                vector.drain()
                vec.inc(vector.tensor_scalar(BSR, KSS, MQ[:, 0:1], -C16, ALU.add, ALU.mult))
            vecn_bsr = vec.n
            @block.tensor
            def _(tensor):
                tensor.wait_ge(vec.h, vecn_bsr)
                ins = None
                for kc in range(NTC):
                    ins = tensor.transpose(SCP[:, kc * 2:kc * 2 + 2],
                                           BSR[:, kc * 128:(kc + 1) * 128], self.IDN[0:2, 0:2])
                pe.inc(ins)
            pen_bt = pe.n
            @block.vector
            def _(vector):
                vector.wait_ge(pe.h, pen_bt)
                vec.inc(vector.tensor_copy(BIASP[:], SCP[:, 0:32]))
            vecn_bp = vec.n
            @block.scalar
            def _(scalar):
                scalar.wait_ge(vec.h, vecn_bp)
                scl.inc(scalar.activation(ALPHP[:], BIASP[:], AF.Exp, scale=-1.0))
            scl_alpha = scl.n

            if stage < 3:
                scp_vec_guard = vec.n
                scp_scl_guard = scl.n
                pe_pair_w.push(pe.n)
                continue
            @block.gpsimd
            def _(gpsimd):
                gpsimd.wait_ge(pe.h, sq_pe)
                for h2 in range(2):
                    b = h2 * 64
                    self.gld.inc(gpsimd.dma_start(out=SC3[b:b + 32, :], in_=SC1[b + 32:b + 64, :]), 16)
                    self.gld.inc(gpsimd.dma_start(out=SC3[b + 32:b + 64, :], in_=SC1[b:b + 32, :]), 16)
            gldn = self.gld.n
            @block.vector
            def _(vector):
                vector.wait_ge(self.gld.h, gldn)
                vector.tensor_tensor(P1[:], SC1[:], C2[:], ALU.mult)
                vector.tensor_tensor(QRB[:], SC3[:], S2[:], ALU.mult)
                vector.drain()
                vec.inc(vector.tensor_tensor(SC1[:], P1[:], QRB[:], ALU.add))
            vec_qrope = vec.n
            @block.gpsimd
            def _(gpsimd):
                gpsimd.wait_ge(vec.h, vec_qrope)
                for h2 in range(2):
                    b = h2 * 64
                    self.gld.inc(gpsimd.dma_start(out=SC3[b:b + 32, :], in_=SC2[b + 32:b + 64, :]), 16)
                    self.gld.inc(gpsimd.dma_start(out=SC3[b + 32:b + 64, :], in_=SC2[b:b + 32, :]), 16)
            gldn = self.gld.n
            @block.vector
            def _(vector):
                vector.wait_ge(self.gld.h, gldn)
                vector.tensor_tensor(P1[:], SC2[:], C2[:], ALU.mult)
                vector.tensor_tensor(QRB[:], SC3[:], S2[:], ALU.mult)
                vector.drain()
                vec.inc(vector.tensor_tensor(SC2[:], P1[:], QRB[:], ALU.add))
            vec_krope = vec.n
            if p == 0 and "qrope" in self.taps:
                @block.gpsimd
                def _(gpsimd):
                    gpsimd.wait_ge(vec.h, vec_qrope)
                    stc.inc(gpsimd.dma_start(out=self.taps["qrope"], in_=SC1[:]), 16)
                    gpsimd.wait_ge(vec.h, vec_krope)
                    stc.inc(gpsimd.dma_start(out=self.taps["krope"], in_=SC2[:]), 16)

            if stage < 4:
                scp_vec_guard = vec.n
                scp_scl_guard = scl.n
                pe_pair_w.push(pe.n)
                continue
            for h2 in range(nheads):
                hb = h2 * 64
                for kc in range(nkc):
                    vpar = kc % 2
                    vld_t = VLD[:, vpar * 128:(vpar + 1) * 128]
                    vtt = VTT[:, vpar * 65:(vpar + 1) * 65]
                    slot_free = vld_ring.old(2)
                    vlds = self.sem(f"vld{vpar}")
                    @block.sync
                    def _(sync):
                        if slot_free is not None:
                            sync.wait_ge(pe.h, slot_free)
                        vlds.inc(sync.dma_start(out=vld_t,
                                                in_=d["v_h"][kc * 128:(kc + 1) * 128, p * 128:(p + 1) * 128]), 16)
                    ld_v = vlds.n
                    @block.vector
                    def _(vector):
                        vector.wait_ge(vlds.h, ld_v)
                        vector.wait_ge(scl.h, scl_alpha)
                        if slot_free is not None:
                            vector.wait_ge(pe.h, slot_free)
                        vector.tensor_scalar(vtt[:, 0:64], vld_t[:, hb:hb + 64],
                                             ALPHP[:, kc * 2 + h2:kc * 2 + h2 + 1], None, ALU.mult)
                        vec.inc(vector.tensor_copy(vtt[:, 64:65], ALPHP[:, kc * 2 + h2:kc * 2 + h2 + 1]))
                    vec_vtt = vec.n
                    scl_exp_q = [0, 0]
                    for qh in range(2):
                        sc_ps = SCP[:, qh * 1024:(qh + 1) * 1024]
                        p1s = P1[:, qh * 1024:(qh + 1) * 1024]
                        el = exp_last[qh]
                        bg = bcs_guard
                        @block.tensor
                        def _(tensor):
                            tensor.wait_ge(vec.h, max(vec_krope, bg))
                            if el:
                                tensor.wait_ge(scl.h, el)
                            ins = None
                            for j in range(2):
                                ins = tensor.matmul(
                                    sc_ps[:, j * 512:(j + 1) * 512],
                                    SC2[hb:hb + 64, kc * 128:(kc + 1) * 128],
                                    SC1[hb:hb + 64, (qh * 2 + j) * 512:(qh * 2 + j + 1) * 512],
                                    start=True, stop=True)
                            pe.inc(ins)
                        pe_sc = pe.n
                        pf = p1_pe[qh]
                        @block.scalar
                        def _(scalar):
                            scalar.wait_ge(pe.h, pe_sc)
                            if pf:
                                scalar.wait_ge(pe.h, pf)
                            scl.inc(scalar.activation(p1s, sc_ps, AF.Exp,
                                                      bias=BIASP[:, kc * 2 + h2:kc * 2 + h2 + 1],
                                                      scale=0.125))
                        exp_last[qh] = scl.n
                        scl_exp_q[qh] = scl.n
                    for qh in range(2):
                        p1s = P1[:, qh * 1024:(qh + 1) * 1024]
                        scl_exp = scl_exp_q[qh]
                        if p == 0 and h2 == 0 and kc == 0 and qh == 0 and "att0" in self.taps:
                            @block.gpsimd
                            def _(gpsimd):
                                gpsimd.wait_ge(scl.h, scl_exp)
                                stc.inc(gpsimd.dma_start(out=self.taps["att0"][:, 2 * T:2 * T + 32], in_=BIASP[:]), 16)
                                stc.inc(gpsimd.dma_start(out=self.taps["att0"][:, 2 * T + 32:2 * T + 64], in_=ALPHP[:]), 16)
                                stc.inc(gpsimd.dma_start(out=self.taps["att0"][:, 3 * T:3 * T + 1024], in_=P1[:, 0:1024]), 16)
                        @block.tensor
                        def _(tensor):
                            tensor.wait_ge(scl.h, scl_exp)
                            tensor.wait_ge(vec.h, vec_vtt)
                            ins = None
                            for j in range(2):
                                qc = qh * 2 + j
                                ins = tensor.matmul(OPS[qc][:, :], vtt,
                                                    p1s[:, j * 512:(j + 1) * 512],
                                                    start=(kc == 0), stop=(kc == nkc - 1))
                            pe.inc(ins)
                        p1_pe[qh] = pe.n
                    vld_ring.push(pe.n)
                pe_pv_done = pe.n
                if stage < 4.5:
                    continue
                for qc in range(4):
                    apar = qc % 2
                    ave = AVE[:, apar * 512:(apar + 1) * 512]
                    st_need = ave_ring.old(2)
                    sta_h = self.ssem("st_ave").h
                    @block.vector
                    def _(vector):
                        vector.wait_ge(pe.h, pe_pv_done)
                        vector.reciprocal(RROW, OPS[qc][64:65, :])
                        vector.drain()
                        vec.inc(vector.tensor_copy(self.MCOL[:, 1:2], self.MCOL[:, 0:1]))
                    vec_rr = vec.n
                    if stage >= 6:
                        @block.tensor
                        def _(tensor):
                            tensor.wait_ge(vec.h, vec_rr)
                            pe.inc(tensor.matmul(SCP[0:64, 0:512], self.ONESR64[:, 0:64], RROW,
                                                 start=True, stop=True))
                        pe_bc = pe.n
                        @block.scalar
                        def _(scalar):
                            scalar.wait_ge(pe.h, pe_bc)
                            scl.inc(scalar.copy(self.MCOL[0:1, 1:2], self.MCOL[0:1, 0:1]))
                        scl_bc = scl.n
                    if stage >= 7:
                        @block.vector
                        def _(vector):
                            vector.wait_ge(scl.h, scl_bc)
                            vector.wait_ge(pe.h, pe_bc)
                            if st_need is not None:
                                vector.wait_ge(sta_h, st_need)
                            vector.tensor_copy(BCS, SCP[0:64, 0:512])
                            vec.inc(vector.tensor_tensor(ave, OPS[qc][0:64, :], BCS, ALU.mult))
                        vec_ave = vec.n
                    else:
                        @block.vector
                        def _(vector):
                            if st_need is not None:
                                vector.wait_ge(sta_h, st_need)
                            if stage >= 6:
                                vector.wait_ge(pe.h, pe_bc)
                            vec.inc(vector.tensor_copy(ave, OPS[qc][0:64, :]))
                        vec_ave = vec.n
                    if p == 0 and h2 == 0 and qc == 0 and "att0" in self.taps:
                        @block.vector
                        def _(vector):
                            vec.inc(vector.tensor_copy(self.taps_ops_buf[0:65, :], OPS[qc][:, :]))
                        vo = vec.n
                        @block.gpsimd
                        def _(gpsimd):
                            gpsimd.wait_ge(vec.h, vo)
                            stc.inc(gpsimd.dma_start(out=self.taps["att0"][0:65, 0:512], in_=self.taps_ops_buf[0:65, :]), 16)
                            stc.inc(gpsimd.dma_start(out=self.taps["att0"][64:65, 512:1024], in_=RROW), 16)
                            stc.inc(gpsimd.dma_start(out=self.taps["att0"][0:64, 1024:1536], in_=BCS), 16)
                            stc.inc(gpsimd.dma_start(out=self.taps["att0"][0:64, 1536:2048], in_=ave), 16)
                        tapn = stc.n
                        @block.vector
                        def _(vector):
                            vector.wait_ge(stc.h, tapn)
                        vec_ave = vec.n
                    r0 = p * 128 + h2 * 64
                    sta = self.ssem("st_ave")
                    @block.gpsimd
                    def _(gpsimd):
                        gpsimd.wait_ge(vec.h, vec_ave)
                        sta.inc(gpsimd.dma_start(
                            out=d["avT_h"][r0:r0 + 64, qc * 512:(qc + 1) * 512], in_=ave), 16)
                    ave_ring.push(sta.n)
                bcs_guard = vec.n
            scp_vec_guard = vec.n
            scp_scl_guard = scl.n
            pe_pair_w.push(pe.n)
        self.w("vector"); self.w("scalar"); self.w("gpsimd")
        self.barrier()
        if "avT" in self.taps:
            @block.gpsimd
            def _(gpsimd):
                self.stc.inc(gpsimd.dma_start(out=self.taps["avT"], in_=d["avT_h"]), 16)
            self.w("gpsimd")
            self.barrier()
        ctx.close()

    def _qdq3(self, vector, dst, src, scol, rscol):
        vector.tensor_scalar(dst, src, rscol, MAGIC, ALU.mult, ALU.add)
        vector.tensor_scalar(dst, dst, MAGIC, 32767.0, ALU.subtract, ALU.min)
        vector.tensor_scalar(dst, dst, -32768.0, scol, ALU.max, ALU.mult)

    def _absmax_into(self, vector, src, gcol, tmpcol):
        vector.tensor_reduce(tmpcol, src, AX.X, ALU.max, apply_absolute_value=True)
        vector.drain()
        vector.tensor_tensor(gcol, gcol, tmpcol, ALU.max)
        vector.drain()

    def _allreduce_scales(self, ar_idx, c0, k, sclb_c0):
        nc, block, d = self.nc, self.block, self.dram
        ld, vec, pe, gp, cc, stc = self.ld, self.vec, self.pe, self.gp, self.cc, self.stc
        MQ2 = self.MCOL
        arctx = ExitStack()
        self.PSC = arctx.enter_context(nc.psum_tensor(f"PSC_{ar_idx}", [128, 256], F32))
        vecn0 = vec.n
        gpn0 = self.gp.n
        @block.tensor
        def _(tensor):
            tensor.wait_ge(vec.h, vecn0)
            if gpn0:
                tensor.wait_ge(self.gp.h, gpn0)
            pe.inc(tensor.transpose(self.PSC[0:k, 0:128], self.GMX[:, c0:c0 + k], self.IDN[:]))
        pen = pe.n
        @block.vector
        def _(vector):
            vector.wait_ge(pe.h, pen)
            vector.tensor_copy(self.TROW[0:k, :], self.PSC[0:k, 0:128])
            vector.drain()
            vector.tensor_reduce(MQ2[0:k, 0:1], self.TROW[0:k, :], AX.X, ALU.max)
            vector.drain()
            vec.inc(vector.tensor_copy(MQ2[0:k, 1:2], MQ2[0:k, 0:1]))
        vecn = vec.n
        ari, aro = d[f"ari{ar_idx}"], d[f"aro{ar_idx}"]
        @block.gpsimd
        def _(gpsimd):
            gpsimd.wait_ge(vec.h, vecn)
            self.gld.inc(gpsimd.dma_start(out=ari[0:1, 0:k], in_=MQ2[0:k, 0:1]), 16)
            if k == 1:
                self.gld.inc(gpsimd.dma_start(out=ari[0:1, 1:2], in_=MQ2[0:1, 0:1]), 16)
            gpsimd.wait_ge(self.gld.h, self.gld.n)
            gpsimd.collective_compute(
                "AllReduce", ALU.max, replica_groups=[list(range(B))],
                ins=[ari], outs=[aro],
            ).then_inc(cc.h, 1)
            cc.n += 1
            gpsimd.wait_ge(cc.h, cc.n)
            self.gld.inc(gpsimd.dma_start(out=self.SCL[0:1, 0:k], in_=aro[0:1, 0:k]), 16)
        gldn = self.gld.n
        @block.vector
        def _(vector):
            vector.wait_ge(self.gld.h, gldn)
            vector.tensor_scalar(self.SCL[0:1, 4:4 + k], self.SCL[0:1, 0:k], QMAXR, 1e-12, ALU.mult, ALU.max)
            vector.drain()
            vector.reciprocal(self.SCL[0:1, 4 + k:4 + 2 * k], self.SCL[0:1, 4:4 + k])
            vector.drain()
            vec.inc(vector.tensor_copy(self.SCL[0:1, 12:13], self.SCL[0:1, 0:1]))
        vecn2 = vec.n
        @block.tensor
        def _(tensor):
            tensor.wait_ge(vec.h, vecn2)
            pe.inc(tensor.matmul(self.PSC[0:128, 0:2 * k], self.ONESR[:, 0:128],
                                 self.SCL[0:1, 4:4 + 2 * k], start=True, stop=True))
        pen2 = pe.n
        @block.vector
        def _(vector):
            vector.wait_ge(pe.h, pen2)
            vec.inc(vector.tensor_copy(self.SCLB[:, sclb_c0:sclb_c0 + 2 * k], self.PSC[0:128, 0:2 * k]))
        self.w("vector"); self.w("gpsimd")
        self.barrier()
        arctx.close()

    def p4_wo(self):
        nc, block, d = self.nc, self.block, self.dram
        ld, vec, scl, pe, stc = self.ld, self.vec, self.scl, self.pe, self.stc
        ctx = ExitStack()
        sb = lambda *a: ctx.enter_context(nc.sbuf_tensor(*a))
        WBIG = sb("WBIG", [128, 4096], F32)
        X2H0 = sb("X2H0", [128, 8192], F32)
        AVB = sb("AVB", [128, 2048], F32)
        BOB = sb("BOB", [128, D], F32)
        YC = sb("YC", [128, D], F32)
        DB = sb("DB4", [128, D], F32)
        JUNK = sb("JUNK4", [128, D], F32)
        COLS = sb("COLS4", [128, 8], F32)
        X2C = sb("X2C", [128, D], F32)
        PW = ctx.enter_context(nc.psum_tensor("PW4", [128, 1024], F32))

        cs = self.sem("const")
        @block.sync
        def _(sync):
            cs.inc(sync.dma_start(out=BOB[:], in_=d["bob"]), 16)
        ld_bob = cs.n
        av_ring = Ring()
        pw_ring = Ring()
        for dh in range(2):
            pe_w = getattr(self, "_p4_pe_w", None)
            wbigs = self.sem("wbig")
            @block.sync
            def _(sync):
                if pe_w is not None:
                    sync.wait_ge(pe.h, pe_w)
                for dc in range(NDC):
                    wbigs.inc(sync.dma_start(out=WBIG[:, dc * 512:(dc + 1) * 512],
                                             in_=d["woT"][dc * 128:(dc + 1) * 128, dh * 512:(dh + 1) * 512]), 16)
            ldn_w = wbigs.n
            for tc in range(NTC):
                if getattr(self, "_p4_interleave_tr", False):
                    self._p6t_emit(2)
                apar = tc % 2
                avb = AVB[:, apar * 1024:(apar + 1) * 1024]
                slot_free = av_ring.old(2)
                avbs = self.sem(f"avb{apar}")
                @block.sync
                def _(sync):
                    if slot_free is not None:
                        sync.wait_ge(pe.h, slot_free)
                    for dc in range(NDC):
                        avbs.inc(sync.dma_start(out=avb[:, dc * 128:(dc + 1) * 128],
                                                in_=d["avT_h"][dc * 128:(dc + 1) * 128, tc * 128:(tc + 1) * 128]), 16)
                ldn_av = avbs.n
                ppar = tc % 2
                pw = PW[:, ppar * 512:(ppar + 1) * 512]
                vfree = pw_ring.old(2)
                @block.tensor
                def _(tensor):
                    tensor.wait_ge(wbigs.h, ldn_w)
                    tensor.wait_ge(avbs.h, ldn_av)
                    if vfree is not None:
                        tensor.wait_ge(vec.h, vfree)
                    ins = None
                    for dc in range(NDC):
                        ins = tensor.matmul(pw, avb[:, dc * 128:(dc + 1) * 128],
                                            WBIG[:, dc * 512:(dc + 1) * 512],
                                            start=(dc == 0), stop=(dc == NDC - 1))
                    pe.inc(ins)
                av_ring.push(pe.n)
                pen = pe.n
                if dh == 0:
                    p4xs = self.sem("p4x")
                    @block.sync
                    def _(sync):
                        sync.wait_ge(vec.h, getattr(self, "_p4_xc", 0))
                        p4xs.inc(sync.dma_start(out=self.CHK[:, 0:512],
                                                in_=d["xin"][tc * 128:(tc + 1) * 128, 0:512]), 16)
                    ldn_x = p4xs.n
                    @block.vector
                    def _(vector):
                        vector.wait_ge(pe.h, pen)
                        vector.wait_ge(p4xs.h, ldn_x)
                        vector.wait_ge(self.sem("const").h, ld_bob)
                        dst = X2H0[:, tc * 512:(tc + 1) * 512]
                        vector.tensor_tensor(dst, pw, self.CHK[:, 0:512], ALU.add)
                        vector.drain()
                        vector.tensor_tensor(dst, dst, BOB[:, 0:512], ALU.add)
                        vector.drain()
                        vec.inc(vector.tensor_tensor(dst, dst, self.CHK[:, 0:512], ALU.add))
                    self._p4_xc = vec.n
                    pw_ring.push(vec.n)
                else:
                    p4xs = self.sem("p4x")
                    @block.sync
                    def _(sync):
                        sync.wait_ge(vec.h, getattr(self, "_p4_xc", 0))
                        p4xs.inc(sync.dma_start(out=self.CHK[:, 0:512],
                                                in_=d["xin"][tc * 128:(tc + 1) * 128, 512:1024]), 16)
                    ldn_x = p4xs.n
                    st_prev = getattr(self, "_p4_st", None)
                    @block.vector
                    def _(vector):
                        vector.wait_ge(pe.h, pen)
                        vector.wait_ge(p4xs.h, ldn_x)
                        dsty = YC[:, 512:1024]
                        vector.tensor_tensor(dsty, pw, self.CHK[:, 0:512], ALU.add)
                        vector.drain()
                        vector.tensor_tensor(dsty, dsty, BOB[:, 512:1024], ALU.add)
                        vector.drain()
                        vector.tensor_tensor(dsty, dsty, self.CHK[:, 0:512], ALU.add)
                        vector.tensor_copy(YC[:, 0:512], X2H0[:, tc * 512:(tc + 1) * 512])
                        vector.drain()
                        vector.tensor_reduce(COLS[:, 0:1], YC[:], AX.X, ALU.add)
                        vector.drain()
                        vector.tensor_scalar_mul(COLS[:, 1:2], COLS[:, 0:1], float(1.0 / D))
                        vector.drain()
                        vec.inc(vector.tensor_scalar_sub(DB[:], YC[:], COLS[:, 1:2]))
                    self._p4_xc = vec.n
                    vecn1 = vec.n
                    @block.scalar
                    def _(scalar):
                        scalar.wait_ge(vec.h, vecn1)
                        scl.inc(scalar.activation(JUNK[:], DB[:], AF.Square, accum_out=COLS[:, 2:3]))
                    scln1 = scl.n
                    @block.vector
                    def _(vector):
                        vector.wait_ge(scl.h, scln1)
                        vec.inc(vector.tensor_scalar(COLS[:, 3:4], COLS[:, 2:3], float(1.0 / D), float(EPS), ALU.mult, ALU.add))
                    vecn2 = vec.n
                    @block.scalar
                    def _(scalar):
                        scalar.wait_ge(vec.h, vecn2)
                        scl.inc(scalar.sqrt(COLS[:, 4:5], COLS[:, 3:4]))
                    scln2 = scl.n
                    @block.vector
                    def _(vector):
                        vector.wait_ge(scl.h, scln2)
                        vector.reciprocal(COLS[:, 5:6], COLS[:, 4:5])
                        vector.drain()
                        if st_prev is not None:
                            vector.wait_ge(self.ssem("st_x2").h, st_prev)
                        vector.tensor_scalar(X2C[:], DB[:], COLS[:, 5:6], None, ALU.mult)
                        vector.drain()
                        self._absmax_into(vector, X2C[:], self.GMX[:, 0:1], COLS[:, 6:7])
                        vec.inc(vector.tensor_copy(COLS[:, 7:8], COLS[:, 6:7]))
                    vecn3 = vec.n
                    stx = self.ssem("st_x2")
                    @block.gpsimd
                    def _(gpsimd):
                        gpsimd.wait_ge(vec.h, vecn3)
                        stx.inc(gpsimd.dma_start(out=d["x2_h"][tc * 128:(tc + 1) * 128, :], in_=X2C[:]), 16)
                    self._p4_st = stx.n
                    pw_ring.push(self._p4_xc)
            self._p4_pe_w = pe.n
        self.w("vector"); self.w("scalar"); self.w("gpsimd")
        self.barrier()
        if "x2" in self.taps:
            @block.gpsimd
            def _(gpsimd):
                self.stc.inc(gpsimd.dma_start(out=self.taps["x2"], in_=d["x2_h"]), 16)
            self.w("gpsimd")
            self.barrier()
        self._allreduce_scales(0, 0, 1, 0)
        TRQ = ctx.enter_context(nc.psum_tensor("TRQ4", [128, 2048], F32))
        for tc in range(NTC):
            par = tc % 2
            xc = self.CHK[:, par * D:(par + 1) * D]
            vprev = getattr(self, "_p4b_vec", None)
            chks = self.sem(f"chk{par}")
            @block.sync
            def _(sync):
                if vprev is not None:
                    sync.wait_ge(vec.h, vprev)
                chks.inc(sync.dma_start(out=xc, in_=d["x2_h"][tc * 128:(tc + 1) * 128, :]), 16)
            ldn = chks.n
            @block.vector
            def _(vector):
                vector.wait_ge(chks.h, ldn)
                self._qdq3(vector, X2C[:], xc, self.SCLB[:, 0:1], self.SCLB[:, 1:2])
                vector.drain()
                vec.inc(vector.tensor_copy(COLS[:, 7:8], COLS[:, 6:7]))
            self._p4b_vec = vec.n
            vecn = vec.n
            pslice = TRQ[:, (tc % 2) * 1024:((tc % 2) + 1) * 1024]
            evac_prev = getattr(self, "_p4b_evac", {}).get(tc - 2)
            @block.tensor
            def _(tensor):
                tensor.wait_ge(vec.h, vecn)
                ins = None
                for dc in range(NDC):
                    ins = tensor.transpose(pslice[:, dc * 128:(dc + 1) * 128],
                                           X2C[:, dc * 128:(dc + 1) * 128], self.IDN[:])
                pe.inc(ins)
            pen = pe.n
            @block.vector
            def _(vector):
                vector.wait_ge(pe.h, pen)
                dst = self.BIGA[:].rearrange("p (c t) -> p c t", c=NDC)[:, :, tc * 128:(tc + 1) * 128]
                src = pslice.rearrange("p (c t) -> p c t", c=NDC)
                vec.inc(vector.tensor_copy(dst, src))
            if not hasattr(self, "_p4b_evac"):
                self._p4b_evac = {}
            self._p4b_evac[tc] = vec.n
        self.w("vector")
        self.barrier()
        if "x2qT" in self.taps:
            @block.gpsimd
            def _(gpsimd):
                self.stc.inc(gpsimd.dma_start(out=self.taps["x2qT"], in_=self.BIGA[:]), 16)
            self.w("gpsimd")
            self.barrier()
        ctx.close()

    def _p6_init(self):
        nc = self.nc
        ctx = self._p6_ctx = ExitStack()
        sb = lambda *a: ctx.enter_context(nc.sbuf_tensor(*a))
        self._P6B = dict(
            WCH=sb("WCH", [128, 2 * 1024], F32),
            SBK=sb("SBK", [128, 32], F32),
            RBK=sb("RBK", [128, 32], F32),
            WQV=sb("WQV", [128, 2 * 1024], F32),
        )
        jobs = [("wup", "wupT_h", II, D), ("wgate", "wgateT_h", II, D), ("wdown", "wdT_h", D, II)]
        self._p6_jobs = [(wname, dsth, rc, co) for (wname, dsth, RO, CO) in jobs
                         for rc in range(RO // 128) for co in range(0, CO, 1024)]
        self._p6_pos = 0
        self._p6_rings = (Ring(), Ring())

    def _p6_emit(self, n):
        if not hasattr(self, "_p6_jobs"):
            self._p6_init()
        nc, block, d = self.nc, self.block, self.dram
        ld, vec, stc = self.ld, self.vec, self.stc
        b = self._P6B
        WCH, SBK, RBK, WQV = b["WCH"], b["SBK"], b["RBK"], b["WQV"]
        wch_ring, st_ring = self._p6_rings
        todo = self._p6_jobs[self._p6_pos:self._p6_pos + n]
        for (wname, dsth, rc, co) in todo:
            u = self._p6_pos
            self._p6_pos += 1
            nb, CO = 32, 1024
            par = len(wch_ring.hist) % 2
            wch = WCH[:, par * 1024: par * 1024 + CO]
            wqv = WQV[:, par * 1024: par * 1024 + CO]
            vprev = wch_ring.old(2)
            wchs = self.sem(f"wch{par}")
            @block.sync
            def _(sync):
                if vprev is not None:
                    sync.wait_ge(vec.h, vprev)
                wchs.inc(sync.dma_start(out=wch, in_=d[wname][rc * 128:(rc + 1) * 128, co:co + CO]), 16)
            ldn = wchs.n
            st_prev = st_ring.old(2)
            @block.vector
            def _(vector):
                vector.wait_ge(wchs.h, ldn)
                if st_prev is not None:
                    vector.wait_ge(self.ssem("st_wq").h, st_prev)
                vector.tensor_reduce(SBK[:, 0:nb].rearrange("p (b o) -> p b o", o=1),
                                     wch.rearrange("p (b c) -> p b c", c=32),
                                     AX.X, ALU.max, apply_absolute_value=True)
                vector.drain()
                vector.tensor_scalar(SBK[:, 0:nb], SBK[:, 0:nb], R7, 1e-12, ALU.mult, ALU.max)
                vector.drain()
                vector.reciprocal(RBK[:, 0:nb], SBK[:, 0:nb])
                vector.drain()
                rb = RBK[:, 0:nb].rearrange("p (b o) -> p b o", o=1).broadcast_to([128, nb, 32])
                sb_ = SBK[:, 0:nb].rearrange("p (b o) -> p b o", o=1).broadcast_to([128, nb, 32])
                w3 = wqv.rearrange("p (b c) -> p b c", c=32)
                vector.tensor_tensor(w3, wch.rearrange("p (b c) -> p b c", c=32), rb, ALU.mult)
                vector.drain()
                vector.tensor_scalar(wqv, wqv, MAGIC, MAGIC, ALU.add, ALU.subtract)
                vector.tensor_scalar(wqv, wqv, 7.0, -8.0, ALU.min, ALU.max)
                vector.drain()
                vec.inc(vector.tensor_tensor(w3, w3, sb_, ALU.mult))
            wch_ring.push(vec.n)
            vecn = vec.n
            stw = self.ssem("st_wq")
            @block.gpsimd
            def _(gpsimd):
                gpsimd.wait_ge(vec.h, vecn)
                stw.inc(gpsimd.dma_start(out=d["wqoi_h"][u * 128:(u + 1) * 128, :], in_=wqv), 16)
            st_ring.push(stw.n)

    def _p6t_init(self):
        nc = self.nc
        ctx = self._p6t_ctx = ExitStack()
        sb = lambda *a: ctx.enter_context(nc.sbuf_tensor(*a))
        self._P6T = dict(
            WTL=sb("WTL", [128, 2 * 1024], F32),
            WEV=sb("WEV", [128, 1024], F32),
            TPD=ctx.enter_context(nc.psum_tensor("TPD6", [128, 1024], F32)),
        )
        self._p6t_pos = 0
        self._p6t_rings = (Ring(), Ring(), Ring())

    def _p6t_emit(self, n):
        if not hasattr(self, "_p6t_pos"):
            self._p6t_init()
        nc, block, d = self.nc, self.block, self.dram
        ld, vec, pe, stc = self.ld, self.vec, self.pe, self.stc
        WTL, WEV, TPD = self._P6T["WTL"], self._P6T["WEV"], self._P6T["TPD"]
        wtl_ring, ev_ring, st_ring = self._p6t_rings
        todo = self._p6_jobs[self._p6t_pos:self._p6t_pos + n]
        for (wname, dsth, rc, co) in todo:
            u = self._p6t_pos
            self._p6t_pos += 1
            par = len(wtl_ring.hist) % 2
            wtl = WTL[:, par * 1024:(par + 1) * 1024]
            pe_prev = wtl_ring.old(2)
            wtls = self.sem(f"wtl{par}")
            @block.sync
            def _(sync):
                if pe_prev is not None:
                    sync.wait_ge(pe.h, pe_prev)
                wtls.inc(sync.dma_start(out=wtl, in_=d["wqoi_h"][u * 128:(u + 1) * 128, :]), 16)
            ldn = wtls.n
            for g in range(2):
                gpar = len(ev_ring.hist) % 2
                tslice = self._P6T["TPD"][:, gpar * 512:(gpar + 1) * 512]
                eslice = WEV[:, gpar * 512:(gpar + 1) * 512]
                ev_free = ev_ring.old(2)
                @block.tensor
                def _(tensor):
                    tensor.wait_ge(wtls.h, ldn)
                    if ev_free is not None:
                        tensor.wait_ge(vec.h, ev_free)
                    ins = None
                    for j in range(4):
                        tj = g * 4 + j
                        ins = tensor.transpose(tslice[:, j * 128:(j + 1) * 128],
                                               wtl[:, tj * 128:(tj + 1) * 128], self.IDN[:])
                    pe.inc(ins)
                pen = pe.n
                st_free = st_ring.old(2)
                stt = self.ssem("st_wt")
                @block.vector
                def _(vector):
                    vector.wait_ge(pe.h, pen)
                    if st_free is not None:
                        vector.wait_ge(stt.h, st_free)
                    vec.inc(vector.tensor_copy(eslice, tslice))
                ev_ring.push(vec.n)
                vecn_e = vec.n
                @block.gpsimd
                def _(gpsimd):
                    gpsimd.wait_ge(vec.h, vecn_e)
                    ins = None
                    for j in range(4):
                        tj = g * 4 + j
                        ins = gpsimd.dma_start(out=d[dsth][co + tj * 128:co + (tj + 1) * 128,
                                                           rc * 128:(rc + 1) * 128],
                                               in_=eslice[:, j * 128:(j + 1) * 128])
                        stt.inc(ins, 16)
                st_ring.push(stt.n)
            wtl_ring.push(pe.n)

    def p6_dequant(self):
        if not hasattr(self, "_p6_jobs"):
            self._p6_init()
        self._p6_emit(len(self._p6_jobs) - self._p6_pos)
        if not hasattr(self, "_p6t_pos"):
            self._p6t_init()
        self._p6t_emit(len(self._p6_jobs) - self._p6t_pos)
        block, d = self.block, self.dram
        self.w("vector"); self.w("gpsimd")
        self.barrier()
        for tap, src in (("wupTq", "wupT_h"), ("wdTq", "wdT_h")):
            if tap in self.taps:
                @block.gpsimd
                def _(gpsimd):
                    self.stc.inc(gpsimd.dma_start(out=self.taps[tap], in_=d[src]), 16)
                self.w("gpsimd")
                self.barrier()
        self._p6t_ctx.close()
        self._p6_ctx.close()

    def p7_upgate(self):
        nc, block, d = self.nc, self.block, self.dram
        ld, vec, pe, stc = self.ld, self.vec, self.pe, self.stc
        ctx = ExitStack()
        sb = lambda *a: ctx.enter_context(nc.sbuf_tensor(*a))
        LHB = sb("LHB", [128, 2048], F32)
        UPC = sb("UPC", [128, 2048], F32)
        TMPC = sb("TMPC", [128, 2], F32)
        UPP = ctx.enter_context(nc.psum_tensor("UPP7", [128, 4096], F32))
        pe_ring = Ring(); vec_ring = Ring(); st_ring = Ring()
        for wi, (wh, dsth, gcol) in enumerate((("wupT_h", "up_h", 1), ("wgateT_h", "gate_h", 2))):
            for ic in range(NIC):
                par = len(pe_ring.hist) % 2
                lhb = LHB[:, par * 1024:(par + 1) * 1024]
                pe_prev = pe_ring.old(2)
                lhbs = self.sem(f"lhb{par}")
                @block.sync
                def _(sync):
                    if pe_prev is not None:
                        sync.wait_ge(pe.h, pe_prev)
                    for hc in range(NDC):
                        lhbs.inc(sync.dma_start(out=lhb[:, hc * 128:(hc + 1) * 128],
                                                in_=d[wh][hc * 128:(hc + 1) * 128, ic * 128:(ic + 1) * 128]), 16)
                ldn = lhbs.n
                vec_prev = vec_ring.old(2)
                for tq in range(4):
                    pslice = UPP[:, (par * 4 + tq) * 512:(par * 4 + tq + 1) * 512]
                    @block.tensor
                    def _(tensor):
                        tensor.wait_ge(lhbs.h, ldn)
                        if tq == 0 and vec_prev is not None:
                            tensor.wait_ge(vec.h, vec_prev)
                        ins = None
                        for hc in range(NDC):
                            ins = tensor.matmul(pslice, lhb[:, hc * 128:(hc + 1) * 128],
                                                self.BIGA[:, hc * T + tq * 512: hc * T + (tq + 1) * 512],
                                                start=(hc == 0), stop=(hc == NDC - 1))
                        pe.inc(ins)
                pe_ring.push(pe.n)
                pen = pe.n
                st_prev = st_ring.old(1)
                stu = self.ssem("st_upc")
                @block.vector
                def _(vector):
                    vector.wait_ge(pe.h, pen)
                    if st_prev is not None:
                        vector.wait_ge(stu.h, st_prev)
                    src = UPP[:, par * 2048:(par + 1) * 2048]
                    vector.tensor_copy(UPC[:], src)
                    vector.drain()
                    self._absmax_into(vector, UPC[:], self.GMX[:, gcol:gcol + 1], TMPC[:, 0:1])
                    vec.inc(vector.tensor_copy(TMPC[:, 1:2], TMPC[:, 0:1]))
                vec_ring.push(vec.n)
                vecn = vec.n
                @block.gpsimd
                def _(gpsimd):
                    gpsimd.wait_ge(vec.h, vecn)
                    stu.inc(gpsimd.dma_start(out=d[dsth][ic * 128:(ic + 1) * 128, :], in_=UPC[:]), 16)
                st_ring.push(stu.n)
        self.w("vector"); self.w("gpsimd")
        self.barrier()
        for tap, src in (("up", "up_h"), ("gate", "gate_h")):
            if tap in self.taps:
                @block.gpsimd
                def _(gpsimd):
                    self.stc.inc(gpsimd.dma_start(out=self.taps[tap], in_=d[src]), 16)
                self.w("gpsimd")
                self.barrier()
        ctx.close()
        self._allreduce_scales(1, 1, 2, 2)

    def p8_stageB(self):
        nc, block, d = self.nc, self.block, self.dram
        ld, vec, scl, stc = self.ld, self.vec, self.scl, self.stc
        ctx = ExitStack()
        sb = lambda *a: ctx.enter_context(nc.sbuf_tensor(*a))
        GCH = sb("GCH", [128, 2 * T], F32)
        GQ = sb("GQ", [128, T], F32)
        SG = sb("SG", [128, T], F32)
        G2 = sb("G2", [128, T], F32)
        TMPC = sb("TMPC8", [128, 2], F32)
        for ic in range(NIC):
            par = ic % 2
            gch = GCH[:, par * T:(par + 1) * T]
            vprev = getattr(self, "_p8_vec", {}).get(ic - 2)
            gchs = self.sem(f"gch{par}")
            @block.sync
            def _(sync):
                if vprev is not None:
                    sync.wait_ge(vec.h, vprev)
                gchs.inc(sync.dma_start(out=gch, in_=d["gate_h"][ic * 128:(ic + 1) * 128, :]), 16)
            ldn = gchs.n
            st_prev = getattr(self, "_p8_st", {}).get(ic - 1)
            @block.scalar
            def _(scalar):
                scalar.wait_ge(gchs.h, ldn)
                scalar.wait_ge(vec.h, getattr(self, "_p8_gq_free", 0))
                scalar.wait_ge(vec.h, self.p0_vec)
                scl.inc(scalar.activation(GQ[:], gch, AF.Identity,
                                          bias=self.MAGC[:, 0:1], scale=self.SCLB[:, 5:6]))
            scl0 = scl.n
            @block.vector
            def _(vector):
                vector.wait_ge(scl.h, scl0)
                vector.tensor_scalar(GQ[:], GQ[:], MAGIC, 32767.0, ALU.subtract, ALU.min)
                vec.inc(vector.tensor_scalar(GQ[:], GQ[:], -32768.0, self.SCLB[:, 3:4], ALU.max, ALU.mult))
            vecn1 = vec.n
            @block.scalar
            def _(scalar):
                scalar.wait_ge(vec.h, vecn1)
                scalar.activation(SG[:], GQ[:], AF.Sigmoid)
                scl.inc(scalar.activation(SG[:], SG[:], AF.Identity, bias=self.MAGC[:, 0:1], scale=65536.0))
            self._p8_scl = scl.n
            scln = scl.n
            @block.vector
            def _(vector):
                vector.wait_ge(scl.h, scln)
                if st_prev is not None:
                    vector.wait_ge(self.ssem("st_g2").h, st_prev)
                vector.tensor_scalar(SG[:], SG[:], MAGIC, 65535.0, ALU.subtract, ALU.min)
                vector.tensor_scalar(SG[:], SG[:], 0.0, float(2.0 ** -16), ALU.max, ALU.mult)
                vector.drain()
                vector.tensor_tensor(G2[:], GQ[:], SG[:], ALU.mult)
                vector.drain()
                self._absmax_into(vector, G2[:], self.GMX[:, 3:4], TMPC[:, 0:1])
                vec.inc(vector.tensor_copy(TMPC[:, 1:2], TMPC[:, 0:1]))
            self._p8_gq_free = vec.n
            if not hasattr(self, "_p8_vec"):
                self._p8_vec = {}
            self._p8_vec[ic] = vec.n
            vecn2 = vec.n
            stg = self.ssem("st_g2")
            @block.gpsimd
            def _(gpsimd):
                gpsimd.wait_ge(vec.h, vecn2)
                stg.inc(gpsimd.dma_start(out=d["g2_h"][ic * 128:(ic + 1) * 128, :], in_=G2[:]), 16)
            if not hasattr(self, "_p8_st"):
                self._p8_st = {}
            self._p8_st[ic] = stg.n
        self.w("vector"); self.w("scalar"); self.w("gpsimd")
        self.barrier()
        if "g2" in self.taps:
            @block.gpsimd
            def _(gpsimd):
                self.stc.inc(gpsimd.dma_start(out=self.taps["g2"], in_=d["g2_h"]), 16)
            self.w("gpsimd")
            self.barrier()
        ctx.close()
        self._allreduce_scales(2, 3, 1, 6)

    def p9_stageC(self):
        nc, block, d = self.nc, self.block, self.dram
        ld, vec, scl, stc = self.ld, self.vec, self.scl, self.stc
        ctx = ExitStack()
        sb = lambda *a: ctx.enter_context(nc.sbuf_tensor(*a))
        ACH = sb("ACH", [128, 2 * T], F32)
        BCH = sb("BCH", [128, 2 * T], F32)
        AQ = sb("AQ", [128, T], F32)
        BQ = sb("BQ9", [128, T], F32)
        TMPC = sb("TMPC9", [128, 2], F32)
        for ic in range(NIC):
            par = ic % 2
            ach = ACH[:, par * T:(par + 1) * T]
            bch = BCH[:, par * T:(par + 1) * T]
            vprev = getattr(self, "_p9_vec", {}).get(ic - 2)
            achs = self.sem(f"ach{par}")
            @block.sync
            def _(sync):
                if vprev is not None:
                    sync.wait_ge(vec.h, vprev)
                achs.inc(sync.dma_start(out=ach, in_=d["g2_h"][ic * 128:(ic + 1) * 128, :]), 16)
                achs.inc(sync.dma_start(out=bch, in_=d["up_h"][ic * 128:(ic + 1) * 128, :]), 16)
            ldn = achs.n
            st_prev = getattr(self, "_p9_st", {}).get(ic - 1)
            @block.scalar
            def _(scalar):
                scalar.wait_ge(achs.h, ldn)
                scalar.wait_ge(vec.h, getattr(self, "_p9_free", 0))
                if st_prev is not None:
                    scalar.wait_ge(self.ssem("st_o").h, st_prev)
                scalar.activation(AQ[:], ach, AF.Identity, bias=self.MAGC[:, 0:1], scale=self.SCLB[:, 7:8])
                scl.inc(scalar.activation(BQ[:], bch, AF.Identity, bias=self.MAGC[:, 0:1], scale=self.SCLB[:, 4:5]))
            scl0 = scl.n
            @block.vector
            def _(vector):
                vector.wait_ge(scl.h, scl0)
                if st_prev is not None:
                    vector.wait_ge(self.ssem("st_o").h, st_prev)
                vector.tensor_scalar(AQ[:], AQ[:], MAGIC, 32767.0, ALU.subtract, ALU.min)
                vector.tensor_scalar(AQ[:], AQ[:], -32768.0, self.SCLB[:, 6:7], ALU.max, ALU.mult)
                vector.tensor_scalar(BQ[:], BQ[:], MAGIC, 32767.0, ALU.subtract, ALU.min)
                vector.tensor_scalar(BQ[:], BQ[:], -32768.0, self.SCLB[:, 2:3], ALU.max, ALU.mult)
                vector.drain()
                vector.tensor_tensor(AQ[:], AQ[:], BQ[:], ALU.mult)
                vector.drain()
                self._absmax_into(vector, AQ[:], self.GMX[:, 4:5], TMPC[:, 0:1])
                vec.inc(vector.tensor_copy(TMPC[:, 1:2], TMPC[:, 0:1]))
            self._p9_free = vec.n
            if not hasattr(self, "_p9_vec"):
                self._p9_vec = {}
            self._p9_vec[ic] = vec.n
            vecn = vec.n
            sto = self.ssem("st_o")
            @block.gpsimd
            def _(gpsimd):
                gpsimd.wait_ge(vec.h, vecn)
                sto.inc(gpsimd.dma_start(out=d["o_h"][ic * 128:(ic + 1) * 128, :], in_=AQ[:]), 16)
            if not hasattr(self, "_p9_st"):
                self._p9_st = {}
            self._p9_st[ic] = sto.n
        self.w("vector"); self.w("gpsimd")
        self.barrier()
        if "o" in self.taps:
            @block.gpsimd
            def _(gpsimd):
                self.stc.inc(gpsimd.dma_start(out=self.taps["o"], in_=d["o_h"]), 16)
            self.w("gpsimd")
            self.barrier()
        ctx.close()
        self._allreduce_scales(3, 4, 1, 8)

    def p10_down(self):
        nc, block, d = self.nc, self.block, self.dram
        ld, vec, scl, pe, stc = self.ld, self.vec, self.scl, self.pe, self.stc
        ctx = ExitStack()
        sb = lambda *a: ctx.enter_context(nc.sbuf_tensor(*a))
        OQB = sb("OQB", [128, 2 * 512], F32)
        OQD = sb("OQD", [128, 2 * 512], F32)
        WDB = sb("WDB", [128, 2 * 1024], F32)
        YC = sb("YC10", [128, D], F32)
        DB = sb("DB10", [128, D], F32)
        JUNK = sb("JUNK10", [128, D], F32)
        COLS = sb("COLS10", [128, 8], F32)
        OUTC = sb("OUTC", [128, D], F32)
        DP = ctx.enter_context(nc.psum_tensor("DP10", [128, 4096], F32))
        for tcg in range(4):
            for ic in range(NIC):
                par = ic % 2
                oqb = OQB[:, par * 512:(par + 1) * 512]
                oqd = OQD[:, par * 512:(par + 1) * 512]
                wdb = WDB[:, par * 1024:(par + 1) * 1024]
                pe_prev = getattr(self, "_p10_pe", {}).get((tcg, ic - 2))
                oqbs = self.sem(f"oqb{par}")
                @block.sync
                def _(sync):
                    if pe_prev is not None:
                        sync.wait_ge(pe.h, pe_prev)
                    oqbs.inc(sync.dma_start(out=oqb, in_=d["o_h"][ic * 128:(ic + 1) * 128,
                                                                  tcg * 512:(tcg + 1) * 512]), 16)
                    oqbs.inc(sync.dma_start(out=wdb, in_=d["wdT_h"][ic * 128:(ic + 1) * 128, :]), 16)
                ldn = oqbs.n
                @block.vector
                def _(vector):
                    vector.wait_ge(oqbs.h, ldn)
                    if pe_prev is not None:
                        vector.wait_ge(pe.h, pe_prev)
                    self._qdq3(vector, oqd, oqb, self.SCLB[:, 8:9], self.SCLB[:, 9:10])
                    vector.drain()
                    vec.inc(vector.tensor_copy(self.MCOL[:, 1:2], self.MCOL[:, 0:1]))
                vecn = vec.n
                vec_evac = getattr(self, "_p10_evac", {}).get(tcg - 1)
                @block.tensor
                def _(tensor):
                    tensor.wait_ge(vec.h, vecn)
                    if ic == 0 and vec_evac is not None:
                        tensor.wait_ge(vec.h, vec_evac)
                    ins = None
                    for tcl in range(4):
                        for hh in range(2):
                            ins = tensor.matmul(DP[:, (tcl * 2 + hh) * 512:(tcl * 2 + hh + 1) * 512],
                                                oqd[:, tcl * 128:(tcl + 1) * 128],
                                                wdb[:, hh * 512:(hh + 1) * 512],
                                                start=(ic == 0), stop=(ic == NIC - 1))
                    pe.inc(ins)
                if not hasattr(self, "_p10_pe"):
                    self._p10_pe = {}
                self._p10_pe[(tcg, ic)] = pe.n
            pen = pe.n
            for tcl in range(4):
                tc = tcg * 4 + tcl
                p10s = self.sem("p10x")
                @block.sync
                def _(sync):
                    sync.wait_ge(vec.h, getattr(self, "_p10_xc", 0))
                    p10s.inc(sync.dma_start(out=self.CHK[:, 0:D],
                                            in_=d["x2_h"][tc * 128:(tc + 1) * 128, :]), 16)
                ldn_x = p10s.n
                @block.vector
                def _(vector):
                    vector.wait_ge(pe.h, pen)
                    vector.wait_ge(p10s.h, ldn_x)
                    for hh in range(2):
                        vector.tensor_tensor(YC[:, hh * 512:(hh + 1) * 512],
                                             DP[:, (tcl * 2 + hh) * 512:(tcl * 2 + hh + 1) * 512],
                                             self.CHK[:, hh * 512:(hh + 1) * 512], ALU.add)
                    vector.drain()
                    vector.tensor_reduce(COLS[:, 0:1], YC[:], AX.X, ALU.add)
                    vector.drain()
                    vector.tensor_scalar_mul(COLS[:, 1:2], COLS[:, 0:1], float(1.0 / D))
                    vector.drain()
                    vec.inc(vector.tensor_scalar_sub(DB[:], YC[:], COLS[:, 1:2]))
                self._p10_xc = vec.n
                vecn1 = vec.n
                @block.scalar
                def _(scalar):
                    scalar.wait_ge(vec.h, vecn1)
                    scl.inc(scalar.activation(JUNK[:], DB[:], AF.Square, accum_out=COLS[:, 2:3]))
                scln1 = scl.n
                @block.vector
                def _(vector):
                    vector.wait_ge(scl.h, scln1)
                    vec.inc(vector.tensor_scalar(COLS[:, 3:4], COLS[:, 2:3], float(1.0 / D), float(EPS), ALU.mult, ALU.add))
                vecn2 = vec.n
                @block.scalar
                def _(scalar):
                    scalar.wait_ge(vec.h, vecn2)
                    scl.inc(scalar.sqrt(COLS[:, 4:5], COLS[:, 3:4]))
                scln2 = scl.n
                st_prev = getattr(self, "_p10_st", None)
                @block.vector
                def _(vector):
                    vector.wait_ge(scl.h, scln2)
                    vector.reciprocal(COLS[:, 5:6], COLS[:, 4:5])
                    vector.drain()
                    if st_prev is not None:
                        vector.wait_ge(self.ssem("st_out").h, st_prev)
                    vector.tensor_scalar(OUTC[:], DB[:], COLS[:, 5:6], None, ALU.mult)
                    vector.drain()
                    vec.inc(vector.tensor_copy(COLS[:, 7:8], COLS[:, 6:7]))
                vecn3 = vec.n
                stq = self.ssem("st_out")
                @block.gpsimd
                def _(gpsimd):
                    gpsimd.wait_ge(vec.h, vecn3)
                    stq.inc(gpsimd.dma_start(out=d["out"][tc * 128:(tc + 1) * 128, :], in_=OUTC[:]), 16)
                self._p10_st = stq.n
            if not hasattr(self, "_p10_evac"):
                self._p10_evac = {}
            self._p10_evac[tcg] = vec.n
        self.w("vector"); self.w("scalar"); self.w("gpsimd")
        self.barrier()
        ctx.close()

    def finish(self):
        block = self.block
        stn = self.stc.n
        store_waits = [(ss.h, ss.n) for ss in getattr(self, "_store_sems", set()) if ss.n]
        @block.gpsimd
        def _(gpsimd):
            if stn:
                gpsimd.wait_ge(self.stc.h, stn)
            for h, n in store_waits:
                gpsimd.wait_ge(h, n)


def host_inputs(L0_attn_x, L0_attn_ln_w, L0_attn_ln_b, L0_attn_Wq, L0_attn_bq,
                L0_attn_Wk, L0_attn_bk, L0_attn_Wv, L0_attn_bv, L0_attn_Wo, L0_attn_bo,
                L0_ffn_w_gate, L0_ffn_w_up, L0_ffn_w_down):
    f = np.float32
    perm = np.empty(D, np.int64)
    for h in range(NH):
        for i in range(32):
            perm[h * 64 + i] = h * 64 + 2 * i
            perm[h * 64 + 32 + i] = h * 64 + 2 * i + 1
    Wq = np.asarray(L0_attn_Wq, f)[perm, :]
    Wk = np.asarray(L0_attn_Wk, f)[perm, :]
    com = {
        "wqT": np.ascontiguousarray(Wq.T),
        "wkT": np.ascontiguousarray(Wk.T),
        "wvT": np.ascontiguousarray(np.asarray(L0_attn_Wv, f).T),
        "woT": np.ascontiguousarray(np.asarray(L0_attn_Wo, f).T),
        "bqc": np.ascontiguousarray(np.asarray(L0_attn_bq, f)[perm].reshape(NDC, 128).T),
        "bkc": np.ascontiguousarray(np.asarray(L0_attn_bk, f)[perm].reshape(NDC, 128).T),
        "bvc": np.ascontiguousarray(np.asarray(L0_attn_bv, f).reshape(NDC, 128).T),
        "bob": np.ascontiguousarray(np.broadcast_to(np.asarray(L0_attn_bo, f), (128, D))),
        "lnwb": np.ascontiguousarray(np.broadcast_to(np.asarray(L0_attn_ln_w, f), (128, D))),
        "lnbb": np.ascontiguousarray(np.broadcast_to(np.asarray(L0_attn_ln_b, f), (128, D))),
        "idn": np.eye(128, dtype=f),
        "onesr": np.ones((1, 128), f),
        "wup": np.ascontiguousarray(np.asarray(L0_ffn_w_up, f)),
        "wgate": np.ascontiguousarray(np.asarray(L0_ffn_w_gate, f)),
        "wdown": np.ascontiguousarray(np.asarray(L0_ffn_w_down, f)),
    }
    o2 = np.zeros((128, 2), f)
    o2[0:64, 0] = 1.0
    o2[64:128, 1] = 1.0
    com["ones2"] = o2
    inv = (1.0 / (f(10000.0) ** (np.arange(0, HD, 2, dtype=f) / f(HD)))).astype(f)
    t = np.arange(T, dtype=f)
    fr = np.outer(t, inv).astype(f)
    c = np.cos(fr).astype(f).T
    s = np.sin(fr).astype(f).T
    com["c2t"] = np.ascontiguousarray(np.concatenate([c, c, c, c], 0))
    com["s2t"] = np.ascontiguousarray(np.concatenate([-s, s, -s, s], 0))
    x = np.asarray(L0_attn_x, f)
    return [dict(com, xin=np.ascontiguousarray(x[b])) for b in range(B)]


def build(dbg=(), upto=99):
    kb = KB(dbg)
    kb.p0_consts()
    kb.p1_ln1()
    if upto >= 2:
        kb.p2_v()
    if upto >= 3:
        if upto >= 6:
            kb._p3_interleave_dequant = True
            kb._p6_init()
        kb.p3_attn()
    if upto >= 4:
        if upto >= 6:
            kb._p4_interleave_tr = True
            kb._p6t_init()
        kb.p4_wo()
    if upto >= 6:
        kb.p6_dequant()
    if upto >= 7:
        kb.p7_upgate()
    if upto >= 8:
        kb.p8_stageB()
    if upto >= 9:
        kb.p9_stageC()
    if upto >= 10:
        kb.p10_down()
    kb.finish()
    return kb


def kernel(**inputs):
    kb = build()
    in_maps = host_inputs(**inputs)
    res = run_bass_kernel_spmd(kb.nc, in_maps, core_ids=list(range(B)))
    return np.stack([np.asarray(r["out"]) for r in res.results], axis=0)


# version 11
# speedup vs baseline: 1.0468x; 1.0468x over previous
import numpy as np
from contextlib import ExitStack
import concourse.bass as bass
import concourse.mybir as mybir
from concourse.bass_utils import run_bass_kernel_spmd

F32 = mybir.dt.float32
ALU = mybir.AluOpType
AF = mybir.ActivationFunctionType
AX = mybir.AxisListType

B, T, D, NH, HD, II = 8, 2048, 1024, 16, 64, 3072
NTC = T // 128
NDC = D // 128
NIC = II // 128
MAGIC = float(1.5 * 2 ** 23)
QMAXR = float(np.float32(1.0) / np.float32(32767.0))
R7 = float(np.float32(1.0) / np.float32(7.0))
EPS = 1e-05
C16 = float(np.float32(1.0 / 16.0) * np.float32(1.00001))


class S:
    def __init__(self, nc, ctx, name):
        self.h = ctx.enter_context(nc.semaphore(name))
        self.n = 0
    def inc(self, ins, k=1):
        ins.then_inc(self.h, k)
        self.n += k
        return self.n


class Ring:
    def __init__(self):
        self.hist = []
    def push(self, v):
        self.hist.append(v)
    def old(self, back):
        return self.hist[-back] if len(self.hist) >= back else None


class KB:
    def __init__(self, dbg=()):
        self.dbg = set(dbg)
        nc = self.nc = bass.Bass("TRN2", target_bir_lowering=False)
        self.ctx = ExitStack()
        d = self.dram = {}
        def din(name, shape):
            d[name] = nc.dram_tensor(name, shape, F32, kind="ExternalInput").ap()
        def dint(name, shape):
            d[name] = nc.dram_tensor(name, shape, F32).ap()
        din("xin", [T, D])
        for w in ("wqT", "wkT", "wvT", "woT"):
            din(w, [D, D])
        din("bqc", [128, NDC]); din("bkc", [128, NDC]); din("bvc", [128, NDC])
        din("bob", [128, D]); din("lnwb", [128, D]); din("lnbb", [128, D])
        din("c2t", [128, T]); din("s2t", [128, T])
        din("idn", [128, 128]); din("onesr", [1, 128]); din("ones2", [128, 2])
        din("wup", [II, D]); din("wgate", [II, D]); din("wdown", [D, II])
        dint("v_h", [T, D]); dint("avT_h", [D, T]); dint("x2_h", [T, D])
        dint("up_h", [II, T]); dint("gate_h", [II, T]); dint("g2_h", [II, T]); dint("o_h", [II, T])
        dint("wupT_h", [D, II]); dint("wgateT_h", [D, II]); dint("wdT_h", [II, D])
        dint("wqoi_h", [72 * 128, 1024])
        for i in range(4):
            dint(f"ari{i}", [1, 2]); dint(f"aro{i}", [1, 2])
        d["out"] = nc.dram_tensor("out", [T, D], F32, kind="ExternalOutput").ap()
        self.tapd = {}

        ctx = self.ctx
        sb = lambda *a: ctx.enter_context(nc.sbuf_tensor(*a))
        self.BIGA = sb("BIGA", [128, NDC * T], F32)
        self.CHK = sb("CHK", [128, 2 * D], F32)
        self.IDN = sb("IDNs", [128, 128], F32)
        self.ONESRt = sb("ONESRs", [128, 128], F32); self.ONESR = self.ONESRt[0:1, :]; self.ONESR64 = self.ONESRt[64:65, :]
        self.ONES2 = sb("ONES2s", [128, 2], F32)
        self.GMX = sb("GMX", [128, 8], F32)
        self.MCOL = sb("MCOL", [128, 2], F32)
        self.MAGC = sb("MAGC", [128, 2], F32)
        self.TROWt = sb("TROW", [128, 128], F32); self.TROW = self.TROWt[0:2, :]
        self.SCLt = sb("SCL", [128, 16], F32); self.SCL = self.SCLt[0:1, :]
        self.SCLB = sb("SCLB", [128, 12], F32)
        self.taps = {}
        for name, shape in (("xn", [T, D]), ("xnT", [128, NDC * T]),
                            ("vtok", [T, D]), ("qrope", [128, T]), ("krope", [128, T]),
                            ("avT", [D, T]), ("x2", [T, D]), ("x2qT", [128, NDC * T]),
                            ("wupTq", [D, II]), ("wdTq", [II, D]),
                            ("up", [II, T]), ("gate", [II, T]), ("g2", [II, T]), ("o", [II, T]),
                            ("sup", [1, 16]), ("att0", [128, 4 * T])):
            if name in self.dbg:
                self.taps[name] = nc.dram_tensor("tap_" + name, shape, F32, kind="ExternalOutput").ap()

        mk = lambda name: S(nc, ctx, name)
        self.ld = mk("ld")
        self.gld = mk("gld")
        self.stc = mk("stc")
        self.vec = mk("vec")
        self.scl = mk("scl")
        self.pe = mk("pe")
        self.gp = mk("gp")
        self.bar = mk("bar")
        self.cc = mk("cc")

        self._sems = {}
        self._sem_pool = [S(nc, ctx, f"aux{i}") for i in range(48)]
        self.block = ctx.enter_context(nc.Block())
        self._worked = {"vector": False, "scalar": False, "gpsimd": False}

    def sem(self, name):
        if name not in self._sems:
            self._sems[name] = self._sem_pool.pop()
        return self._sems[name]

    def ssem(self, name):
        st = self.sem(name)
        if not hasattr(self, "_store_sems"):
            self._store_sems = set()
        self._store_sems.add(st)
        return st

    def w(self, eng):
        self._worked[eng] = True

    def barrier(self):
        block, nc = self.block, self.nc
        bar, stc = self.bar, self.stc
        n_inc = 0
        if self._worked["vector"]:
            @block.vector
            def _(vector):
                bar.inc(vector.tensor_copy(self.MCOL[:, 1:2], self.MCOL[:, 0:1]))
            n_inc += 1
        if self._worked["scalar"]:
            @block.scalar
            def _(scalar):
                bar.inc(scalar.copy(self.MCOL[0:1, 1:2], self.MCOL[0:1, 0:1]))
            n_inc += 1
        if self._worked["gpsimd"]:
            stn = stc.n
            store_waits = [(ss.h, ss.n) for ss in getattr(self, "_store_sems", set()) if ss.n]
            @block.gpsimd
            def _(gpsimd):
                if stn:
                    gpsimd.wait_ge(stc.h, stn)
                for h, n in store_waits:
                    gpsimd.wait_ge(h, n)
                bar.inc(gpsimd.tensor_copy(self.MCOL[:, 1:2], self.MCOL[:, 0:1]))
            n_inc += 1
        target = bar.n
        @block.sync
        def _(sync):
            sync.wait_ge(bar.h, target)
        @block.tensor
        def _(tensor):
            tensor.wait_ge(bar.h, target)
        @block.vector
        def _(vector):
            vector.wait_ge(bar.h, target)
        @block.scalar
        def _(scalar):
            scalar.wait_ge(bar.h, target)
        @block.gpsimd
        def _(gpsimd):
            gpsimd.wait_ge(bar.h, target)
        self._worked = {"vector": False, "scalar": False, "gpsimd": False}

    def p0_consts(self):
        block, d = self.block, self.dram
        cs = self.sem("const")
        @block.sync
        def _(sync):
            cs.inc(sync.dma_start(out=self.IDN[:], in_=d["idn"]), 16)
            cs.inc(sync.dma_start(out=self.ONESR, in_=d["onesr"]), 16)
            cs.inc(sync.dma_start(out=self.ONESR64, in_=d["onesr"]), 16)
            cs.inc(sync.dma_start(out=self.ONES2[:], in_=d["ones2"]), 16)
        self.ld_consts = cs.n
        @block.vector
        def _(vector):
            vector.memset(self.GMX[:], 0.0)
            self.vec.inc(vector.memset(self.MAGC[:], MAGIC))
        self.p0_vec = self.vec.n
        self.w("vector")

    def p1_ln1(self):
        nc, block, d = self.nc, self.block, self.dram
        ld, vec, scl, pe = self.ld, self.vec, self.scl, self.pe
        ctx = ExitStack()
        sb = lambda *a: ctx.enter_context(nc.sbuf_tensor(*a))
        LNWB = sb("LNWB", [128, D], F32)
        LNBB = sb("LNBB", [128, D], F32)
        DB = sb("DB", [128, D], F32)
        JUNK = sb("JUNK", [128, D], F32)
        COLS = sb("COLS", [128, 8], F32)
        XN = sb("XN", [128, D], F32)
        PT = ctx.enter_context(nc.psum_tensor("PT1", [128, 2 * D], F32))

        cs = self.sem("const")
        @block.sync
        def _(sync):
            cs.inc(sync.dma_start(out=LNWB[:], in_=d["lnwb"]), 16)
            cs.inc(sync.dma_start(out=LNBB[:], in_=d["lnbb"]), 16)
        ld_tabs = cs.n

        for tc in range(NTC):
            par = tc % 2
            xc = self.CHK[:, par * D:(par + 1) * D]
            vec_need = self.vec.n if tc >= 2 else None
            chks = self.sem(f"chk{par}")
            @block.sync
            def _(sync):
                if vec_need is not None:
                    sync.wait_ge(vec.h, vec_need)
                chks.inc(sync.dma_start(out=xc, in_=d["xin"][tc * 128:(tc + 1) * 128, :]), 16)
            ldn = chks.n

            xn_tap_wait = getattr(self, "_xn_tap_stc", None)
            @block.vector
            def _(vector):
                vector.wait_ge(chks.h, ldn)
                vector.wait_ge(self.sem("const").h, ld_tabs)
                if xn_tap_wait is not None:
                    vector.wait_ge(self.stc.h, xn_tap_wait)
                vector.tensor_reduce(COLS[:, 0:1], xc, AX.X, ALU.add)
                vector.drain()
                vector.tensor_scalar_mul(COLS[:, 1:2], COLS[:, 0:1], float(1.0 / D))
                vector.drain()
                vec.inc(vector.tensor_scalar_sub(DB[:], xc, COLS[:, 1:2]))
            vecn1 = vec.n
            @block.scalar
            def _(scalar):
                scalar.wait_ge(vec.h, vecn1)
                scl.inc(scalar.activation(JUNK[:], DB[:], AF.Square, accum_out=COLS[:, 2:3]))
            scln1 = scl.n
            @block.vector
            def _(vector):
                vector.wait_ge(scl.h, scln1)
                vec.inc(vector.tensor_scalar(COLS[:, 3:4], COLS[:, 2:3], float(1.0 / D), float(EPS), ALU.mult, ALU.add))
            vecn2 = vec.n
            @block.scalar
            def _(scalar):
                scalar.wait_ge(vec.h, vecn2)
                scl.inc(scalar.sqrt(COLS[:, 4:5], COLS[:, 3:4]))
            scln2 = scl.n
            @block.vector
            def _(vector):
                vector.wait_ge(scl.h, scln2)
                vector.reciprocal(COLS[:, 5:6], COLS[:, 4:5])
                vector.drain()
                vector.scalar_tensor_tensor(XN[:], DB[:], COLS[:, 5:6], LNWB[:], ALU.mult, ALU.mult)
                vector.drain()
                vec.inc(vector.tensor_tensor(XN[:], XN[:], LNBB[:], ALU.add))
            vecn3 = vec.n
            ppar = tc % 2
            pslice = PT[:, ppar * D:(ppar + 1) * D]
            pe_prev = pe.n
            @block.tensor
            def _(tensor):
                tensor.wait_ge(vec.h, vecn3)
                ins = None
                for dc in range(NDC):
                    ins = tensor.transpose(pslice[:, dc * 128:(dc + 1) * 128],
                                           XN[:, dc * 128:(dc + 1) * 128], self.IDN[:])
                pe.inc(ins)
            pen = pe.n
            @block.vector
            def _(vector):
                vector.wait_ge(pe.h, pen)
                dst = self.BIGA[:].rearrange("p (c t) -> p c t", c=NDC)[:, :, tc * 128:(tc + 1) * 128]
                src = pslice.rearrange("p (c t) -> p c t", c=NDC)
                vec.inc(vector.tensor_copy(dst, src))
            if "xn" in self.taps:
                vecn4 = vec.n
                @block.gpsimd
                def _(gpsimd):
                    gpsimd.wait_ge(vec.h, vecn4)
                    self.stc.inc(gpsimd.dma_start(out=self.taps["xn"][tc * 128:(tc + 1) * 128, :], in_=XN[:]), 16)
                self._xn_tap_stc = self.stc.n
                self.w("gpsimd")
        self.w("vector"); self.w("scalar")
        if "xnT" in self.taps:
            vecn = vec.n
            @block.gpsimd
            def _(gpsimd):
                gpsimd.wait_ge(vec.h, vecn)
                self.stc.inc(gpsimd.dma_start(out=self.taps["xnT"], in_=self.BIGA[:]), 16)
            self.w("gpsimd")
        self.barrier()
        ctx.close()

    def p2_v(self):
        nc, block, d = self.nc, self.block, self.dram
        ld, vec, pe, stc = self.ld, self.vec, self.pe, self.stc
        ctx = ExitStack()
        sb = lambda *a: ctx.enter_context(nc.sbuf_tensor(*a))
        WVS = sb("WVS", [128, 2 * D], F32)
        BVC = sb("BVC", [128, NDC], F32)
        VF = sb("VF", [128, T], F32)
        VTOK = sb("VTOK", [128, 256], F32)
        VP = ctx.enter_context(nc.psum_tensor("VP2", [128, 1024], F32))
        TP = ctx.enter_context(nc.psum_tensor("TP2", [128, 1024], F32))

        cs = self.sem("const")
        @block.sync
        def _(sync):
            cs.inc(sync.dma_start(out=BVC[:], in_=d["bvc"]), 16)
        ld_bvc = cs.n
        pe_after_proj = {}
        vec_after_evac = {}
        stc_after_store = {}
        tr_guard = 0
        for oc in range(NDC):
            par = oc % 2
            ws = WVS[:, par * D:(par + 1) * D]
            pe_need = pe_after_proj.get(oc - 2)
            wvss = self.sem(f"wvs{par}")
            @block.sync
            def _(sync):
                if pe_need is not None:
                    sync.wait_ge(pe.h, pe_need)
                for dc in range(NDC):
                    wvss.inc(sync.dma_start(
                        out=ws[:, dc * 128:(dc + 1) * 128],
                        in_=d["wvT"][dc * 128:(dc + 1) * 128, oc * 128:(oc + 1) * 128]), 16)
            ldn = wvss.n
            for tq in range(4):
                vpar = tq % 2
                pslice = VP[:, vpar * 512:(vpar + 1) * 512]
                vec_need = vec_after_evac.get((oc, tq - 2)) or vec_after_evac.get((oc - 1, tq + 2))
                @block.tensor
                def _(tensor):
                    tensor.wait_ge(wvss.h, ldn)
                    if vec_need is not None:
                        tensor.wait_ge(vec.h, vec_need)
                    ins = None
                    for dc in range(NDC):
                        ins = tensor.matmul(pslice, ws[:, dc * 128:(dc + 1) * 128],
                                            self.BIGA[:, dc * T + tq * 512: dc * T + (tq + 1) * 512],
                                            start=(dc == 0), stop=(dc == NDC - 1))
                    pe.inc(ins)
                pen = pe.n
                tg = tr_guard
                @block.vector
                def _(vector):
                    vector.wait_ge(pe.h, max(pen, tg))
                    vector.wait_ge(self.sem("const").h, ld_bvc)
                    vec.inc(vector.tensor_scalar(VF[:, tq * 512:(tq + 1) * 512], pslice,
                                                 BVC[:, oc:oc + 1], None, ALU.add))
                vec_after_evac[(oc, tq)] = vec.n
            pe_after_proj[oc] = pe.n
            for tc in range(NTC):
                tpar = tc % 2
                tps = TP[:, tpar * 512:tpar * 512 + 128]
                vtk = VTOK[:, tpar * 128:(tpar + 1) * 128]
                vec_need = vec_after_evac[(oc, 3)]
                evac_need = vec_after_evac.get(("t", oc, tc - 2))
                @block.tensor
                def _(tensor):
                    tensor.wait_ge(vec.h, max(vec_need, evac_need or 0))
                    pe.inc(tensor.transpose(tps, VF[:, tc * 128:(tc + 1) * 128], self.IDN[:]))
                pen = pe.n
                st_need = stc_after_store.get((oc, tc - 2))
                @block.vector
                def _(vector):
                    vector.wait_ge(pe.h, pen)
                    if st_need is not None:
                        vector.wait_ge(self.ssem("st_v").h, st_need)
                    vec.inc(vector.tensor_copy(vtk, tps))
                vec_after_evac[("t", oc, tc)] = vec.n
                vecn = vec.n
                stv = self.ssem("st_v")
                @block.gpsimd
                def _(gpsimd):
                    gpsimd.wait_ge(vec.h, vecn)
                    stv.inc(gpsimd.dma_start(
                        out=d["v_h"][tc * 128:(tc + 1) * 128, oc * 128:(oc + 1) * 128], in_=vtk), 16)
                stc_after_store[(oc, tc)] = stv.n
            tr_guard = pe.n
        self.w("vector"); self.w("gpsimd")
        self.barrier()
        if "vtok" in self.taps:
            @block.gpsimd
            def _(gpsimd):
                self.stc.inc(gpsimd.dma_start(out=self.taps["vtok"], in_=d["v_h"]), 16)
            self.w("gpsimd")
            self.barrier()
        ctx.close()

    def p3_attn(self, stage=7, npairs=8, nheads=2, nkc=NTC):
        nc, block, d = self.nc, self.block, self.dram
        ld, vec, scl, pe, stc = self.ld, self.vec, self.scl, self.pe, self.stc
        ctx = ExitStack()
        sb = lambda *a: ctx.enter_context(nc.sbuf_tensor(*a))
        WQS = sb("WQS", [128, 2 * D], F32)
        WKS = sb("WKS", [128, 2 * D], F32)
        BQC = sb("BQC", [128, NDC], F32)
        BKC = sb("BKC", [128, NDC], F32)
        C2 = sb("C2", [128, T], F32)
        S2 = sb("S2", [128, T], F32)
        SC1 = sb("SC1", [128, T], F32)
        SC2 = sb("SC2", [128, T], F32)
        SC3 = sb("SC3", [128, T], F32)
        QRB = sb("QRB", [128, T], F32)
        P1 = sb("P1", [128, T], F32)
        STATR = sb("STATR", [128, 3 * T], F32)
        QSS = STATR[0:2, 0:T]
        KSS = STATR[0:2, T:2 * T]
        BSR = STATR[0:2, 2 * T:3 * T]
        MQt = sb("MQ", [128, 2], F32)
        MQ = MQt[0:2, :]
        BIASP = sb("BIASP", [128, 32], F32)
        ALPHP = sb("ALPHP", [128, 32], F32)
        VLD = sb("VLD", [128, 256], F32)
        VTT = sb("VTT", [128, 130], F32)
        RBt = sb("RBt", [128, 1024], F32)
        self.taps_ops_buf = sb("TOB", [128, 512], F32)
        RROW = RBt[64:65, 0:512]
        BCS = RBt[0:64, 512:1024]
        AVEt = sb("AVE", [128, 1024], F32)
        AVE = AVEt[0:64, :]
        SCP = ctx.enter_context(nc.psum_tensor("SCP3", [128, T], F32))
        OPS = [ctx.enter_context(nc.psum_tensor(f"OPS{i}", [65, 512], F32)) for i in range(4)]

        cs = self.sem("const")
        @block.sync
        def _(sync):
            cs.inc(sync.dma_start(out=BQC[:], in_=d["bqc"]), 16)
            cs.inc(sync.dma_start(out=BKC[:], in_=d["bkc"]), 16)
            cs.inc(sync.dma_start(out=C2[:], in_=d["c2t"]), 16)
            cs.inc(sync.dma_start(out=S2[:], in_=d["s2t"]), 16)
        ld_tabs = cs.n

        bcs_guard = 0
        vld_ring = Ring()
        ave_ring = Ring()
        exp_last = {0: 0, 1: 0}
        p1_pe = {0: 0, 1: 0}
        scp_vec_guard = 0
        scp_scl_guard = 0
        pe_pair_w = Ring()

        for p in range(npairs):
            if getattr(self, "_p3_interleave_dequant", False):
                self._p6_emit(9)
            wpar = p % 2
            wq = WQS[:, wpar * D:(wpar + 1) * D]
            wk = WKS[:, wpar * D:(wpar + 1) * D]
            pe_need = pe_pair_w.old(2)
            wqss = self.sem(f"wqs{wpar}")
            @block.sync
            def _(sync):
                if pe_need is not None:
                    sync.wait_ge(pe.h, pe_need)
                for dc in range(NDC):
                    wqss.inc(sync.dma_start(
                        out=wq[:, dc * 128:(dc + 1) * 128],
                        in_=d["wqT"][dc * 128:(dc + 1) * 128, p * 128:(p + 1) * 128]), 16)
                    wqss.inc(sync.dma_start(
                        out=wk[:, dc * 128:(dc + 1) * 128],
                        in_=d["wkT"][dc * 128:(dc + 1) * 128, p * 128:(p + 1) * 128]), 16)
            ldn_w = wqss.n

            vg, sg = scp_vec_guard, scp_scl_guard
            evac_n = {}
            for wi, (ws, dst, bcol) in enumerate(((wq, SC1, BQC), (wk, SC2, BKC))):
                for tq in range(4):
                    pslice = SCP[:, tq * 512:(tq + 1) * 512]
                    bank_guard = evac_n.get(tq)
                    @block.tensor
                    def _(tensor):
                        tensor.wait_ge(wqss.h, ldn_w)
                        tensor.wait_ge(self.sem("const").h, ld_tabs)
                        if wi == 0 and tq == 0:
                            if vg:
                                tensor.wait_ge(vec.h, vg)
                            if sg:
                                tensor.wait_ge(scl.h, sg)
                        if bank_guard is not None:
                            tensor.wait_ge(vec.h, bank_guard)
                        ins = None
                        for dc in range(NDC):
                            ins = tensor.matmul(pslice, ws[:, dc * 128:(dc + 1) * 128],
                                                self.BIGA[:, dc * T + tq * 512: dc * T + (tq + 1) * 512],
                                                start=(dc == 0), stop=(dc == NDC - 1))
                        pe.inc(ins)
                    pen = pe.n
                    @block.vector
                    def _(vector):
                        vector.wait_ge(pe.h, pen)
                        vec.inc(vector.tensor_scalar(dst[:, tq * 512:(tq + 1) * 512], pslice,
                                                     bcol[:, p:p + 1], None, ALU.add))
                    evac_n[tq] = vec.n
            vec_proj = vec.n

            if stage < 2:
                if p == 0 and "qrope" in self.taps:
                    vpn = vec.n
                    @block.gpsimd
                    def _(gpsimd):
                        gpsimd.wait_ge(vec.h, vpn)
                        stc.inc(gpsimd.dma_start(out=self.taps["qrope"], in_=SC1[:]), 16)
                        stc.inc(gpsimd.dma_start(out=self.taps["krope"], in_=SC2[:]), 16)
                scp_vec_guard = vec.n
                scp_scl_guard = scl.n
                pe_pair_w.push(pe.n)
                continue
            sq_pe = 0
            for (src, ss) in ((SC1, QSS), (SC2, KSS)):
                vecn = vec.n
                @block.scalar
                def _(scalar):
                    scalar.wait_ge(vec.h, vecn)
                    if sq_pe:
                        scalar.wait_ge(pe.h, sq_pe)
                    scl.inc(scalar.activation(SC3[:], src[:], AF.Square))
                scln = scl.n
                @block.tensor
                def _(tensor):
                    tensor.wait_ge(scl.h, scln)
                    tensor.wait_ge(vec.h, vec_proj)
                    ins = None
                    for tq in range(4):
                        ins = tensor.matmul(SCP[0:2, tq * 512:(tq + 1) * 512], self.ONES2[:],
                                            SC3[:, tq * 512:(tq + 1) * 512], start=True, stop=True)
                    pe.inc(ins)
                sq_pe = pe.n
                pen = pe.n
                @block.vector
                def _(vector):
                    vector.wait_ge(pe.h, pen)
                    vec.inc(vector.tensor_copy(ss, SCP[0:2, :]))

            @block.vector
            def _(vector):
                vector.tensor_reduce(MQ[:, 0:1], QSS, AX.X, ALU.max)
                vector.drain()
                vec.inc(vector.tensor_scalar(BSR, KSS, MQ[:, 0:1], -C16, ALU.add, ALU.mult))
            vecn_bsr = vec.n
            @block.tensor
            def _(tensor):
                tensor.wait_ge(vec.h, vecn_bsr)
                ins = None
                for kc in range(NTC):
                    ins = tensor.transpose(SCP[:, kc * 2:kc * 2 + 2],
                                           BSR[:, kc * 128:(kc + 1) * 128], self.IDN[0:2, 0:2])
                pe.inc(ins)
            pen_bt = pe.n
            @block.vector
            def _(vector):
                vector.wait_ge(pe.h, pen_bt)
                vec.inc(vector.tensor_copy(BIASP[:], SCP[:, 0:32]))
            vecn_bp = vec.n
            @block.scalar
            def _(scalar):
                scalar.wait_ge(vec.h, vecn_bp)
                scl.inc(scalar.activation(ALPHP[:], BIASP[:], AF.Exp, scale=-1.0))
            scl_alpha = scl.n

            if stage < 3:
                scp_vec_guard = vec.n
                scp_scl_guard = scl.n
                pe_pair_w.push(pe.n)
                continue
            @block.gpsimd
            def _(gpsimd):
                gpsimd.wait_ge(pe.h, sq_pe)
                for h2 in range(2):
                    b = h2 * 64
                    self.gld.inc(gpsimd.dma_start(out=SC3[b:b + 32, :], in_=SC1[b + 32:b + 64, :]), 16)
                    self.gld.inc(gpsimd.dma_start(out=SC3[b + 32:b + 64, :], in_=SC1[b:b + 32, :]), 16)
            gldn = self.gld.n
            @block.vector
            def _(vector):
                vector.wait_ge(self.gld.h, gldn)
                vector.tensor_tensor(P1[:], SC1[:], C2[:], ALU.mult)
                vector.tensor_tensor(QRB[:], SC3[:], S2[:], ALU.mult)
                vector.drain()
                vec.inc(vector.tensor_tensor(SC1[:], P1[:], QRB[:], ALU.add))
            vec_qrope = vec.n
            @block.gpsimd
            def _(gpsimd):
                gpsimd.wait_ge(vec.h, vec_qrope)
                for h2 in range(2):
                    b = h2 * 64
                    self.gld.inc(gpsimd.dma_start(out=SC3[b:b + 32, :], in_=SC2[b + 32:b + 64, :]), 16)
                    self.gld.inc(gpsimd.dma_start(out=SC3[b + 32:b + 64, :], in_=SC2[b:b + 32, :]), 16)
            gldn = self.gld.n
            @block.vector
            def _(vector):
                vector.wait_ge(self.gld.h, gldn)
                vector.tensor_tensor(P1[:], SC2[:], C2[:], ALU.mult)
                vector.tensor_tensor(QRB[:], SC3[:], S2[:], ALU.mult)
                vector.drain()
                vec.inc(vector.tensor_tensor(SC2[:], P1[:], QRB[:], ALU.add))
            vec_krope = vec.n
            if p == 0 and "qrope" in self.taps:
                @block.gpsimd
                def _(gpsimd):
                    gpsimd.wait_ge(vec.h, vec_qrope)
                    stc.inc(gpsimd.dma_start(out=self.taps["qrope"], in_=SC1[:]), 16)
                    gpsimd.wait_ge(vec.h, vec_krope)
                    stc.inc(gpsimd.dma_start(out=self.taps["krope"], in_=SC2[:]), 16)

            if stage < 4:
                scp_vec_guard = vec.n
                scp_scl_guard = scl.n
                pe_pair_w.push(pe.n)
                continue
            for h2 in range(nheads):
                hb = h2 * 64
                for kc in range(nkc):
                    vpar = kc % 2
                    vld_t = VLD[:, vpar * 128:(vpar + 1) * 128]
                    vtt = VTT[:, vpar * 65:(vpar + 1) * 65]
                    slot_free = vld_ring.old(2)
                    vlds = self.sem(f"vld{vpar}")
                    @block.sync
                    def _(sync):
                        if slot_free is not None:
                            sync.wait_ge(pe.h, slot_free)
                        vlds.inc(sync.dma_start(out=vld_t,
                                                in_=d["v_h"][kc * 128:(kc + 1) * 128, p * 128:(p + 1) * 128]), 16)
                    ld_v = vlds.n
                    @block.vector
                    def _(vector):
                        vector.wait_ge(vlds.h, ld_v)
                        vector.wait_ge(scl.h, scl_alpha)
                        if slot_free is not None:
                            vector.wait_ge(pe.h, slot_free)
                        vector.tensor_scalar(vtt[:, 0:64], vld_t[:, hb:hb + 64],
                                             ALPHP[:, kc * 2 + h2:kc * 2 + h2 + 1], None, ALU.mult)
                        vec.inc(vector.tensor_copy(vtt[:, 64:65], ALPHP[:, kc * 2 + h2:kc * 2 + h2 + 1]))
                    vec_vtt = vec.n
                    scl_exp_q = [0, 0]
                    for qh in range(2):
                        sc_ps = SCP[:, qh * 1024:(qh + 1) * 1024]
                        p1s = P1[:, qh * 1024:(qh + 1) * 1024]
                        el = exp_last[qh]
                        bg = bcs_guard
                        @block.tensor
                        def _(tensor):
                            tensor.wait_ge(vec.h, max(vec_krope, bg))
                            if el:
                                tensor.wait_ge(scl.h, el)
                            ins = None
                            for j in range(2):
                                ins = tensor.matmul(
                                    sc_ps[:, j * 512:(j + 1) * 512],
                                    SC2[hb:hb + 64, kc * 128:(kc + 1) * 128],
                                    SC1[hb:hb + 64, (qh * 2 + j) * 512:(qh * 2 + j + 1) * 512],
                                    start=True, stop=True)
                            pe.inc(ins)
                        pe_sc = pe.n
                        pf = p1_pe[qh]
                        @block.scalar
                        def _(scalar):
                            scalar.wait_ge(pe.h, pe_sc)
                            if pf:
                                scalar.wait_ge(pe.h, pf)
                            scl.inc(scalar.activation(p1s, sc_ps, AF.Exp,
                                                      bias=BIASP[:, kc * 2 + h2:kc * 2 + h2 + 1],
                                                      scale=0.125))
                        exp_last[qh] = scl.n
                        scl_exp_q[qh] = scl.n
                    for qh in range(2):
                        p1s = P1[:, qh * 1024:(qh + 1) * 1024]
                        scl_exp = scl_exp_q[qh]
                        if p == 0 and h2 == 0 and kc == 0 and qh == 0 and "att0" in self.taps:
                            @block.gpsimd
                            def _(gpsimd):
                                gpsimd.wait_ge(scl.h, scl_exp)
                                stc.inc(gpsimd.dma_start(out=self.taps["att0"][:, 2 * T:2 * T + 32], in_=BIASP[:]), 16)
                                stc.inc(gpsimd.dma_start(out=self.taps["att0"][:, 2 * T + 32:2 * T + 64], in_=ALPHP[:]), 16)
                                stc.inc(gpsimd.dma_start(out=self.taps["att0"][:, 3 * T:3 * T + 1024], in_=P1[:, 0:1024]), 16)
                        @block.tensor
                        def _(tensor):
                            tensor.wait_ge(scl.h, scl_exp)
                            tensor.wait_ge(vec.h, vec_vtt)
                            ins = None
                            for j in range(2):
                                qc = qh * 2 + j
                                ins = tensor.matmul(OPS[qc][:, :], vtt,
                                                    p1s[:, j * 512:(j + 1) * 512],
                                                    start=(kc == 0), stop=(kc == nkc - 1))
                            pe.inc(ins)
                        p1_pe[qh] = pe.n
                    vld_ring.push(pe.n)
                pe_pv_done = pe.n
                if stage < 4.5:
                    continue
                for qc in range(4):
                    apar = qc % 2
                    ave = AVE[:, apar * 512:(apar + 1) * 512]
                    st_need = ave_ring.old(2)
                    sta_h = self.ssem("st_ave").h
                    @block.vector
                    def _(vector):
                        vector.wait_ge(pe.h, pe_pv_done)
                        vector.reciprocal(RROW, OPS[qc][64:65, :])
                        vector.drain()
                        vec.inc(vector.tensor_copy(self.MCOL[:, 1:2], self.MCOL[:, 0:1]))
                    vec_rr = vec.n
                    if stage >= 6:
                        @block.tensor
                        def _(tensor):
                            tensor.wait_ge(vec.h, vec_rr)
                            pe.inc(tensor.matmul(SCP[0:64, 0:512], self.ONESR64[:, 0:64], RROW,
                                                 start=True, stop=True))
                        pe_bc = pe.n
                        @block.scalar
                        def _(scalar):
                            scalar.wait_ge(pe.h, pe_bc)
                            scl.inc(scalar.copy(self.MCOL[0:1, 1:2], self.MCOL[0:1, 0:1]))
                        scl_bc = scl.n
                    if stage >= 7:
                        @block.vector
                        def _(vector):
                            vector.wait_ge(scl.h, scl_bc)
                            vector.wait_ge(pe.h, pe_bc)
                            if st_need is not None:
                                vector.wait_ge(sta_h, st_need)
                            vector.tensor_copy(BCS, SCP[0:64, 0:512])
                            vec.inc(vector.tensor_tensor(ave, OPS[qc][0:64, :], BCS, ALU.mult))
                        vec_ave = vec.n
                    else:
                        @block.vector
                        def _(vector):
                            if st_need is not None:
                                vector.wait_ge(sta_h, st_need)
                            if stage >= 6:
                                vector.wait_ge(pe.h, pe_bc)
                            vec.inc(vector.tensor_copy(ave, OPS[qc][0:64, :]))
                        vec_ave = vec.n
                    if p == 0 and h2 == 0 and qc == 0 and "att0" in self.taps:
                        @block.vector
                        def _(vector):
                            vec.inc(vector.tensor_copy(self.taps_ops_buf[0:65, :], OPS[qc][:, :]))
                        vo = vec.n
                        @block.gpsimd
                        def _(gpsimd):
                            gpsimd.wait_ge(vec.h, vo)
                            stc.inc(gpsimd.dma_start(out=self.taps["att0"][0:65, 0:512], in_=self.taps_ops_buf[0:65, :]), 16)
                            stc.inc(gpsimd.dma_start(out=self.taps["att0"][64:65, 512:1024], in_=RROW), 16)
                            stc.inc(gpsimd.dma_start(out=self.taps["att0"][0:64, 1024:1536], in_=BCS), 16)
                            stc.inc(gpsimd.dma_start(out=self.taps["att0"][0:64, 1536:2048], in_=ave), 16)
                        tapn = stc.n
                        @block.vector
                        def _(vector):
                            vector.wait_ge(stc.h, tapn)
                        vec_ave = vec.n
                    r0 = p * 128 + h2 * 64
                    sta = self.ssem("st_ave")
                    @block.gpsimd
                    def _(gpsimd):
                        gpsimd.wait_ge(vec.h, vec_ave)
                        sta.inc(gpsimd.dma_start(
                            out=d["avT_h"][r0:r0 + 64, qc * 512:(qc + 1) * 512], in_=ave), 16)
                    ave_ring.push(sta.n)
                bcs_guard = vec.n
            scp_vec_guard = vec.n
            scp_scl_guard = scl.n
            pe_pair_w.push(pe.n)
        self.w("vector"); self.w("scalar"); self.w("gpsimd")
        self.barrier()
        if "avT" in self.taps:
            @block.gpsimd
            def _(gpsimd):
                self.stc.inc(gpsimd.dma_start(out=self.taps["avT"], in_=d["avT_h"]), 16)
            self.w("gpsimd")
            self.barrier()
        ctx.close()

    def _qdq3(self, vector, dst, src, scol, rscol):
        vector.tensor_scalar(dst, src, rscol, MAGIC, ALU.mult, ALU.add)
        vector.tensor_scalar(dst, dst, MAGIC, 32767.0, ALU.subtract, ALU.min)
        vector.tensor_scalar(dst, dst, -32768.0, scol, ALU.max, ALU.mult)

    def _absmax_into(self, vector, src, gcol, tmpcol):
        vector.tensor_reduce(tmpcol, src, AX.X, ALU.max, apply_absolute_value=True)
        vector.drain()
        vector.tensor_tensor(gcol, gcol, tmpcol, ALU.max)
        vector.drain()

    def _allreduce_scales(self, ar_idx, c0, k, sclb_c0):
        nc, block, d = self.nc, self.block, self.dram
        ld, vec, pe, gp, cc, stc = self.ld, self.vec, self.pe, self.gp, self.cc, self.stc
        MQ2 = self.MCOL
        arctx = ExitStack()
        self.PSC = arctx.enter_context(nc.psum_tensor(f"PSC_{ar_idx}", [128, 256], F32))
        vecn0 = vec.n
        gpn0 = self.gp.n
        @block.tensor
        def _(tensor):
            tensor.wait_ge(vec.h, vecn0)
            if gpn0:
                tensor.wait_ge(self.gp.h, gpn0)
            pe.inc(tensor.transpose(self.PSC[0:k, 0:128], self.GMX[:, c0:c0 + k], self.IDN[:]))
        pen = pe.n
        @block.vector
        def _(vector):
            vector.wait_ge(pe.h, pen)
            vector.tensor_copy(self.TROW[0:k, :], self.PSC[0:k, 0:128])
            vector.drain()
            vector.tensor_reduce(MQ2[0:k, 0:1], self.TROW[0:k, :], AX.X, ALU.max)
            vector.drain()
            vec.inc(vector.tensor_copy(MQ2[0:k, 1:2], MQ2[0:k, 0:1]))
        vecn = vec.n
        ari, aro = d[f"ari{ar_idx}"], d[f"aro{ar_idx}"]
        @block.gpsimd
        def _(gpsimd):
            gpsimd.wait_ge(vec.h, vecn)
            self.gld.inc(gpsimd.dma_start(out=ari[0:1, 0:k], in_=MQ2[0:k, 0:1]), 16)
            if k == 1:
                self.gld.inc(gpsimd.dma_start(out=ari[0:1, 1:2], in_=MQ2[0:1, 0:1]), 16)
            gpsimd.wait_ge(self.gld.h, self.gld.n)
            gpsimd.collective_compute(
                "AllReduce", ALU.max, replica_groups=[list(range(B))],
                ins=[ari], outs=[aro],
            ).then_inc(cc.h, 1)
            cc.n += 1
            gpsimd.wait_ge(cc.h, cc.n)
            self.gld.inc(gpsimd.dma_start(out=self.SCL[0:1, 0:k], in_=aro[0:1, 0:k]), 16)
        gldn = self.gld.n
        @block.vector
        def _(vector):
            vector.wait_ge(self.gld.h, gldn)
            vector.tensor_scalar(self.SCL[0:1, 4:4 + k], self.SCL[0:1, 0:k], QMAXR, 1e-12, ALU.mult, ALU.max)
            vector.drain()
            vector.reciprocal(self.SCL[0:1, 4 + k:4 + 2 * k], self.SCL[0:1, 4:4 + k])
            vector.drain()
            vec.inc(vector.tensor_copy(self.SCL[0:1, 12:13], self.SCL[0:1, 0:1]))
        vecn2 = vec.n
        @block.tensor
        def _(tensor):
            tensor.wait_ge(vec.h, vecn2)
            pe.inc(tensor.matmul(self.PSC[0:128, 0:2 * k], self.ONESR[:, 0:128],
                                 self.SCL[0:1, 4:4 + 2 * k], start=True, stop=True))
        pen2 = pe.n
        @block.vector
        def _(vector):
            vector.wait_ge(pe.h, pen2)
            vec.inc(vector.tensor_copy(self.SCLB[:, sclb_c0:sclb_c0 + 2 * k], self.PSC[0:128, 0:2 * k]))
        self.w("vector"); self.w("gpsimd")
        self.barrier()
        arctx.close()

    def p4_wo(self):
        nc, block, d = self.nc, self.block, self.dram
        ld, vec, scl, pe, stc = self.ld, self.vec, self.scl, self.pe, self.stc
        ctx = ExitStack()
        sb = lambda *a: ctx.enter_context(nc.sbuf_tensor(*a))
        WBIG = sb("WBIG", [128, 4096], F32)
        X2H0 = sb("X2H0", [128, 8192], F32)
        AVB = sb("AVB", [128, 2048], F32)
        BOB = sb("BOB", [128, D], F32)
        YC = sb("YC", [128, D], F32)
        DB = sb("DB4", [128, D], F32)
        JUNK = sb("JUNK4", [128, D], F32)
        COLS = sb("COLS4", [128, 8], F32)
        X2C = sb("X2C", [128, D], F32)
        PW = ctx.enter_context(nc.psum_tensor("PW4", [128, 1024], F32))

        cs = self.sem("const")
        @block.sync
        def _(sync):
            cs.inc(sync.dma_start(out=BOB[:], in_=d["bob"]), 16)
        ld_bob = cs.n
        av_ring = Ring()
        pw_ring = Ring()
        for dh in range(2):
            pe_w = getattr(self, "_p4_pe_w", None)
            wbigs = self.sem("wbig")
            @block.sync
            def _(sync):
                if pe_w is not None:
                    sync.wait_ge(pe.h, pe_w)
                for dc in range(NDC):
                    wbigs.inc(sync.dma_start(out=WBIG[:, dc * 512:(dc + 1) * 512],
                                             in_=d["woT"][dc * 128:(dc + 1) * 128, dh * 512:(dh + 1) * 512]), 16)
            ldn_w = wbigs.n
            for tc in range(NTC):
                if getattr(self, "_p4_interleave_tr", False):
                    self._p6t_emit(2)
                apar = tc % 2
                avb = AVB[:, apar * 1024:(apar + 1) * 1024]
                slot_free = av_ring.old(2)
                avbs = self.sem(f"avb{apar}")
                @block.sync
                def _(sync):
                    if slot_free is not None:
                        sync.wait_ge(pe.h, slot_free)
                    for dc in range(NDC):
                        avbs.inc(sync.dma_start(out=avb[:, dc * 128:(dc + 1) * 128],
                                                in_=d["avT_h"][dc * 128:(dc + 1) * 128, tc * 128:(tc + 1) * 128]), 16)
                ldn_av = avbs.n
                ppar = tc % 2
                pw = PW[:, ppar * 512:(ppar + 1) * 512]
                vfree = pw_ring.old(2)
                @block.tensor
                def _(tensor):
                    tensor.wait_ge(wbigs.h, ldn_w)
                    tensor.wait_ge(avbs.h, ldn_av)
                    if vfree is not None:
                        tensor.wait_ge(vec.h, vfree)
                    ins = None
                    for dc in range(NDC):
                        ins = tensor.matmul(pw, avb[:, dc * 128:(dc + 1) * 128],
                                            WBIG[:, dc * 512:(dc + 1) * 512],
                                            start=(dc == 0), stop=(dc == NDC - 1))
                    pe.inc(ins)
                av_ring.push(pe.n)
                pen = pe.n
                if dh == 0:
                    p4xs = self.sem("p4x")
                    @block.sync
                    def _(sync):
                        sync.wait_ge(vec.h, getattr(self, "_p4_xc", 0))
                        p4xs.inc(sync.dma_start(out=self.CHK[:, 0:512],
                                                in_=d["xin"][tc * 128:(tc + 1) * 128, 0:512]), 16)
                    ldn_x = p4xs.n
                    @block.vector
                    def _(vector):
                        vector.wait_ge(pe.h, pen)
                        vector.wait_ge(p4xs.h, ldn_x)
                        vector.wait_ge(self.sem("const").h, ld_bob)
                        dst = X2H0[:, tc * 512:(tc + 1) * 512]
                        vector.tensor_tensor(dst, pw, self.CHK[:, 0:512], ALU.add)
                        vector.drain()
                        vector.tensor_tensor(dst, dst, BOB[:, 0:512], ALU.add)
                        vector.drain()
                        vec.inc(vector.tensor_tensor(dst, dst, self.CHK[:, 0:512], ALU.add))
                    self._p4_xc = vec.n
                    pw_ring.push(vec.n)
                else:
                    p4xs = self.sem("p4x")
                    @block.sync
                    def _(sync):
                        sync.wait_ge(vec.h, getattr(self, "_p4_xc", 0))
                        p4xs.inc(sync.dma_start(out=self.CHK[:, 0:512],
                                                in_=d["xin"][tc * 128:(tc + 1) * 128, 512:1024]), 16)
                    ldn_x = p4xs.n
                    st_prev = getattr(self, "_p4_st", None)
                    @block.vector
                    def _(vector):
                        vector.wait_ge(pe.h, pen)
                        vector.wait_ge(p4xs.h, ldn_x)
                        dsty = YC[:, 512:1024]
                        vector.tensor_tensor(dsty, pw, self.CHK[:, 0:512], ALU.add)
                        vector.drain()
                        vector.tensor_tensor(dsty, dsty, BOB[:, 512:1024], ALU.add)
                        vector.drain()
                        vector.tensor_tensor(dsty, dsty, self.CHK[:, 0:512], ALU.add)
                        vector.tensor_copy(YC[:, 0:512], X2H0[:, tc * 512:(tc + 1) * 512])
                        vector.drain()
                        vector.tensor_reduce(COLS[:, 0:1], YC[:], AX.X, ALU.add)
                        vector.drain()
                        vector.tensor_scalar_mul(COLS[:, 1:2], COLS[:, 0:1], float(1.0 / D))
                        vector.drain()
                        vec.inc(vector.tensor_scalar_sub(DB[:], YC[:], COLS[:, 1:2]))
                    self._p4_xc = vec.n
                    vecn1 = vec.n
                    @block.scalar
                    def _(scalar):
                        scalar.wait_ge(vec.h, vecn1)
                        scl.inc(scalar.activation(JUNK[:], DB[:], AF.Square, accum_out=COLS[:, 2:3]))
                    scln1 = scl.n
                    @block.vector
                    def _(vector):
                        vector.wait_ge(scl.h, scln1)
                        vec.inc(vector.tensor_scalar(COLS[:, 3:4], COLS[:, 2:3], float(1.0 / D), float(EPS), ALU.mult, ALU.add))
                    vecn2 = vec.n
                    @block.scalar
                    def _(scalar):
                        scalar.wait_ge(vec.h, vecn2)
                        scl.inc(scalar.sqrt(COLS[:, 4:5], COLS[:, 3:4]))
                    scln2 = scl.n
                    @block.vector
                    def _(vector):
                        vector.wait_ge(scl.h, scln2)
                        vector.reciprocal(COLS[:, 5:6], COLS[:, 4:5])
                        vector.drain()
                        if st_prev is not None:
                            vector.wait_ge(self.ssem("st_x2").h, st_prev)
                        vector.tensor_scalar(X2C[:], DB[:], COLS[:, 5:6], None, ALU.mult)
                        vector.drain()
                        self._absmax_into(vector, X2C[:], self.GMX[:, 0:1], COLS[:, 6:7])
                        vec.inc(vector.tensor_copy(COLS[:, 7:8], COLS[:, 6:7]))
                    vecn3 = vec.n
                    stx = self.ssem("st_x2")
                    @block.gpsimd
                    def _(gpsimd):
                        gpsimd.wait_ge(vec.h, vecn3)
                        stx.inc(gpsimd.dma_start(out=d["x2_h"][tc * 128:(tc + 1) * 128, :], in_=X2C[:]), 16)
                    self._p4_st = stx.n
                    pw_ring.push(self._p4_xc)
            self._p4_pe_w = pe.n
        self.w("vector"); self.w("scalar"); self.w("gpsimd")
        self.barrier()
        if "x2" in self.taps:
            @block.gpsimd
            def _(gpsimd):
                self.stc.inc(gpsimd.dma_start(out=self.taps["x2"], in_=d["x2_h"]), 16)
            self.w("gpsimd")
            self.barrier()
        self._allreduce_scales(0, 0, 1, 0)
        TRQ = ctx.enter_context(nc.psum_tensor("TRQ4", [128, 2048], F32))
        for tc in range(NTC):
            par = tc % 2
            xc = self.CHK[:, par * D:(par + 1) * D]
            vprev = getattr(self, "_p4b_vec", None)
            chks = self.sem(f"chk{par}")
            @block.sync
            def _(sync):
                if vprev is not None:
                    sync.wait_ge(vec.h, vprev)
                chks.inc(sync.dma_start(out=xc, in_=d["x2_h"][tc * 128:(tc + 1) * 128, :]), 16)
            ldn = chks.n
            @block.vector
            def _(vector):
                vector.wait_ge(chks.h, ldn)
                self._qdq3(vector, X2C[:], xc, self.SCLB[:, 0:1], self.SCLB[:, 1:2])
                vector.drain()
                vec.inc(vector.tensor_copy(COLS[:, 7:8], COLS[:, 6:7]))
            self._p4b_vec = vec.n
            vecn = vec.n
            pslice = TRQ[:, (tc % 2) * 1024:((tc % 2) + 1) * 1024]
            evac_prev = getattr(self, "_p4b_evac", {}).get(tc - 2)
            @block.tensor
            def _(tensor):
                tensor.wait_ge(vec.h, vecn)
                ins = None
                for dc in range(NDC):
                    ins = tensor.transpose(pslice[:, dc * 128:(dc + 1) * 128],
                                           X2C[:, dc * 128:(dc + 1) * 128], self.IDN[:])
                pe.inc(ins)
            pen = pe.n
            @block.vector
            def _(vector):
                vector.wait_ge(pe.h, pen)
                dst = self.BIGA[:].rearrange("p (c t) -> p c t", c=NDC)[:, :, tc * 128:(tc + 1) * 128]
                src = pslice.rearrange("p (c t) -> p c t", c=NDC)
                vec.inc(vector.tensor_copy(dst, src))
            if not hasattr(self, "_p4b_evac"):
                self._p4b_evac = {}
            self._p4b_evac[tc] = vec.n
        self.w("vector")
        self.barrier()
        if "x2qT" in self.taps:
            @block.gpsimd
            def _(gpsimd):
                self.stc.inc(gpsimd.dma_start(out=self.taps["x2qT"], in_=self.BIGA[:]), 16)
            self.w("gpsimd")
            self.barrier()
        ctx.close()

    def _p6_init(self):
        nc = self.nc
        ctx = self._p6_ctx = ExitStack()
        sb = lambda *a: ctx.enter_context(nc.sbuf_tensor(*a))
        self._P6B = dict(
            WCH=sb("WCH", [128, 2 * 1024], F32),
            SBK=sb("SBK", [128, 32], F32),
            RBK=sb("RBK", [128, 32], F32),
            WQV=sb("WQV", [128, 2 * 1024], F32),
        )
        jobs = [("wup", "wupT_h", II, D), ("wgate", "wgateT_h", II, D), ("wdown", "wdT_h", D, II)]
        self._p6_jobs = [(wname, dsth, rc, co) for (wname, dsth, RO, CO) in jobs
                         for rc in range(RO // 128) for co in range(0, CO, 1024)]
        self._p6_pos = 0
        self._p6_rings = (Ring(), Ring())

    def _p6_emit(self, n):
        if not hasattr(self, "_p6_jobs"):
            self._p6_init()
        nc, block, d = self.nc, self.block, self.dram
        ld, vec, stc = self.ld, self.vec, self.stc
        b = self._P6B
        WCH, SBK, RBK, WQV = b["WCH"], b["SBK"], b["RBK"], b["WQV"]
        wch_ring, st_ring = self._p6_rings
        todo = self._p6_jobs[self._p6_pos:self._p6_pos + n]
        for (wname, dsth, rc, co) in todo:
            u = self._p6_pos
            self._p6_pos += 1
            nb, CO = 32, 1024
            par = len(wch_ring.hist) % 2
            wch = WCH[:, par * 1024: par * 1024 + CO]
            wqv = WQV[:, par * 1024: par * 1024 + CO]
            vprev = wch_ring.old(2)
            wchs = self.sem(f"wch{par}")
            @block.sync
            def _(sync):
                if vprev is not None:
                    sync.wait_ge(vec.h, vprev)
                wchs.inc(sync.dma_start(out=wch, in_=d[wname][rc * 128:(rc + 1) * 128, co:co + CO]), 16)
            ldn = wchs.n
            st_prev = st_ring.old(2)
            @block.vector
            def _(vector):
                vector.wait_ge(wchs.h, ldn)
                if st_prev is not None:
                    vector.wait_ge(self.ssem("st_wq").h, st_prev)
                vector.tensor_reduce(SBK[:, 0:nb].rearrange("p (b o) -> p b o", o=1),
                                     wch.rearrange("p (b c) -> p b c", c=32),
                                     AX.X, ALU.max, apply_absolute_value=True)
                vector.drain()
                vector.tensor_scalar(SBK[:, 0:nb], SBK[:, 0:nb], R7, 1e-12, ALU.mult, ALU.max)
                vector.drain()
                vector.reciprocal(RBK[:, 0:nb], SBK[:, 0:nb])
                vector.drain()
                rb = RBK[:, 0:nb].rearrange("p (b o) -> p b o", o=1).broadcast_to([128, nb, 32])
                sb_ = SBK[:, 0:nb].rearrange("p (b o) -> p b o", o=1).broadcast_to([128, nb, 32])
                w3 = wqv.rearrange("p (b c) -> p b c", c=32)
                vector.tensor_tensor(w3, wch.rearrange("p (b c) -> p b c", c=32), rb, ALU.mult)
                vector.drain()
                vector.tensor_scalar(wqv, wqv, MAGIC, MAGIC, ALU.add, ALU.subtract)
                vector.tensor_scalar(wqv, wqv, 7.0, -8.0, ALU.min, ALU.max)
                vector.drain()
                vec.inc(vector.tensor_tensor(w3, w3, sb_, ALU.mult))
            wch_ring.push(vec.n)
            vecn = vec.n
            stw = self.ssem("st_wq")
            @block.gpsimd
            def _(gpsimd):
                gpsimd.wait_ge(vec.h, vecn)
                stw.inc(gpsimd.dma_start(out=d["wqoi_h"][u * 128:(u + 1) * 128, :], in_=wqv), 16)
            st_ring.push(stw.n)

    def _p6t_init(self):
        nc = self.nc
        ctx = self._p6t_ctx = ExitStack()
        sb = lambda *a: ctx.enter_context(nc.sbuf_tensor(*a))
        self._P6T = dict(
            WTL=sb("WTL", [128, 2 * 1024], F32),
            WEV=sb("WEV", [128, 1024], F32),
            TPD=ctx.enter_context(nc.psum_tensor("TPD6", [128, 1024], F32)),
        )
        self._p6t_pos = 0
        self._p6t_rings = (Ring(), Ring(), Ring())

    def _p6t_emit(self, n):
        if not hasattr(self, "_p6t_pos"):
            self._p6t_init()
        nc, block, d = self.nc, self.block, self.dram
        ld, vec, pe, stc = self.ld, self.vec, self.pe, self.stc
        WTL, WEV, TPD = self._P6T["WTL"], self._P6T["WEV"], self._P6T["TPD"]
        wtl_ring, ev_ring, st_ring = self._p6t_rings
        todo = self._p6_jobs[self._p6t_pos:self._p6t_pos + n]
        for (wname, dsth, rc, co) in todo:
            u = self._p6t_pos
            self._p6t_pos += 1
            par = len(wtl_ring.hist) % 2
            wtl = WTL[:, par * 1024:(par + 1) * 1024]
            pe_prev = wtl_ring.old(2)
            wtls = self.sem(f"wtl{par}")
            @block.sync
            def _(sync):
                if pe_prev is not None:
                    sync.wait_ge(pe.h, pe_prev)
                wtls.inc(sync.dma_start(out=wtl, in_=d["wqoi_h"][u * 128:(u + 1) * 128, :]), 16)
            ldn = wtls.n
            for g in range(2):
                gpar = len(ev_ring.hist) % 2
                tslice = self._P6T["TPD"][:, gpar * 512:(gpar + 1) * 512]
                eslice = WEV[:, gpar * 512:(gpar + 1) * 512]
                ev_free = ev_ring.old(2)
                @block.tensor
                def _(tensor):
                    tensor.wait_ge(wtls.h, ldn)
                    if ev_free is not None:
                        tensor.wait_ge(vec.h, ev_free)
                    ins = None
                    for j in range(4):
                        tj = g * 4 + j
                        ins = tensor.transpose(tslice[:, j * 128:(j + 1) * 128],
                                               wtl[:, tj * 128:(tj + 1) * 128], self.IDN[:])
                    pe.inc(ins)
                pen = pe.n
                st_free = st_ring.old(2)
                stt = self.ssem("st_wt")
                @block.vector
                def _(vector):
                    vector.wait_ge(pe.h, pen)
                    if st_free is not None:
                        vector.wait_ge(stt.h, st_free)
                    vec.inc(vector.tensor_copy(eslice, tslice))
                ev_ring.push(vec.n)
                vecn_e = vec.n
                @block.gpsimd
                def _(gpsimd):
                    gpsimd.wait_ge(vec.h, vecn_e)
                    ins = None
                    for j in range(4):
                        tj = g * 4 + j
                        ins = gpsimd.dma_start(out=d[dsth][co + tj * 128:co + (tj + 1) * 128,
                                                           rc * 128:(rc + 1) * 128],
                                               in_=eslice[:, j * 128:(j + 1) * 128])
                        stt.inc(ins, 16)
                st_ring.push(stt.n)
            wtl_ring.push(pe.n)

    def p6_dequant(self):
        if not hasattr(self, "_p6_jobs"):
            self._p6_init()
        self._p6_emit(len(self._p6_jobs) - self._p6_pos)
        if not hasattr(self, "_p6t_pos"):
            self._p6t_init()
        self._p6t_emit(len(self._p6_jobs) - self._p6t_pos)
        block, d = self.block, self.dram
        self.w("vector"); self.w("gpsimd")
        self.barrier()
        for tap, src in (("wupTq", "wupT_h"), ("wdTq", "wdT_h")):
            if tap in self.taps:
                @block.gpsimd
                def _(gpsimd):
                    self.stc.inc(gpsimd.dma_start(out=self.taps[tap], in_=d[src]), 16)
                self.w("gpsimd")
                self.barrier()
        self._p6t_ctx.close()
        self._p6_ctx.close()

    def p7_upgate(self):
        nc, block, d = self.nc, self.block, self.dram
        ld, vec, pe, stc = self.ld, self.vec, self.pe, self.stc
        ctx = ExitStack()
        sb = lambda *a: ctx.enter_context(nc.sbuf_tensor(*a))
        LHB = sb("LHB", [128, 2048], F32)
        UPC = sb("UPC", [128, 2048], F32)
        TMPC = sb("TMPC", [128, 2], F32)
        UPP = ctx.enter_context(nc.psum_tensor("UPP7", [128, 4096], F32))
        pe_ring = Ring(); vec_ring = Ring(); st_ring = Ring()
        for wi, (wh, dsth, gcol) in enumerate((("wupT_h", "up_h", 1), ("wgateT_h", "gate_h", 2))):
            for ic in range(NIC):
                par = len(pe_ring.hist) % 2
                lhb = LHB[:, par * 1024:(par + 1) * 1024]
                pe_prev = pe_ring.old(2)
                lhbs = self.sem(f"lhb{par}")
                @block.sync
                def _(sync):
                    if pe_prev is not None:
                        sync.wait_ge(pe.h, pe_prev)
                    for hc in range(NDC):
                        lhbs.inc(sync.dma_start(out=lhb[:, hc * 128:(hc + 1) * 128],
                                                in_=d[wh][hc * 128:(hc + 1) * 128, ic * 128:(ic + 1) * 128]), 16)
                ldn = lhbs.n
                vec_prev = vec_ring.old(2)
                for tq in range(4):
                    pslice = UPP[:, (par * 4 + tq) * 512:(par * 4 + tq + 1) * 512]
                    @block.tensor
                    def _(tensor):
                        tensor.wait_ge(lhbs.h, ldn)
                        if tq == 0 and vec_prev is not None:
                            tensor.wait_ge(vec.h, vec_prev)
                        ins = None
                        for hc in range(NDC):
                            ins = tensor.matmul(pslice, lhb[:, hc * 128:(hc + 1) * 128],
                                                self.BIGA[:, hc * T + tq * 512: hc * T + (tq + 1) * 512],
                                                start=(hc == 0), stop=(hc == NDC - 1))
                        pe.inc(ins)
                pe_ring.push(pe.n)
                pen = pe.n
                st_prev = st_ring.old(1)
                stu = self.ssem("st_upc")
                @block.vector
                def _(vector):
                    vector.wait_ge(pe.h, pen)
                    if st_prev is not None:
                        vector.wait_ge(stu.h, st_prev)
                    src = UPP[:, par * 2048:(par + 1) * 2048]
                    vector.tensor_copy(UPC[:], src)
                    vector.drain()
                    self._absmax_into(vector, UPC[:], self.GMX[:, gcol:gcol + 1], TMPC[:, 0:1])
                    vec.inc(vector.tensor_copy(TMPC[:, 1:2], TMPC[:, 0:1]))
                vec_ring.push(vec.n)
                vecn = vec.n
                @block.gpsimd
                def _(gpsimd):
                    gpsimd.wait_ge(vec.h, vecn)
                    stu.inc(gpsimd.dma_start(out=d[dsth][ic * 128:(ic + 1) * 128, :], in_=UPC[:]), 16)
                st_ring.push(stu.n)
        self.w("vector"); self.w("gpsimd")
        self.barrier()
        for tap, src in (("up", "up_h"), ("gate", "gate_h")):
            if tap in self.taps:
                @block.gpsimd
                def _(gpsimd):
                    self.stc.inc(gpsimd.dma_start(out=self.taps[tap], in_=d[src]), 16)
                self.w("gpsimd")
                self.barrier()
        ctx.close()
        self._allreduce_scales(1, 1, 2, 2)

    def p8_stageB(self):
        nc, block, d = self.nc, self.block, self.dram
        ld, vec, scl, stc = self.ld, self.vec, self.scl, self.stc
        ctx = ExitStack()
        sb = lambda *a: ctx.enter_context(nc.sbuf_tensor(*a))
        GCH = sb("GCH", [128, 2 * T], F32)
        GQt = sb("GQ", [128, 2 * T], F32)
        SGt = sb("SG", [128, 2 * T], F32)
        G2t = sb("G2", [128, 2 * T], F32)
        TMPC = sb("TMPC8", [128, 2], F32)
        gq_ring = Ring()
        for ic in range(NIC):
            par = ic % 2
            gch = GCH[:, par * T:(par + 1) * T]
            GQ = GQt[:, par * T:(par + 1) * T]
            SG = SGt[:, par * T:(par + 1) * T]
            G2 = G2t[:, par * T:(par + 1) * T]
            vprev = getattr(self, "_p8_vec", {}).get(ic - 2)
            gchs = self.sem(f"gch{par}")
            @block.sync
            def _(sync):
                if vprev is not None:
                    sync.wait_ge(vec.h, vprev)
                gchs.inc(sync.dma_start(out=gch, in_=d["gate_h"][ic * 128:(ic + 1) * 128, :]), 16)
            ldn = gchs.n
            st_prev = getattr(self, "_p8_st", {}).get(ic - 2)
            gqf = gq_ring.old(2)
            @block.scalar
            def _(scalar):
                scalar.wait_ge(gchs.h, ldn)
                if gqf is not None:
                    scalar.wait_ge(vec.h, gqf)
                scalar.wait_ge(vec.h, self.p0_vec)
                scl.inc(scalar.activation(GQ, gch, AF.Identity,
                                          bias=self.MAGC[:, 0:1], scale=self.SCLB[:, 5:6]))
            scl0 = scl.n
            @block.vector
            def _(vector):
                vector.wait_ge(scl.h, scl0)
                vector.tensor_scalar(GQ, GQ, MAGIC, 32767.0, ALU.subtract, ALU.min)
                vec.inc(vector.tensor_scalar(GQ, GQ, -32768.0, self.SCLB[:, 3:4], ALU.max, ALU.mult))
            vecn1 = vec.n
            @block.scalar
            def _(scalar):
                scalar.wait_ge(vec.h, vecn1)
                scalar.activation(SG, GQ, AF.Sigmoid)
                scl.inc(scalar.activation(SG, SG, AF.Identity, bias=self.MAGC[:, 0:1], scale=65536.0))
            self._p8_scl = scl.n
            scln = scl.n
            @block.vector
            def _(vector):
                vector.wait_ge(scl.h, scln)
                if st_prev is not None:
                    vector.wait_ge(self.ssem("st_g2").h, st_prev)
                vector.tensor_scalar(SG, SG, MAGIC, 65535.0, ALU.subtract, ALU.min)
                vector.tensor_scalar(SG, SG, 0.0, float(2.0 ** -16), ALU.max, ALU.mult)
                vector.drain()
                vector.tensor_tensor(G2, GQ, SG, ALU.mult)
                vector.drain()
                self._absmax_into(vector, G2, self.GMX[:, 3:4], TMPC[:, 0:1])
                vec.inc(vector.tensor_copy(TMPC[:, 1:2], TMPC[:, 0:1]))
            gq_ring.push(vec.n)
            if not hasattr(self, "_p8_vec"):
                self._p8_vec = {}
            self._p8_vec[ic] = vec.n
            vecn2 = vec.n
            stg = self.ssem("st_g2")
            @block.gpsimd
            def _(gpsimd):
                gpsimd.wait_ge(vec.h, vecn2)
                stg.inc(gpsimd.dma_start(out=d["g2_h"][ic * 128:(ic + 1) * 128, :], in_=G2), 16)
            if not hasattr(self, "_p8_st"):
                self._p8_st = {}
            self._p8_st[ic] = stg.n
        self.w("vector"); self.w("scalar"); self.w("gpsimd")
        self.barrier()
        if "g2" in self.taps:
            @block.gpsimd
            def _(gpsimd):
                self.stc.inc(gpsimd.dma_start(out=self.taps["g2"], in_=d["g2_h"]), 16)
            self.w("gpsimd")
            self.barrier()
        ctx.close()
        self._allreduce_scales(2, 3, 1, 6)

    def p9_stageC(self):
        nc, block, d = self.nc, self.block, self.dram
        ld, vec, scl, stc = self.ld, self.vec, self.scl, self.stc
        ctx = ExitStack()
        sb = lambda *a: ctx.enter_context(nc.sbuf_tensor(*a))
        ACH = sb("ACH", [128, 2 * T], F32)
        BCH = sb("BCH", [128, 2 * T], F32)
        AQt = sb("AQ", [128, 2 * T], F32)
        BQt = sb("BQ9", [128, 2 * T], F32)
        TMPC = sb("TMPC9", [128, 2], F32)
        aq_ring = Ring()
        for ic in range(NIC):
            par = ic % 2
            ach = ACH[:, par * T:(par + 1) * T]
            bch = BCH[:, par * T:(par + 1) * T]
            AQ = AQt[:, par * T:(par + 1) * T]
            BQ = BQt[:, par * T:(par + 1) * T]
            vprev = getattr(self, "_p9_vec", {}).get(ic - 2)
            achs = self.sem(f"ach{par}")
            @block.sync
            def _(sync):
                if vprev is not None:
                    sync.wait_ge(vec.h, vprev)
                achs.inc(sync.dma_start(out=ach, in_=d["g2_h"][ic * 128:(ic + 1) * 128, :]), 16)
                achs.inc(sync.dma_start(out=bch, in_=d["up_h"][ic * 128:(ic + 1) * 128, :]), 16)
            ldn = achs.n
            st_prev = getattr(self, "_p9_st", {}).get(ic - 2)
            aqf = aq_ring.old(2)
            @block.scalar
            def _(scalar):
                scalar.wait_ge(achs.h, ldn)
                if aqf is not None:
                    scalar.wait_ge(vec.h, aqf)
                if st_prev is not None:
                    scalar.wait_ge(self.ssem("st_o").h, st_prev)
                scalar.activation(AQ, ach, AF.Identity, bias=self.MAGC[:, 0:1], scale=self.SCLB[:, 7:8])
                scl.inc(scalar.activation(BQ, bch, AF.Identity, bias=self.MAGC[:, 0:1], scale=self.SCLB[:, 4:5]))
            scl0 = scl.n
            @block.vector
            def _(vector):
                vector.wait_ge(scl.h, scl0)
                if st_prev is not None:
                    vector.wait_ge(self.ssem("st_o").h, st_prev)
                vector.tensor_scalar(AQ, AQ, MAGIC, 32767.0, ALU.subtract, ALU.min)
                vector.tensor_scalar(AQ, AQ, -32768.0, self.SCLB[:, 6:7], ALU.max, ALU.mult)
                vector.tensor_scalar(BQ, BQ, MAGIC, 32767.0, ALU.subtract, ALU.min)
                vector.tensor_scalar(BQ, BQ, -32768.0, self.SCLB[:, 2:3], ALU.max, ALU.mult)
                vector.drain()
                vector.tensor_tensor(AQ, AQ, BQ, ALU.mult)
                vector.drain()
                self._absmax_into(vector, AQ, self.GMX[:, 4:5], TMPC[:, 0:1])
                vec.inc(vector.tensor_copy(TMPC[:, 1:2], TMPC[:, 0:1]))
            aq_ring.push(vec.n)
            if not hasattr(self, "_p9_vec"):
                self._p9_vec = {}
            self._p9_vec[ic] = vec.n
            vecn = vec.n
            sto = self.ssem("st_o")
            @block.gpsimd
            def _(gpsimd):
                gpsimd.wait_ge(vec.h, vecn)
                sto.inc(gpsimd.dma_start(out=d["o_h"][ic * 128:(ic + 1) * 128, :], in_=AQ), 16)
            if not hasattr(self, "_p9_st"):
                self._p9_st = {}
            self._p9_st[ic] = sto.n
        self.w("vector"); self.w("gpsimd")
        self.barrier()
        if "o" in self.taps:
            @block.gpsimd
            def _(gpsimd):
                self.stc.inc(gpsimd.dma_start(out=self.taps["o"], in_=d["o_h"]), 16)
            self.w("gpsimd")
            self.barrier()
        ctx.close()
        self._allreduce_scales(3, 4, 1, 8)

    def p10_down(self):
        nc, block, d = self.nc, self.block, self.dram
        ld, vec, scl, pe, stc = self.ld, self.vec, self.scl, self.pe, self.stc
        ctx = ExitStack()
        sb = lambda *a: ctx.enter_context(nc.sbuf_tensor(*a))
        OQB = sb("OQB", [128, 2 * 512], F32)
        OQD = sb("OQD", [128, 2 * 512], F32)
        WDB = sb("WDB", [128, 2 * 1024], F32)
        YC = sb("YC10", [128, D], F32)
        DB = sb("DB10", [128, D], F32)
        JUNK = sb("JUNK10", [128, D], F32)
        COLS = sb("COLS10", [128, 8], F32)
        OUTC = sb("OUTC", [128, D], F32)
        DP = ctx.enter_context(nc.psum_tensor("DP10", [128, 4096], F32))
        for tcg in range(4):
            for ic in range(NIC):
                par = ic % 2
                oqb = OQB[:, par * 512:(par + 1) * 512]
                oqd = OQD[:, par * 512:(par + 1) * 512]
                wdb = WDB[:, par * 1024:(par + 1) * 1024]
                pe_prev = getattr(self, "_p10_pe", {}).get((tcg, ic - 2))
                oqbs = self.sem(f"oqb{par}")
                @block.sync
                def _(sync):
                    if pe_prev is not None:
                        sync.wait_ge(pe.h, pe_prev)
                    oqbs.inc(sync.dma_start(out=oqb, in_=d["o_h"][ic * 128:(ic + 1) * 128,
                                                                  tcg * 512:(tcg + 1) * 512]), 16)
                    oqbs.inc(sync.dma_start(out=wdb, in_=d["wdT_h"][ic * 128:(ic + 1) * 128, :]), 16)
                ldn = oqbs.n
                @block.vector
                def _(vector):
                    vector.wait_ge(oqbs.h, ldn)
                    if pe_prev is not None:
                        vector.wait_ge(pe.h, pe_prev)
                    self._qdq3(vector, oqd, oqb, self.SCLB[:, 8:9], self.SCLB[:, 9:10])
                    vector.drain()
                    vec.inc(vector.tensor_copy(self.MCOL[:, 1:2], self.MCOL[:, 0:1]))
                vecn = vec.n
                vec_evac = getattr(self, "_p10_evac", {}).get(tcg - 1)
                @block.tensor
                def _(tensor):
                    tensor.wait_ge(vec.h, vecn)
                    if ic == 0 and vec_evac is not None:
                        tensor.wait_ge(vec.h, vec_evac)
                    ins = None
                    for tcl in range(4):
                        for hh in range(2):
                            ins = tensor.matmul(DP[:, (tcl * 2 + hh) * 512:(tcl * 2 + hh + 1) * 512],
                                                oqd[:, tcl * 128:(tcl + 1) * 128],
                                                wdb[:, hh * 512:(hh + 1) * 512],
                                                start=(ic == 0), stop=(ic == NIC - 1))
                    pe.inc(ins)
                if not hasattr(self, "_p10_pe"):
                    self._p10_pe = {}
                self._p10_pe[(tcg, ic)] = pe.n
            pen = pe.n
            for tcl in range(4):
                tc = tcg * 4 + tcl
                p10s = self.sem("p10x")
                @block.sync
                def _(sync):
                    sync.wait_ge(vec.h, getattr(self, "_p10_xc", 0))
                    p10s.inc(sync.dma_start(out=self.CHK[:, 0:D],
                                            in_=d["x2_h"][tc * 128:(tc + 1) * 128, :]), 16)
                ldn_x = p10s.n
                @block.vector
                def _(vector):
                    vector.wait_ge(pe.h, pen)
                    vector.wait_ge(p10s.h, ldn_x)
                    for hh in range(2):
                        vector.tensor_tensor(YC[:, hh * 512:(hh + 1) * 512],
                                             DP[:, (tcl * 2 + hh) * 512:(tcl * 2 + hh + 1) * 512],
                                             self.CHK[:, hh * 512:(hh + 1) * 512], ALU.add)
                    vector.drain()
                    vector.tensor_reduce(COLS[:, 0:1], YC[:], AX.X, ALU.add)
                    vector.drain()
                    vector.tensor_scalar_mul(COLS[:, 1:2], COLS[:, 0:1], float(1.0 / D))
                    vector.drain()
                    vec.inc(vector.tensor_scalar_sub(DB[:], YC[:], COLS[:, 1:2]))
                self._p10_xc = vec.n
                vecn1 = vec.n
                @block.scalar
                def _(scalar):
                    scalar.wait_ge(vec.h, vecn1)
                    scl.inc(scalar.activation(JUNK[:], DB[:], AF.Square, accum_out=COLS[:, 2:3]))
                scln1 = scl.n
                @block.vector
                def _(vector):
                    vector.wait_ge(scl.h, scln1)
                    vec.inc(vector.tensor_scalar(COLS[:, 3:4], COLS[:, 2:3], float(1.0 / D), float(EPS), ALU.mult, ALU.add))
                vecn2 = vec.n
                @block.scalar
                def _(scalar):
                    scalar.wait_ge(vec.h, vecn2)
                    scl.inc(scalar.sqrt(COLS[:, 4:5], COLS[:, 3:4]))
                scln2 = scl.n
                st_prev = getattr(self, "_p10_st", None)
                @block.vector
                def _(vector):
                    vector.wait_ge(scl.h, scln2)
                    vector.reciprocal(COLS[:, 5:6], COLS[:, 4:5])
                    vector.drain()
                    if st_prev is not None:
                        vector.wait_ge(self.ssem("st_out").h, st_prev)
                    vector.tensor_scalar(OUTC[:], DB[:], COLS[:, 5:6], None, ALU.mult)
                    vector.drain()
                    vec.inc(vector.tensor_copy(COLS[:, 7:8], COLS[:, 6:7]))
                vecn3 = vec.n
                stq = self.ssem("st_out")
                @block.gpsimd
                def _(gpsimd):
                    gpsimd.wait_ge(vec.h, vecn3)
                    stq.inc(gpsimd.dma_start(out=d["out"][tc * 128:(tc + 1) * 128, :], in_=OUTC[:]), 16)
                self._p10_st = stq.n
            if not hasattr(self, "_p10_evac"):
                self._p10_evac = {}
            self._p10_evac[tcg] = vec.n
        self.w("vector"); self.w("scalar"); self.w("gpsimd")
        self.barrier()
        ctx.close()

    def finish(self):
        block = self.block
        stn = self.stc.n
        store_waits = [(ss.h, ss.n) for ss in getattr(self, "_store_sems", set()) if ss.n]
        @block.gpsimd
        def _(gpsimd):
            if stn:
                gpsimd.wait_ge(self.stc.h, stn)
            for h, n in store_waits:
                gpsimd.wait_ge(h, n)


def host_inputs(L0_attn_x, L0_attn_ln_w, L0_attn_ln_b, L0_attn_Wq, L0_attn_bq,
                L0_attn_Wk, L0_attn_bk, L0_attn_Wv, L0_attn_bv, L0_attn_Wo, L0_attn_bo,
                L0_ffn_w_gate, L0_ffn_w_up, L0_ffn_w_down):
    f = np.float32
    perm = np.empty(D, np.int64)
    for h in range(NH):
        for i in range(32):
            perm[h * 64 + i] = h * 64 + 2 * i
            perm[h * 64 + 32 + i] = h * 64 + 2 * i + 1
    Wq = np.asarray(L0_attn_Wq, f)[perm, :]
    Wk = np.asarray(L0_attn_Wk, f)[perm, :]
    com = {
        "wqT": np.ascontiguousarray(Wq.T),
        "wkT": np.ascontiguousarray(Wk.T),
        "wvT": np.ascontiguousarray(np.asarray(L0_attn_Wv, f).T),
        "woT": np.ascontiguousarray(np.asarray(L0_attn_Wo, f).T),
        "bqc": np.ascontiguousarray(np.asarray(L0_attn_bq, f)[perm].reshape(NDC, 128).T),
        "bkc": np.ascontiguousarray(np.asarray(L0_attn_bk, f)[perm].reshape(NDC, 128).T),
        "bvc": np.ascontiguousarray(np.asarray(L0_attn_bv, f).reshape(NDC, 128).T),
        "bob": np.ascontiguousarray(np.broadcast_to(np.asarray(L0_attn_bo, f), (128, D))),
        "lnwb": np.ascontiguousarray(np.broadcast_to(np.asarray(L0_attn_ln_w, f), (128, D))),
        "lnbb": np.ascontiguousarray(np.broadcast_to(np.asarray(L0_attn_ln_b, f), (128, D))),
        "idn": np.eye(128, dtype=f),
        "onesr": np.ones((1, 128), f),
        "wup": np.ascontiguousarray(np.asarray(L0_ffn_w_up, f)),
        "wgate": np.ascontiguousarray(np.asarray(L0_ffn_w_gate, f)),
        "wdown": np.ascontiguousarray(np.asarray(L0_ffn_w_down, f)),
    }
    o2 = np.zeros((128, 2), f)
    o2[0:64, 0] = 1.0
    o2[64:128, 1] = 1.0
    com["ones2"] = o2
    inv = (1.0 / (f(10000.0) ** (np.arange(0, HD, 2, dtype=f) / f(HD)))).astype(f)
    t = np.arange(T, dtype=f)
    fr = np.outer(t, inv).astype(f)
    c = np.cos(fr).astype(f).T
    s = np.sin(fr).astype(f).T
    com["c2t"] = np.ascontiguousarray(np.concatenate([c, c, c, c], 0))
    com["s2t"] = np.ascontiguousarray(np.concatenate([-s, s, -s, s], 0))
    x = np.asarray(L0_attn_x, f)
    return [dict(com, xin=np.ascontiguousarray(x[b])) for b in range(B)]


def build(dbg=(), upto=99):
    kb = KB(dbg)
    kb.p0_consts()
    kb.p1_ln1()
    if upto >= 2:
        kb.p2_v()
    if upto >= 3:
        if upto >= 6:
            kb._p3_interleave_dequant = True
            kb._p6_init()
        kb.p3_attn()
    if upto >= 4:
        if upto >= 6:
            kb._p4_interleave_tr = True
            kb._p6t_init()
        kb.p4_wo()
    if upto >= 6:
        kb.p6_dequant()
    if upto >= 7:
        kb.p7_upgate()
    if upto >= 8:
        kb.p8_stageB()
    if upto >= 9:
        kb.p9_stageC()
    if upto >= 10:
        kb.p10_down()
    kb.finish()
    return kb


def kernel(**inputs):
    kb = build()
    in_maps = host_inputs(**inputs)
    res = run_bass_kernel_spmd(kb.nc, in_maps, core_ids=list(range(B)))
    return np.stack([np.asarray(r["out"]) for r in res.results], axis=0)


# version 12
# speedup vs baseline: 1.1009x; 1.0518x over previous
import numpy as np
from contextlib import ExitStack
import concourse.bass as bass
import concourse.mybir as mybir
from concourse.bass_utils import run_bass_kernel_spmd

F32 = mybir.dt.float32
ALU = mybir.AluOpType
AF = mybir.ActivationFunctionType
AX = mybir.AxisListType

B, T, D, NH, HD, II = 8, 2048, 1024, 16, 64, 3072
NTC = T // 128
NDC = D // 128
NIC = II // 128
MAGIC = float(1.5 * 2 ** 23)
QMAXR = float(np.float32(1.0) / np.float32(32767.0))
R7 = float(np.float32(1.0) / np.float32(7.0))
EPS = 1e-05
C16 = float(np.float32(1.0 / 16.0) * np.float32(1.00001))


class S:
    def __init__(self, nc, ctx, name):
        self.h = ctx.enter_context(nc.semaphore(name))
        self.n = 0
    def inc(self, ins, k=1):
        ins.then_inc(self.h, k)
        self.n += k
        return self.n


class Ring:
    def __init__(self):
        self.hist = []
    def push(self, v):
        self.hist.append(v)
    def old(self, back):
        return self.hist[-back] if len(self.hist) >= back else None


class KB:
    def __init__(self, dbg=()):
        self.dbg = set(dbg)
        nc = self.nc = bass.Bass("TRN2", target_bir_lowering=False)
        self.ctx = ExitStack()
        d = self.dram = {}
        def din(name, shape):
            d[name] = nc.dram_tensor(name, shape, F32, kind="ExternalInput").ap()
        def dint(name, shape):
            d[name] = nc.dram_tensor(name, shape, F32).ap()
        din("xin", [T, D])
        for w in ("wqT", "wkT", "wvT", "woT"):
            din(w, [D, D])
        din("bqc", [128, NDC]); din("bkc", [128, NDC]); din("bvc", [128, NDC])
        din("bob", [128, D]); din("lnwb", [128, D]); din("lnbb", [128, D])
        din("c2t", [128, T]); din("s2t", [128, T])
        din("idn", [128, 128]); din("onesr", [1, 128]); din("ones2", [128, 2])
        din("wup", [II, D]); din("wgate", [II, D]); din("wdown", [D, II])
        dint("v_h", [T, D]); dint("avT_h", [D, T]); dint("x2_h", [T, D])
        dint("up_h", [II, T]); dint("gate_h", [II, T]); dint("g2_h", [II, T]); dint("o_h", [II, T])
        dint("wupT_h", [D, II]); dint("wgateT_h", [D, II]); dint("wdT_h", [II, D])
        dint("wqoi_h", [72 * 128, 1024])
        for i in range(4):
            dint(f"ari{i}", [1, 2]); dint(f"aro{i}", [1, 2])
        d["out"] = nc.dram_tensor("out", [T, D], F32, kind="ExternalOutput").ap()
        self.tapd = {}

        ctx = self.ctx
        sb = lambda *a: ctx.enter_context(nc.sbuf_tensor(*a))
        self.BIGA = sb("BIGA", [128, NDC * T], F32)
        self.CHK = sb("CHK", [128, 2 * D], F32)
        self.IDN = sb("IDNs", [128, 128], F32)
        self.ONESRt = sb("ONESRs", [128, 128], F32); self.ONESR = self.ONESRt[0:1, :]; self.ONESR64 = self.ONESRt[64:65, :]
        self.ONES2 = sb("ONES2s", [128, 2], F32)
        self.GMX = sb("GMX", [128, 8], F32)
        self.MCOL = sb("MCOL", [128, 2], F32)
        self.MAGC = sb("MAGC", [128, 2], F32)
        self.TROWt = sb("TROW", [128, 128], F32); self.TROW = self.TROWt[0:2, :]
        self.SCLt = sb("SCL", [128, 16], F32); self.SCL = self.SCLt[0:1, :]
        self.SCLB = sb("SCLB", [128, 12], F32)
        self.taps = {}
        for name, shape in (("xn", [T, D]), ("xnT", [128, NDC * T]),
                            ("vtok", [T, D]), ("qrope", [128, T]), ("krope", [128, T]),
                            ("avT", [D, T]), ("x2", [T, D]), ("x2qT", [128, NDC * T]),
                            ("wupTq", [D, II]), ("wdTq", [II, D]),
                            ("up", [II, T]), ("gate", [II, T]), ("g2", [II, T]), ("o", [II, T]),
                            ("sup", [1, 16]), ("att0", [128, 4 * T])):
            if name in self.dbg:
                self.taps[name] = nc.dram_tensor("tap_" + name, shape, F32, kind="ExternalOutput").ap()

        mk = lambda name: S(nc, ctx, name)
        self.ld = mk("ld")
        self.gld = mk("gld")
        self.stc = mk("stc")
        self.vec = mk("vec")
        self.scl = mk("scl")
        self.pe = mk("pe")
        self.gp = mk("gp")
        self.bar = mk("bar")
        self.cc = mk("cc")

        self._sems = {}
        self._sem_pool = [S(nc, ctx, f"aux{i}") for i in range(48)]
        self.block = ctx.enter_context(nc.Block())
        self._worked = {"vector": False, "scalar": False, "gpsimd": False}

    def sem(self, name):
        if name not in self._sems:
            self._sems[name] = self._sem_pool.pop()
        return self._sems[name]

    def ssem(self, name):
        st = self.sem(name)
        if not hasattr(self, "_store_sems"):
            self._store_sems = set()
        self._store_sems.add(st)
        return st

    def w(self, eng):
        self._worked[eng] = True

    def barrier(self):
        block, nc = self.block, self.nc
        bar, stc = self.bar, self.stc
        n_inc = 0
        if self._worked["vector"]:
            @block.vector
            def _(vector):
                bar.inc(vector.tensor_copy(self.MCOL[:, 1:2], self.MCOL[:, 0:1]))
            n_inc += 1
        if self._worked["scalar"]:
            @block.scalar
            def _(scalar):
                bar.inc(scalar.copy(self.MCOL[0:1, 1:2], self.MCOL[0:1, 0:1]))
            n_inc += 1
        if self._worked["gpsimd"]:
            stn = stc.n
            store_waits = [(ss.h, ss.n) for ss in getattr(self, "_store_sems", set()) if ss.n]
            @block.gpsimd
            def _(gpsimd):
                if stn:
                    gpsimd.wait_ge(stc.h, stn)
                for h, n in store_waits:
                    gpsimd.wait_ge(h, n)
                bar.inc(gpsimd.tensor_copy(self.MCOL[:, 1:2], self.MCOL[:, 0:1]))
            n_inc += 1
        target = bar.n
        @block.sync
        def _(sync):
            sync.wait_ge(bar.h, target)
        @block.tensor
        def _(tensor):
            tensor.wait_ge(bar.h, target)
        @block.vector
        def _(vector):
            vector.wait_ge(bar.h, target)
        @block.scalar
        def _(scalar):
            scalar.wait_ge(bar.h, target)
        @block.gpsimd
        def _(gpsimd):
            gpsimd.wait_ge(bar.h, target)
        self._worked = {"vector": False, "scalar": False, "gpsimd": False}

    def p0_consts(self):
        block, d = self.block, self.dram
        cs = self.sem("const")
        @block.sync
        def _(sync):
            cs.inc(sync.dma_start(out=self.IDN[:], in_=d["idn"]), 16)
            cs.inc(sync.dma_start(out=self.ONESR, in_=d["onesr"]), 16)
            cs.inc(sync.dma_start(out=self.ONESR64, in_=d["onesr"]), 16)
            cs.inc(sync.dma_start(out=self.ONES2[:], in_=d["ones2"]), 16)
        self.ld_consts = cs.n
        @block.vector
        def _(vector):
            vector.memset(self.GMX[:], 0.0)
            self.vec.inc(vector.memset(self.MAGC[:], MAGIC))
        self.p0_vec = self.vec.n
        self.w("vector")

    def p1_ln1(self):
        nc, block, d = self.nc, self.block, self.dram
        ld, vec, scl, pe = self.ld, self.vec, self.scl, self.pe
        ctx = ExitStack()
        sb = lambda *a: ctx.enter_context(nc.sbuf_tensor(*a))
        LNWB = sb("LNWB", [128, D], F32)
        LNBB = sb("LNBB", [128, D], F32)
        DB = sb("DB", [128, D], F32)
        JUNK = sb("JUNK", [128, D], F32)
        COLS = sb("COLS", [128, 8], F32)
        XN = sb("XN", [128, D], F32)
        PT = ctx.enter_context(nc.psum_tensor("PT1", [128, 2 * D], F32))

        cs = self.sem("const")
        @block.sync
        def _(sync):
            cs.inc(sync.dma_start(out=LNWB[:], in_=d["lnwb"]), 16)
            cs.inc(sync.dma_start(out=LNBB[:], in_=d["lnbb"]), 16)
        ld_tabs = cs.n

        for tc in range(NTC):
            par = tc % 2
            xc = self.CHK[:, par * D:(par + 1) * D]
            vec_need = self.vec.n if tc >= 2 else None
            chks = self.sem(f"chk{par}")
            @block.sync
            def _(sync):
                if vec_need is not None:
                    sync.wait_ge(vec.h, vec_need)
                chks.inc(sync.dma_start(out=xc, in_=d["xin"][tc * 128:(tc + 1) * 128, :]), 16)
            ldn = chks.n

            xn_tap_wait = getattr(self, "_xn_tap_stc", None)
            @block.vector
            def _(vector):
                vector.wait_ge(chks.h, ldn)
                vector.wait_ge(self.sem("const").h, ld_tabs)
                if xn_tap_wait is not None:
                    vector.wait_ge(self.stc.h, xn_tap_wait)
                vector.tensor_reduce(COLS[:, 0:1], xc, AX.X, ALU.add)
                vector.drain()
                vector.tensor_scalar_mul(COLS[:, 1:2], COLS[:, 0:1], float(1.0 / D))
                vector.drain()
                vec.inc(vector.tensor_scalar_sub(DB[:], xc, COLS[:, 1:2]))
            vecn1 = vec.n
            @block.scalar
            def _(scalar):
                scalar.wait_ge(vec.h, vecn1)
                scl.inc(scalar.activation(JUNK[:], DB[:], AF.Square, accum_out=COLS[:, 2:3]))
            scln1 = scl.n
            @block.vector
            def _(vector):
                vector.wait_ge(scl.h, scln1)
                vec.inc(vector.tensor_scalar(COLS[:, 3:4], COLS[:, 2:3], float(1.0 / D), float(EPS), ALU.mult, ALU.add))
            vecn2 = vec.n
            @block.scalar
            def _(scalar):
                scalar.wait_ge(vec.h, vecn2)
                scl.inc(scalar.sqrt(COLS[:, 4:5], COLS[:, 3:4]))
            scln2 = scl.n
            @block.vector
            def _(vector):
                vector.wait_ge(scl.h, scln2)
                vector.reciprocal(COLS[:, 5:6], COLS[:, 4:5])
                vector.drain()
                vector.scalar_tensor_tensor(XN[:], DB[:], COLS[:, 5:6], LNWB[:], ALU.mult, ALU.mult)
                vector.drain()
                vec.inc(vector.tensor_tensor(XN[:], XN[:], LNBB[:], ALU.add))
            vecn3 = vec.n
            ppar = tc % 2
            pslice = PT[:, ppar * D:(ppar + 1) * D]
            pe_prev = pe.n
            @block.tensor
            def _(tensor):
                tensor.wait_ge(vec.h, vecn3)
                ins = None
                for dc in range(NDC):
                    ins = tensor.transpose(pslice[:, dc * 128:(dc + 1) * 128],
                                           XN[:, dc * 128:(dc + 1) * 128], self.IDN[:])
                pe.inc(ins)
            pen = pe.n
            @block.vector
            def _(vector):
                vector.wait_ge(pe.h, pen)
                dst = self.BIGA[:].rearrange("p (c t) -> p c t", c=NDC)[:, :, tc * 128:(tc + 1) * 128]
                src = pslice.rearrange("p (c t) -> p c t", c=NDC)
                vec.inc(vector.tensor_copy(dst, src))
            if "xn" in self.taps:
                vecn4 = vec.n
                @block.gpsimd
                def _(gpsimd):
                    gpsimd.wait_ge(vec.h, vecn4)
                    self.stc.inc(gpsimd.dma_start(out=self.taps["xn"][tc * 128:(tc + 1) * 128, :], in_=XN[:]), 16)
                self._xn_tap_stc = self.stc.n
                self.w("gpsimd")
        self.w("vector"); self.w("scalar")
        if "xnT" in self.taps:
            vecn = vec.n
            @block.gpsimd
            def _(gpsimd):
                gpsimd.wait_ge(vec.h, vecn)
                self.stc.inc(gpsimd.dma_start(out=self.taps["xnT"], in_=self.BIGA[:]), 16)
            self.w("gpsimd")
        self.barrier()
        ctx.close()

    def p2_v(self):
        nc, block, d = self.nc, self.block, self.dram
        ld, vec, pe, stc = self.ld, self.vec, self.pe, self.stc
        ctx = ExitStack()
        sb = lambda *a: ctx.enter_context(nc.sbuf_tensor(*a))
        WVS = sb("WVS", [128, 2 * D], F32)
        BVC = sb("BVC", [128, NDC], F32)
        VF = sb("VF", [128, T], F32)
        VTOK = sb("VTOK", [128, 256], F32)
        VP = ctx.enter_context(nc.psum_tensor("VP2", [128, 1024], F32))
        TP = ctx.enter_context(nc.psum_tensor("TP2", [128, 1024], F32))

        cs = self.sem("const")
        @block.sync
        def _(sync):
            cs.inc(sync.dma_start(out=BVC[:], in_=d["bvc"]), 16)
        ld_bvc = cs.n
        pe_after_proj = {}
        vec_after_evac = {}
        stc_after_store = {}
        tr_guard = 0
        for oc in range(NDC):
            par = oc % 2
            ws = WVS[:, par * D:(par + 1) * D]
            pe_need = pe_after_proj.get(oc - 2)
            wvss = self.sem(f"wvs{par}")
            @block.sync
            def _(sync):
                if pe_need is not None:
                    sync.wait_ge(pe.h, pe_need)
                for dc in range(NDC):
                    wvss.inc(sync.dma_start(
                        out=ws[:, dc * 128:(dc + 1) * 128],
                        in_=d["wvT"][dc * 128:(dc + 1) * 128, oc * 128:(oc + 1) * 128]), 16)
            ldn = wvss.n
            for tq in range(4):
                vpar = tq % 2
                pslice = VP[:, vpar * 512:(vpar + 1) * 512]
                vec_need = vec_after_evac.get((oc, tq - 2)) or vec_after_evac.get((oc - 1, tq + 2))
                @block.tensor
                def _(tensor):
                    tensor.wait_ge(wvss.h, ldn)
                    if vec_need is not None:
                        tensor.wait_ge(vec.h, vec_need)
                    ins = None
                    for dc in range(NDC):
                        ins = tensor.matmul(pslice, ws[:, dc * 128:(dc + 1) * 128],
                                            self.BIGA[:, dc * T + tq * 512: dc * T + (tq + 1) * 512],
                                            start=(dc == 0), stop=(dc == NDC - 1))
                    pe.inc(ins)
                pen = pe.n
                tg = tr_guard
                @block.vector
                def _(vector):
                    vector.wait_ge(pe.h, max(pen, tg))
                    vector.wait_ge(self.sem("const").h, ld_bvc)
                    vec.inc(vector.tensor_scalar(VF[:, tq * 512:(tq + 1) * 512], pslice,
                                                 BVC[:, oc:oc + 1], None, ALU.add))
                vec_after_evac[(oc, tq)] = vec.n
            pe_after_proj[oc] = pe.n
            for tc in range(NTC):
                tpar = tc % 2
                tps = TP[:, tpar * 512:tpar * 512 + 128]
                vtk = VTOK[:, tpar * 128:(tpar + 1) * 128]
                vec_need = vec_after_evac[(oc, 3)]
                evac_need = vec_after_evac.get(("t", oc, tc - 2))
                @block.tensor
                def _(tensor):
                    tensor.wait_ge(vec.h, max(vec_need, evac_need or 0))
                    pe.inc(tensor.transpose(tps, VF[:, tc * 128:(tc + 1) * 128], self.IDN[:]))
                pen = pe.n
                st_need = stc_after_store.get((oc, tc - 2))
                @block.vector
                def _(vector):
                    vector.wait_ge(pe.h, pen)
                    if st_need is not None:
                        vector.wait_ge(self.ssem("st_v").h, st_need)
                    vec.inc(vector.tensor_copy(vtk, tps))
                vec_after_evac[("t", oc, tc)] = vec.n
                vecn = vec.n
                stv = self.ssem("st_v")
                @block.gpsimd
                def _(gpsimd):
                    gpsimd.wait_ge(vec.h, vecn)
                    stv.inc(gpsimd.dma_start(
                        out=d["v_h"][tc * 128:(tc + 1) * 128, oc * 128:(oc + 1) * 128], in_=vtk), 16)
                stc_after_store[(oc, tc)] = stv.n
            tr_guard = pe.n
        self.w("vector"); self.w("gpsimd")
        self.barrier()
        if "vtok" in self.taps:
            @block.gpsimd
            def _(gpsimd):
                self.stc.inc(gpsimd.dma_start(out=self.taps["vtok"], in_=d["v_h"]), 16)
            self.w("gpsimd")
            self.barrier()
        ctx.close()

    def p3_attn(self, stage=7, npairs=8, nheads=2, nkc=NTC):
        nc, block, d = self.nc, self.block, self.dram
        ld, vec, scl, pe, stc = self.ld, self.vec, self.scl, self.pe, self.stc
        ctx = ExitStack()
        sb = lambda *a: ctx.enter_context(nc.sbuf_tensor(*a))
        WQS = sb("WQS", [128, 2 * D], F32)
        WKS = sb("WKS", [128, 2 * D], F32)
        BQC = sb("BQC", [128, NDC], F32)
        BKC = sb("BKC", [128, NDC], F32)
        C2 = sb("C2", [128, T], F32)
        S2 = sb("S2", [128, T], F32)
        SC1 = sb("SC1", [128, T], F32)
        SC2 = sb("SC2", [128, T], F32)
        SC3 = sb("SC3", [128, T], F32)
        QRB = sb("QRB", [128, T], F32)
        P1 = sb("P1", [128, T], F32)
        STATR = sb("STATR", [128, 3 * T], F32)
        QSS = STATR[0:2, 0:T]
        KSS = STATR[0:2, T:2 * T]
        BSR = STATR[0:2, 2 * T:3 * T]
        MQt = sb("MQ", [128, 2], F32)
        MQ = MQt[0:2, :]
        BIASP = sb("BIASP", [128, 32], F32)
        ALPHP = sb("ALPHP", [128, 32], F32)
        VLD = sb("VLD", [128, 256], F32)
        VTT = sb("VTT", [128, 130], F32)
        RBt = sb("RBt", [128, 1024], F32)
        self.taps_ops_buf = sb("TOB", [128, 512], F32)
        RROW = RBt[64:65, 0:512]
        BCS = RBt[0:64, 512:1024]
        AVEt = sb("AVE", [128, 1024], F32)
        AVE = AVEt[0:64, :]
        SCP = ctx.enter_context(nc.psum_tensor("SCP3", [128, T], F32))
        OPS = [ctx.enter_context(nc.psum_tensor(f"OPS{i}", [65, 512], F32)) for i in range(4)]

        cs = self.sem("const")
        @block.sync
        def _(sync):
            cs.inc(sync.dma_start(out=BQC[:], in_=d["bqc"]), 16)
            cs.inc(sync.dma_start(out=BKC[:], in_=d["bkc"]), 16)
            cs.inc(sync.dma_start(out=C2[:], in_=d["c2t"]), 16)
            cs.inc(sync.dma_start(out=S2[:], in_=d["s2t"]), 16)
        ld_tabs = cs.n

        bcs_guard = 0
        vld_ring = Ring()
        ave_ring = Ring()
        exp_last = {0: 0, 1: 0}
        p1_pe = {0: 0, 1: 0}
        scp_vec_guard = 0
        scp_scl_guard = 0
        pe_pair_w = Ring()

        for p in range(npairs):
            wpar = p % 2
            wq = WQS[:, wpar * D:(wpar + 1) * D]
            wk = WKS[:, wpar * D:(wpar + 1) * D]
            pe_need = pe_pair_w.old(2)
            wqss = self.sem(f"wqs{wpar}")
            @block.sync
            def _(sync):
                if pe_need is not None:
                    sync.wait_ge(pe.h, pe_need)
                for dc in range(NDC):
                    wqss.inc(sync.dma_start(
                        out=wq[:, dc * 128:(dc + 1) * 128],
                        in_=d["wqT"][dc * 128:(dc + 1) * 128, p * 128:(p + 1) * 128]), 16)
                    wqss.inc(sync.dma_start(
                        out=wk[:, dc * 128:(dc + 1) * 128],
                        in_=d["wkT"][dc * 128:(dc + 1) * 128, p * 128:(p + 1) * 128]), 16)
            ldn_w = wqss.n

            vg, sg = scp_vec_guard, scp_scl_guard
            evac_n = {}
            for wi, (ws, dst, bcol) in enumerate(((wq, SC1, BQC), (wk, SC2, BKC))):
                for tq in range(4):
                    pslice = SCP[:, tq * 512:(tq + 1) * 512]
                    bank_guard = evac_n.get(tq)
                    @block.tensor
                    def _(tensor):
                        tensor.wait_ge(wqss.h, ldn_w)
                        tensor.wait_ge(self.sem("const").h, ld_tabs)
                        if wi == 0 and tq == 0:
                            if vg:
                                tensor.wait_ge(vec.h, vg)
                            if sg:
                                tensor.wait_ge(scl.h, sg)
                        if bank_guard is not None:
                            tensor.wait_ge(vec.h, bank_guard)
                        ins = None
                        for dc in range(NDC):
                            ins = tensor.matmul(pslice, ws[:, dc * 128:(dc + 1) * 128],
                                                self.BIGA[:, dc * T + tq * 512: dc * T + (tq + 1) * 512],
                                                start=(dc == 0), stop=(dc == NDC - 1))
                        pe.inc(ins)
                    pen = pe.n
                    @block.vector
                    def _(vector):
                        vector.wait_ge(pe.h, pen)
                        vec.inc(vector.tensor_scalar(dst[:, tq * 512:(tq + 1) * 512], pslice,
                                                     bcol[:, p:p + 1], None, ALU.add))
                    evac_n[tq] = vec.n
            vec_proj = vec.n

            if stage < 2:
                if p == 0 and "qrope" in self.taps:
                    vpn = vec.n
                    @block.gpsimd
                    def _(gpsimd):
                        gpsimd.wait_ge(vec.h, vpn)
                        stc.inc(gpsimd.dma_start(out=self.taps["qrope"], in_=SC1[:]), 16)
                        stc.inc(gpsimd.dma_start(out=self.taps["krope"], in_=SC2[:]), 16)
                scp_vec_guard = vec.n
                scp_scl_guard = scl.n
                pe_pair_w.push(pe.n)
                continue
            sq_pe = 0
            for (src, ss) in ((SC1, QSS), (SC2, KSS)):
                vecn = vec.n
                @block.scalar
                def _(scalar):
                    scalar.wait_ge(vec.h, vecn)
                    if sq_pe:
                        scalar.wait_ge(pe.h, sq_pe)
                    scl.inc(scalar.activation(SC3[:], src[:], AF.Square))
                scln = scl.n
                @block.tensor
                def _(tensor):
                    tensor.wait_ge(scl.h, scln)
                    tensor.wait_ge(vec.h, vec_proj)
                    ins = None
                    for tq in range(4):
                        ins = tensor.matmul(SCP[0:2, tq * 512:(tq + 1) * 512], self.ONES2[:],
                                            SC3[:, tq * 512:(tq + 1) * 512], start=True, stop=True)
                    pe.inc(ins)
                sq_pe = pe.n
                pen = pe.n
                @block.vector
                def _(vector):
                    vector.wait_ge(pe.h, pen)
                    vec.inc(vector.tensor_copy(ss, SCP[0:2, :]))

            @block.vector
            def _(vector):
                vector.tensor_reduce(MQ[:, 0:1], QSS, AX.X, ALU.max)
                vector.drain()
                vec.inc(vector.tensor_scalar(BSR, KSS, MQ[:, 0:1], -C16, ALU.add, ALU.mult))
            vecn_bsr = vec.n
            @block.tensor
            def _(tensor):
                tensor.wait_ge(vec.h, vecn_bsr)
                ins = None
                for kc in range(NTC):
                    ins = tensor.transpose(SCP[:, kc * 2:kc * 2 + 2],
                                           BSR[:, kc * 128:(kc + 1) * 128], self.IDN[0:2, 0:2])
                pe.inc(ins)
            pen_bt = pe.n
            @block.vector
            def _(vector):
                vector.wait_ge(pe.h, pen_bt)
                vec.inc(vector.tensor_copy(BIASP[:], SCP[:, 0:32]))
            vecn_bp = vec.n
            @block.scalar
            def _(scalar):
                scalar.wait_ge(vec.h, vecn_bp)
                scl.inc(scalar.activation(ALPHP[:], BIASP[:], AF.Exp, scale=-1.0))
            scl_alpha = scl.n

            if stage < 3:
                scp_vec_guard = vec.n
                scp_scl_guard = scl.n
                pe_pair_w.push(pe.n)
                continue
            @block.gpsimd
            def _(gpsimd):
                gpsimd.wait_ge(pe.h, sq_pe)
                for h2 in range(2):
                    b = h2 * 64
                    self.gld.inc(gpsimd.dma_start(out=SC3[b:b + 32, :], in_=SC1[b + 32:b + 64, :]), 16)
                    self.gld.inc(gpsimd.dma_start(out=SC3[b + 32:b + 64, :], in_=SC1[b:b + 32, :]), 16)
            gldn = self.gld.n
            @block.vector
            def _(vector):
                vector.wait_ge(self.gld.h, gldn)
                vector.tensor_tensor(P1[:], SC1[:], C2[:], ALU.mult)
                vector.tensor_tensor(QRB[:], SC3[:], S2[:], ALU.mult)
                vector.drain()
                vec.inc(vector.tensor_tensor(SC1[:], P1[:], QRB[:], ALU.add))
            vec_qrope = vec.n
            @block.gpsimd
            def _(gpsimd):
                gpsimd.wait_ge(vec.h, vec_qrope)
                for h2 in range(2):
                    b = h2 * 64
                    self.gld.inc(gpsimd.dma_start(out=SC3[b:b + 32, :], in_=SC2[b + 32:b + 64, :]), 16)
                    self.gld.inc(gpsimd.dma_start(out=SC3[b + 32:b + 64, :], in_=SC2[b:b + 32, :]), 16)
            gldn = self.gld.n
            @block.vector
            def _(vector):
                vector.wait_ge(self.gld.h, gldn)
                vector.tensor_tensor(P1[:], SC2[:], C2[:], ALU.mult)
                vector.tensor_tensor(QRB[:], SC3[:], S2[:], ALU.mult)
                vector.drain()
                vec.inc(vector.tensor_tensor(SC2[:], P1[:], QRB[:], ALU.add))
            vec_krope = vec.n
            if p == 0 and "qrope" in self.taps:
                @block.gpsimd
                def _(gpsimd):
                    gpsimd.wait_ge(vec.h, vec_qrope)
                    stc.inc(gpsimd.dma_start(out=self.taps["qrope"], in_=SC1[:]), 16)
                    gpsimd.wait_ge(vec.h, vec_krope)
                    stc.inc(gpsimd.dma_start(out=self.taps["krope"], in_=SC2[:]), 16)

            if stage < 4:
                scp_vec_guard = vec.n
                scp_scl_guard = scl.n
                pe_pair_w.push(pe.n)
                continue
            for h2 in range(nheads):
                hb = h2 * 64
                for kc in range(nkc):
                    if getattr(self, "_p3_interleave_dequant", False) and h2 == 0 and 2 <= kc <= 10:
                        self._p6_emit(1)
                    vpar = kc % 2
                    vld_t = VLD[:, vpar * 128:(vpar + 1) * 128]
                    vtt = VTT[:, vpar * 65:(vpar + 1) * 65]
                    slot_free = vld_ring.old(2)
                    vlds = self.sem(f"vld{vpar}")
                    @block.sync
                    def _(sync):
                        if slot_free is not None:
                            sync.wait_ge(pe.h, slot_free)
                        vlds.inc(sync.dma_start(out=vld_t,
                                                in_=d["v_h"][kc * 128:(kc + 1) * 128, p * 128:(p + 1) * 128]), 16)
                    ld_v = vlds.n
                    @block.vector
                    def _(vector):
                        vector.wait_ge(vlds.h, ld_v)
                        vector.wait_ge(scl.h, scl_alpha)
                        if slot_free is not None:
                            vector.wait_ge(pe.h, slot_free)
                        vector.tensor_scalar(vtt[:, 0:64], vld_t[:, hb:hb + 64],
                                             ALPHP[:, kc * 2 + h2:kc * 2 + h2 + 1], None, ALU.mult)
                        vec.inc(vector.tensor_copy(vtt[:, 64:65], ALPHP[:, kc * 2 + h2:kc * 2 + h2 + 1]))
                    vec_vtt = vec.n
                    scl_exp_q = [0, 0]
                    for qh in range(2):
                        sc_ps = SCP[:, qh * 1024:(qh + 1) * 1024]
                        p1s = P1[:, qh * 1024:(qh + 1) * 1024]
                        el = exp_last[qh]
                        bg = bcs_guard
                        @block.tensor
                        def _(tensor):
                            tensor.wait_ge(vec.h, max(vec_krope, bg))
                            if el:
                                tensor.wait_ge(scl.h, el)
                            ins = None
                            for j in range(2):
                                ins = tensor.matmul(
                                    sc_ps[:, j * 512:(j + 1) * 512],
                                    SC2[hb:hb + 64, kc * 128:(kc + 1) * 128],
                                    SC1[hb:hb + 64, (qh * 2 + j) * 512:(qh * 2 + j + 1) * 512],
                                    start=True, stop=True)
                            pe.inc(ins)
                        pe_sc = pe.n
                        pf = p1_pe[qh]
                        @block.scalar
                        def _(scalar):
                            scalar.wait_ge(pe.h, pe_sc)
                            if pf:
                                scalar.wait_ge(pe.h, pf)
                            scl.inc(scalar.activation(p1s, sc_ps, AF.Exp,
                                                      bias=BIASP[:, kc * 2 + h2:kc * 2 + h2 + 1],
                                                      scale=0.125))
                        exp_last[qh] = scl.n
                        scl_exp_q[qh] = scl.n
                    for qh in range(2):
                        p1s = P1[:, qh * 1024:(qh + 1) * 1024]
                        scl_exp = scl_exp_q[qh]
                        if p == 0 and h2 == 0 and kc == 0 and qh == 0 and "att0" in self.taps:
                            @block.gpsimd
                            def _(gpsimd):
                                gpsimd.wait_ge(scl.h, scl_exp)
                                stc.inc(gpsimd.dma_start(out=self.taps["att0"][:, 2 * T:2 * T + 32], in_=BIASP[:]), 16)
                                stc.inc(gpsimd.dma_start(out=self.taps["att0"][:, 2 * T + 32:2 * T + 64], in_=ALPHP[:]), 16)
                                stc.inc(gpsimd.dma_start(out=self.taps["att0"][:, 3 * T:3 * T + 1024], in_=P1[:, 0:1024]), 16)
                        @block.tensor
                        def _(tensor):
                            tensor.wait_ge(scl.h, scl_exp)
                            tensor.wait_ge(vec.h, vec_vtt)
                            ins = None
                            for j in range(2):
                                qc = qh * 2 + j
                                ins = tensor.matmul(OPS[qc][:, :], vtt,
                                                    p1s[:, j * 512:(j + 1) * 512],
                                                    start=(kc == 0), stop=(kc == nkc - 1))
                            pe.inc(ins)
                        p1_pe[qh] = pe.n
                    vld_ring.push(pe.n)
                pe_pv_done = pe.n
                if stage < 4.5:
                    continue
                for qc in range(4):
                    apar = qc % 2
                    ave = AVE[:, apar * 512:(apar + 1) * 512]
                    st_need = ave_ring.old(2)
                    sta_h = self.ssem("st_ave").h
                    @block.vector
                    def _(vector):
                        vector.wait_ge(pe.h, pe_pv_done)
                        vector.reciprocal(RROW, OPS[qc][64:65, :])
                        vector.drain()
                        vec.inc(vector.tensor_copy(self.MCOL[:, 1:2], self.MCOL[:, 0:1]))
                    vec_rr = vec.n
                    if stage >= 6:
                        @block.tensor
                        def _(tensor):
                            tensor.wait_ge(vec.h, vec_rr)
                            pe.inc(tensor.matmul(SCP[0:64, 0:512], self.ONESR64[:, 0:64], RROW,
                                                 start=True, stop=True))
                        pe_bc = pe.n
                        @block.scalar
                        def _(scalar):
                            scalar.wait_ge(pe.h, pe_bc)
                            scl.inc(scalar.copy(self.MCOL[0:1, 1:2], self.MCOL[0:1, 0:1]))
                        scl_bc = scl.n
                    if stage >= 7:
                        @block.vector
                        def _(vector):
                            vector.wait_ge(scl.h, scl_bc)
                            vector.wait_ge(pe.h, pe_bc)
                            if st_need is not None:
                                vector.wait_ge(sta_h, st_need)
                            vector.tensor_copy(BCS, SCP[0:64, 0:512])
                            vec.inc(vector.tensor_tensor(ave, OPS[qc][0:64, :], BCS, ALU.mult))
                        vec_ave = vec.n
                    else:
                        @block.vector
                        def _(vector):
                            if st_need is not None:
                                vector.wait_ge(sta_h, st_need)
                            if stage >= 6:
                                vector.wait_ge(pe.h, pe_bc)
                            vec.inc(vector.tensor_copy(ave, OPS[qc][0:64, :]))
                        vec_ave = vec.n
                    if p == 0 and h2 == 0 and qc == 0 and "att0" in self.taps:
                        @block.vector
                        def _(vector):
                            vec.inc(vector.tensor_copy(self.taps_ops_buf[0:65, :], OPS[qc][:, :]))
                        vo = vec.n
                        @block.gpsimd
                        def _(gpsimd):
                            gpsimd.wait_ge(vec.h, vo)
                            stc.inc(gpsimd.dma_start(out=self.taps["att0"][0:65, 0:512], in_=self.taps_ops_buf[0:65, :]), 16)
                            stc.inc(gpsimd.dma_start(out=self.taps["att0"][64:65, 512:1024], in_=RROW), 16)
                            stc.inc(gpsimd.dma_start(out=self.taps["att0"][0:64, 1024:1536], in_=BCS), 16)
                            stc.inc(gpsimd.dma_start(out=self.taps["att0"][0:64, 1536:2048], in_=ave), 16)
                        tapn = stc.n
                        @block.vector
                        def _(vector):
                            vector.wait_ge(stc.h, tapn)
                        vec_ave = vec.n
                    r0 = p * 128 + h2 * 64
                    sta = self.ssem("st_ave")
                    @block.gpsimd
                    def _(gpsimd):
                        gpsimd.wait_ge(vec.h, vec_ave)
                        sta.inc(gpsimd.dma_start(
                            out=d["avT_h"][r0:r0 + 64, qc * 512:(qc + 1) * 512], in_=ave), 16)
                    ave_ring.push(sta.n)
                bcs_guard = vec.n
            scp_vec_guard = vec.n
            scp_scl_guard = scl.n
            pe_pair_w.push(pe.n)
        self.w("vector"); self.w("scalar"); self.w("gpsimd")
        self.barrier()
        if "avT" in self.taps:
            @block.gpsimd
            def _(gpsimd):
                self.stc.inc(gpsimd.dma_start(out=self.taps["avT"], in_=d["avT_h"]), 16)
            self.w("gpsimd")
            self.barrier()
        ctx.close()

    def _qdq3(self, vector, dst, src, scol, rscol):
        vector.tensor_scalar(dst, src, rscol, MAGIC, ALU.mult, ALU.add)
        vector.tensor_scalar(dst, dst, MAGIC, 32767.0, ALU.subtract, ALU.min)
        vector.tensor_scalar(dst, dst, -32768.0, scol, ALU.max, ALU.mult)

    def _absmax_into(self, vector, src, gcol, tmpcol):
        vector.tensor_reduce(tmpcol, src, AX.X, ALU.max, apply_absolute_value=True)
        vector.drain()
        vector.tensor_tensor(gcol, gcol, tmpcol, ALU.max)
        vector.drain()

    def _allreduce_scales(self, ar_idx, c0, k, sclb_c0):
        nc, block, d = self.nc, self.block, self.dram
        ld, vec, pe, gp, cc, stc = self.ld, self.vec, self.pe, self.gp, self.cc, self.stc
        MQ2 = self.MCOL
        arctx = ExitStack()
        self.PSC = arctx.enter_context(nc.psum_tensor(f"PSC_{ar_idx}", [128, 256], F32))
        vecn0 = vec.n
        gpn0 = self.gp.n
        @block.tensor
        def _(tensor):
            tensor.wait_ge(vec.h, vecn0)
            if gpn0:
                tensor.wait_ge(self.gp.h, gpn0)
            pe.inc(tensor.transpose(self.PSC[0:k, 0:128], self.GMX[:, c0:c0 + k], self.IDN[:]))
        pen = pe.n
        @block.vector
        def _(vector):
            vector.wait_ge(pe.h, pen)
            vector.tensor_copy(self.TROW[0:k, :], self.PSC[0:k, 0:128])
            vector.drain()
            vector.tensor_reduce(MQ2[0:k, 0:1], self.TROW[0:k, :], AX.X, ALU.max)
            vector.drain()
            vec.inc(vector.tensor_copy(MQ2[0:k, 1:2], MQ2[0:k, 0:1]))
        vecn = vec.n
        ari, aro = d[f"ari{ar_idx}"], d[f"aro{ar_idx}"]
        @block.gpsimd
        def _(gpsimd):
            gpsimd.wait_ge(vec.h, vecn)
            self.gld.inc(gpsimd.dma_start(out=ari[0:1, 0:k], in_=MQ2[0:k, 0:1]), 16)
            if k == 1:
                self.gld.inc(gpsimd.dma_start(out=ari[0:1, 1:2], in_=MQ2[0:1, 0:1]), 16)
            gpsimd.wait_ge(self.gld.h, self.gld.n)
            gpsimd.collective_compute(
                "AllReduce", ALU.max, replica_groups=[list(range(B))],
                ins=[ari], outs=[aro],
            ).then_inc(cc.h, 1)
            cc.n += 1
            gpsimd.wait_ge(cc.h, cc.n)
            self.gld.inc(gpsimd.dma_start(out=self.SCL[0:1, 0:k], in_=aro[0:1, 0:k]), 16)
        gldn = self.gld.n
        @block.vector
        def _(vector):
            vector.wait_ge(self.gld.h, gldn)
            vector.tensor_scalar(self.SCL[0:1, 4:4 + k], self.SCL[0:1, 0:k], QMAXR, 1e-12, ALU.mult, ALU.max)
            vector.drain()
            vector.reciprocal(self.SCL[0:1, 4 + k:4 + 2 * k], self.SCL[0:1, 4:4 + k])
            vector.drain()
            vec.inc(vector.tensor_copy(self.SCL[0:1, 12:13], self.SCL[0:1, 0:1]))
        vecn2 = vec.n
        @block.tensor
        def _(tensor):
            tensor.wait_ge(vec.h, vecn2)
            pe.inc(tensor.matmul(self.PSC[0:128, 0:2 * k], self.ONESR[:, 0:128],
                                 self.SCL[0:1, 4:4 + 2 * k], start=True, stop=True))
        pen2 = pe.n
        @block.vector
        def _(vector):
            vector.wait_ge(pe.h, pen2)
            vec.inc(vector.tensor_copy(self.SCLB[:, sclb_c0:sclb_c0 + 2 * k], self.PSC[0:128, 0:2 * k]))
        self.w("vector"); self.w("gpsimd")
        self.barrier()
        arctx.close()

    def p4_wo(self):
        nc, block, d = self.nc, self.block, self.dram
        ld, vec, scl, pe, stc = self.ld, self.vec, self.scl, self.pe, self.stc
        ctx = ExitStack()
        sb = lambda *a: ctx.enter_context(nc.sbuf_tensor(*a))
        WBIG = sb("WBIG", [128, 4096], F32)
        X2H0 = sb("X2H0", [128, 8192], F32)
        AVB = sb("AVB", [128, 2048], F32)
        BOB = sb("BOB", [128, D], F32)
        YC = sb("YC", [128, D], F32)
        DB = sb("DB4", [128, D], F32)
        JUNK = sb("JUNK4", [128, D], F32)
        COLS = sb("COLS4", [128, 8], F32)
        X2C = sb("X2C", [128, D], F32)
        PW = ctx.enter_context(nc.psum_tensor("PW4", [128, 1024], F32))

        cs = self.sem("const")
        @block.sync
        def _(sync):
            cs.inc(sync.dma_start(out=BOB[:], in_=d["bob"]), 16)
        ld_bob = cs.n
        av_ring = Ring()
        pw_ring = Ring()
        for dh in range(2):
            pe_w = getattr(self, "_p4_pe_w", None)
            wbigs = self.sem("wbig")
            @block.sync
            def _(sync):
                if pe_w is not None:
                    sync.wait_ge(pe.h, pe_w)
                for dc in range(NDC):
                    wbigs.inc(sync.dma_start(out=WBIG[:, dc * 512:(dc + 1) * 512],
                                             in_=d["woT"][dc * 128:(dc + 1) * 128, dh * 512:(dh + 1) * 512]), 16)
            ldn_w = wbigs.n
            for tc in range(NTC):
                if getattr(self, "_p4_interleave_tr", False):
                    self._p6t_emit(2)
                apar = tc % 2
                avb = AVB[:, apar * 1024:(apar + 1) * 1024]
                slot_free = av_ring.old(2)
                avbs = self.sem(f"avb{apar}")
                @block.sync
                def _(sync):
                    if slot_free is not None:
                        sync.wait_ge(pe.h, slot_free)
                    for dc in range(NDC):
                        avbs.inc(sync.dma_start(out=avb[:, dc * 128:(dc + 1) * 128],
                                                in_=d["avT_h"][dc * 128:(dc + 1) * 128, tc * 128:(tc + 1) * 128]), 16)
                ldn_av = avbs.n
                ppar = tc % 2
                pw = PW[:, ppar * 512:(ppar + 1) * 512]
                vfree = pw_ring.old(2)
                @block.tensor
                def _(tensor):
                    tensor.wait_ge(wbigs.h, ldn_w)
                    tensor.wait_ge(avbs.h, ldn_av)
                    if vfree is not None:
                        tensor.wait_ge(vec.h, vfree)
                    ins = None
                    for dc in range(NDC):
                        ins = tensor.matmul(pw, avb[:, dc * 128:(dc + 1) * 128],
                                            WBIG[:, dc * 512:(dc + 1) * 512],
                                            start=(dc == 0), stop=(dc == NDC - 1))
                    pe.inc(ins)
                av_ring.push(pe.n)
                pen = pe.n
                if dh == 0:
                    p4xs = self.sem("p4x")
                    @block.sync
                    def _(sync):
                        sync.wait_ge(vec.h, getattr(self, "_p4_xc", 0))
                        p4xs.inc(sync.dma_start(out=self.CHK[:, 0:512],
                                                in_=d["xin"][tc * 128:(tc + 1) * 128, 0:512]), 16)
                    ldn_x = p4xs.n
                    @block.vector
                    def _(vector):
                        vector.wait_ge(pe.h, pen)
                        vector.wait_ge(p4xs.h, ldn_x)
                        vector.wait_ge(self.sem("const").h, ld_bob)
                        dst = X2H0[:, tc * 512:(tc + 1) * 512]
                        vector.tensor_tensor(dst, pw, self.CHK[:, 0:512], ALU.add)
                        vector.drain()
                        vector.tensor_tensor(dst, dst, BOB[:, 0:512], ALU.add)
                        vector.drain()
                        vec.inc(vector.tensor_tensor(dst, dst, self.CHK[:, 0:512], ALU.add))
                    self._p4_xc = vec.n
                    pw_ring.push(vec.n)
                else:
                    p4xs = self.sem("p4x")
                    @block.sync
                    def _(sync):
                        sync.wait_ge(vec.h, getattr(self, "_p4_xc", 0))
                        p4xs.inc(sync.dma_start(out=self.CHK[:, 0:512],
                                                in_=d["xin"][tc * 128:(tc + 1) * 128, 512:1024]), 16)
                    ldn_x = p4xs.n
                    st_prev = getattr(self, "_p4_st", None)
                    @block.vector
                    def _(vector):
                        vector.wait_ge(pe.h, pen)
                        vector.wait_ge(p4xs.h, ldn_x)
                        dsty = YC[:, 512:1024]
                        vector.tensor_tensor(dsty, pw, self.CHK[:, 0:512], ALU.add)
                        vector.drain()
                        vector.tensor_tensor(dsty, dsty, BOB[:, 512:1024], ALU.add)
                        vector.drain()
                        vector.tensor_tensor(dsty, dsty, self.CHK[:, 0:512], ALU.add)
                        vector.tensor_copy(YC[:, 0:512], X2H0[:, tc * 512:(tc + 1) * 512])
                        vector.drain()
                        vector.tensor_reduce(COLS[:, 0:1], YC[:], AX.X, ALU.add)
                        vector.drain()
                        vector.tensor_scalar_mul(COLS[:, 1:2], COLS[:, 0:1], float(1.0 / D))
                        vector.drain()
                        vec.inc(vector.tensor_scalar_sub(DB[:], YC[:], COLS[:, 1:2]))
                    self._p4_xc = vec.n
                    vecn1 = vec.n
                    @block.scalar
                    def _(scalar):
                        scalar.wait_ge(vec.h, vecn1)
                        scl.inc(scalar.activation(JUNK[:], DB[:], AF.Square, accum_out=COLS[:, 2:3]))
                    scln1 = scl.n
                    @block.vector
                    def _(vector):
                        vector.wait_ge(scl.h, scln1)
                        vec.inc(vector.tensor_scalar(COLS[:, 3:4], COLS[:, 2:3], float(1.0 / D), float(EPS), ALU.mult, ALU.add))
                    vecn2 = vec.n
                    @block.scalar
                    def _(scalar):
                        scalar.wait_ge(vec.h, vecn2)
                        scl.inc(scalar.sqrt(COLS[:, 4:5], COLS[:, 3:4]))
                    scln2 = scl.n
                    @block.vector
                    def _(vector):
                        vector.wait_ge(scl.h, scln2)
                        vector.reciprocal(COLS[:, 5:6], COLS[:, 4:5])
                        vector.drain()
                        if st_prev is not None:
                            vector.wait_ge(self.ssem("st_x2").h, st_prev)
                        vector.tensor_scalar(X2C[:], DB[:], COLS[:, 5:6], None, ALU.mult)
                        vector.drain()
                        self._absmax_into(vector, X2C[:], self.GMX[:, 0:1], COLS[:, 6:7])
                        vec.inc(vector.tensor_copy(COLS[:, 7:8], COLS[:, 6:7]))
                    vecn3 = vec.n
                    stx = self.ssem("st_x2")
                    @block.gpsimd
                    def _(gpsimd):
                        gpsimd.wait_ge(vec.h, vecn3)
                        stx.inc(gpsimd.dma_start(out=d["x2_h"][tc * 128:(tc + 1) * 128, :], in_=X2C[:]), 16)
                    self._p4_st = stx.n
                    pw_ring.push(self._p4_xc)
            self._p4_pe_w = pe.n
        self.w("vector"); self.w("scalar"); self.w("gpsimd")
        self.barrier()
        if "x2" in self.taps:
            @block.gpsimd
            def _(gpsimd):
                self.stc.inc(gpsimd.dma_start(out=self.taps["x2"], in_=d["x2_h"]), 16)
            self.w("gpsimd")
            self.barrier()
        self._allreduce_scales(0, 0, 1, 0)
        TRQ = ctx.enter_context(nc.psum_tensor("TRQ4", [128, 2048], F32))
        for tc in range(NTC):
            par = tc % 2
            xc = self.CHK[:, par * D:(par + 1) * D]
            vprev = getattr(self, "_p4b_vec", None)
            chks = self.sem(f"chk{par}")
            @block.sync
            def _(sync):
                if vprev is not None:
                    sync.wait_ge(vec.h, vprev)
                chks.inc(sync.dma_start(out=xc, in_=d["x2_h"][tc * 128:(tc + 1) * 128, :]), 16)
            ldn = chks.n
            @block.vector
            def _(vector):
                vector.wait_ge(chks.h, ldn)
                self._qdq3(vector, X2C[:], xc, self.SCLB[:, 0:1], self.SCLB[:, 1:2])
                vector.drain()
                vec.inc(vector.tensor_copy(COLS[:, 7:8], COLS[:, 6:7]))
            self._p4b_vec = vec.n
            vecn = vec.n
            pslice = TRQ[:, (tc % 2) * 1024:((tc % 2) + 1) * 1024]
            evac_prev = getattr(self, "_p4b_evac", {}).get(tc - 2)
            @block.tensor
            def _(tensor):
                tensor.wait_ge(vec.h, vecn)
                ins = None
                for dc in range(NDC):
                    ins = tensor.transpose(pslice[:, dc * 128:(dc + 1) * 128],
                                           X2C[:, dc * 128:(dc + 1) * 128], self.IDN[:])
                pe.inc(ins)
            pen = pe.n
            @block.vector
            def _(vector):
                vector.wait_ge(pe.h, pen)
                dst = self.BIGA[:].rearrange("p (c t) -> p c t", c=NDC)[:, :, tc * 128:(tc + 1) * 128]
                src = pslice.rearrange("p (c t) -> p c t", c=NDC)
                vec.inc(vector.tensor_copy(dst, src))
            if not hasattr(self, "_p4b_evac"):
                self._p4b_evac = {}
            self._p4b_evac[tc] = vec.n
        self.w("vector")
        self.barrier()
        if "x2qT" in self.taps:
            @block.gpsimd
            def _(gpsimd):
                self.stc.inc(gpsimd.dma_start(out=self.taps["x2qT"], in_=self.BIGA[:]), 16)
            self.w("gpsimd")
            self.barrier()
        ctx.close()

    def _p6_init(self):
        nc = self.nc
        ctx = self._p6_ctx = ExitStack()
        sb = lambda *a: ctx.enter_context(nc.sbuf_tensor(*a))
        self._P6B = dict(
            WCH=sb("WCH", [128, 2 * 1024], F32),
            SBK=sb("SBK", [128, 32], F32),
            RBK=sb("RBK", [128, 32], F32),
            WQV=sb("WQV", [128, 2 * 1024], F32),
        )
        jobs = [("wup", "wupT_h", II, D), ("wgate", "wgateT_h", II, D), ("wdown", "wdT_h", D, II)]
        self._p6_jobs = [(wname, dsth, rc, co) for (wname, dsth, RO, CO) in jobs
                         for rc in range(RO // 128) for co in range(0, CO, 1024)]
        self._p6_pos = 0
        self._p6_rings = (Ring(), Ring())

    def _p6_emit(self, n):
        if not hasattr(self, "_p6_jobs"):
            self._p6_init()
        nc, block, d = self.nc, self.block, self.dram
        ld, vec, stc = self.ld, self.vec, self.stc
        b = self._P6B
        WCH, SBK, RBK, WQV = b["WCH"], b["SBK"], b["RBK"], b["WQV"]
        wch_ring, st_ring = self._p6_rings
        todo = self._p6_jobs[self._p6_pos:self._p6_pos + n]
        for (wname, dsth, rc, co) in todo:
            u = self._p6_pos
            self._p6_pos += 1
            nb, CO = 32, 1024
            par = len(wch_ring.hist) % 2
            wch = WCH[:, par * 1024: par * 1024 + CO]
            wqv = WQV[:, par * 1024: par * 1024 + CO]
            vprev = wch_ring.old(2)
            wchs = self.sem(f"wch{par}")
            @block.sync
            def _(sync):
                if vprev is not None:
                    sync.wait_ge(vec.h, vprev)
                wchs.inc(sync.dma_start(out=wch, in_=d[wname][rc * 128:(rc + 1) * 128, co:co + CO]), 16)
            ldn = wchs.n
            st_prev = st_ring.old(2)
            @block.vector
            def _(vector):
                vector.wait_ge(wchs.h, ldn)
                if st_prev is not None:
                    vector.wait_ge(self.ssem("st_wq").h, st_prev)
                vector.tensor_reduce(SBK[:, 0:nb].rearrange("p (b o) -> p b o", o=1),
                                     wch.rearrange("p (b c) -> p b c", c=32),
                                     AX.X, ALU.max, apply_absolute_value=True)
                vector.drain()
                vector.tensor_scalar(SBK[:, 0:nb], SBK[:, 0:nb], R7, 1e-12, ALU.mult, ALU.max)
                vector.drain()
                vector.reciprocal(RBK[:, 0:nb], SBK[:, 0:nb])
                vector.drain()
                rb = RBK[:, 0:nb].rearrange("p (b o) -> p b o", o=1).broadcast_to([128, nb, 32])
                sb_ = SBK[:, 0:nb].rearrange("p (b o) -> p b o", o=1).broadcast_to([128, nb, 32])
                w3 = wqv.rearrange("p (b c) -> p b c", c=32)
                vector.tensor_tensor(w3, wch.rearrange("p (b c) -> p b c", c=32), rb, ALU.mult)
                vector.drain()
                vector.tensor_scalar(wqv, wqv, MAGIC, MAGIC, ALU.add, ALU.subtract)
                vector.tensor_scalar(wqv, wqv, 7.0, -8.0, ALU.min, ALU.max)
                vector.drain()
                vec.inc(vector.tensor_tensor(w3, w3, sb_, ALU.mult))
            wch_ring.push(vec.n)
            vecn = vec.n
            stw = self.ssem("st_wq")
            @block.gpsimd
            def _(gpsimd):
                gpsimd.wait_ge(vec.h, vecn)
                stw.inc(gpsimd.dma_start(out=d["wqoi_h"][u * 128:(u + 1) * 128, :], in_=wqv), 16)
            st_ring.push(stw.n)

    def _p6t_init(self):
        nc = self.nc
        ctx = self._p6t_ctx = ExitStack()
        sb = lambda *a: ctx.enter_context(nc.sbuf_tensor(*a))
        self._P6T = dict(
            WTL=sb("WTL", [128, 2 * 1024], F32),
            WEV=sb("WEV", [128, 1024], F32),
            TPD=ctx.enter_context(nc.psum_tensor("TPD6", [128, 1024], F32)),
        )
        self._p6t_pos = 0
        self._p6t_rings = (Ring(), Ring(), Ring())

    def _p6t_emit(self, n):
        if not hasattr(self, "_p6t_pos"):
            self._p6t_init()
        nc, block, d = self.nc, self.block, self.dram
        ld, vec, pe, stc = self.ld, self.vec, self.pe, self.stc
        WTL, WEV, TPD = self._P6T["WTL"], self._P6T["WEV"], self._P6T["TPD"]
        wtl_ring, ev_ring, st_ring = self._p6t_rings
        todo = self._p6_jobs[self._p6t_pos:self._p6t_pos + n]
        for (wname, dsth, rc, co) in todo:
            u = self._p6t_pos
            self._p6t_pos += 1
            par = len(wtl_ring.hist) % 2
            wtl = WTL[:, par * 1024:(par + 1) * 1024]
            pe_prev = wtl_ring.old(2)
            wtls = self.sem(f"wtl{par}")
            @block.sync
            def _(sync):
                if pe_prev is not None:
                    sync.wait_ge(pe.h, pe_prev)
                wtls.inc(sync.dma_start(out=wtl, in_=d["wqoi_h"][u * 128:(u + 1) * 128, :]), 16)
            ldn = wtls.n
            for g in range(2):
                gpar = len(ev_ring.hist) % 2
                tslice = self._P6T["TPD"][:, gpar * 512:(gpar + 1) * 512]
                eslice = WEV[:, gpar * 512:(gpar + 1) * 512]
                ev_free = ev_ring.old(2)
                @block.tensor
                def _(tensor):
                    tensor.wait_ge(wtls.h, ldn)
                    if ev_free is not None:
                        tensor.wait_ge(vec.h, ev_free)
                    ins = None
                    for j in range(4):
                        tj = g * 4 + j
                        ins = tensor.transpose(tslice[:, j * 128:(j + 1) * 128],
                                               wtl[:, tj * 128:(tj + 1) * 128], self.IDN[:])
                    pe.inc(ins)
                pen = pe.n
                st_free = st_ring.old(2)
                stt = self.ssem("st_wt")
                @block.vector
                def _(vector):
                    vector.wait_ge(pe.h, pen)
                    if st_free is not None:
                        vector.wait_ge(stt.h, st_free)
                    vec.inc(vector.tensor_copy(eslice, tslice))
                ev_ring.push(vec.n)
                vecn_e = vec.n
                @block.gpsimd
                def _(gpsimd):
                    gpsimd.wait_ge(vec.h, vecn_e)
                    ins = None
                    for j in range(4):
                        tj = g * 4 + j
                        ins = gpsimd.dma_start(out=d[dsth][co + tj * 128:co + (tj + 1) * 128,
                                                           rc * 128:(rc + 1) * 128],
                                               in_=eslice[:, j * 128:(j + 1) * 128])
                        stt.inc(ins, 16)
                st_ring.push(stt.n)
            wtl_ring.push(pe.n)

    def p6_dequant(self):
        if not hasattr(self, "_p6_jobs"):
            self._p6_init()
        self._p6_emit(len(self._p6_jobs) - self._p6_pos)
        if not hasattr(self, "_p6t_pos"):
            self._p6t_init()
        self._p6t_emit(len(self._p6_jobs) - self._p6t_pos)
        block, d = self.block, self.dram
        self.w("vector"); self.w("gpsimd")
        self.barrier()
        for tap, src in (("wupTq", "wupT_h"), ("wdTq", "wdT_h")):
            if tap in self.taps:
                @block.gpsimd
                def _(gpsimd):
                    self.stc.inc(gpsimd.dma_start(out=self.taps[tap], in_=d[src]), 16)
                self.w("gpsimd")
                self.barrier()
        self._p6t_ctx.close()
        self._p6_ctx.close()

    def p7_upgate(self):
        nc, block, d = self.nc, self.block, self.dram
        ld, vec, pe, stc = self.ld, self.vec, self.pe, self.stc
        ctx = ExitStack()
        sb = lambda *a: ctx.enter_context(nc.sbuf_tensor(*a))
        LHB = sb("LHB", [128, 2048], F32)
        UPC = sb("UPC", [128, 2048], F32)
        TMPC = sb("TMPC", [128, 2], F32)
        UPP = ctx.enter_context(nc.psum_tensor("UPP7", [128, 4096], F32))
        pe_ring = Ring(); vec_ring = Ring(); st_ring = Ring()
        for wi, (wh, dsth, gcol) in enumerate((("wupT_h", "up_h", 1), ("wgateT_h", "gate_h", 2))):
            for ic in range(NIC):
                par = len(pe_ring.hist) % 2
                lhb = LHB[:, par * 1024:(par + 1) * 1024]
                pe_prev = pe_ring.old(2)
                lhbs = self.sem(f"lhb{par}")
                @block.sync
                def _(sync):
                    if pe_prev is not None:
                        sync.wait_ge(pe.h, pe_prev)
                    for hc in range(NDC):
                        lhbs.inc(sync.dma_start(out=lhb[:, hc * 128:(hc + 1) * 128],
                                                in_=d[wh][hc * 128:(hc + 1) * 128, ic * 128:(ic + 1) * 128]), 16)
                ldn = lhbs.n
                vec_prev = vec_ring.old(2)
                for tq in range(4):
                    pslice = UPP[:, (par * 4 + tq) * 512:(par * 4 + tq + 1) * 512]
                    @block.tensor
                    def _(tensor):
                        tensor.wait_ge(lhbs.h, ldn)
                        if tq == 0 and vec_prev is not None:
                            tensor.wait_ge(vec.h, vec_prev)
                        ins = None
                        for hc in range(NDC):
                            ins = tensor.matmul(pslice, lhb[:, hc * 128:(hc + 1) * 128],
                                                self.BIGA[:, hc * T + tq * 512: hc * T + (tq + 1) * 512],
                                                start=(hc == 0), stop=(hc == NDC - 1))
                        pe.inc(ins)
                pe_ring.push(pe.n)
                pen = pe.n
                st_prev = st_ring.old(1)
                stu = self.ssem("st_upc")
                @block.vector
                def _(vector):
                    vector.wait_ge(pe.h, pen)
                    if st_prev is not None:
                        vector.wait_ge(stu.h, st_prev)
                    src = UPP[:, par * 2048:(par + 1) * 2048]
                    vector.tensor_copy(UPC[:], src)
                    vector.drain()
                    self._absmax_into(vector, UPC[:], self.GMX[:, gcol:gcol + 1], TMPC[:, 0:1])
                    vec.inc(vector.tensor_copy(TMPC[:, 1:2], TMPC[:, 0:1]))
                vec_ring.push(vec.n)
                vecn = vec.n
                @block.gpsimd
                def _(gpsimd):
                    gpsimd.wait_ge(vec.h, vecn)
                    stu.inc(gpsimd.dma_start(out=d[dsth][ic * 128:(ic + 1) * 128, :], in_=UPC[:]), 16)
                st_ring.push(stu.n)
        self.w("vector"); self.w("gpsimd")
        self.barrier()
        for tap, src in (("up", "up_h"), ("gate", "gate_h")):
            if tap in self.taps:
                @block.gpsimd
                def _(gpsimd):
                    self.stc.inc(gpsimd.dma_start(out=self.taps[tap], in_=d[src]), 16)
                self.w("gpsimd")
                self.barrier()
        ctx.close()
        self._allreduce_scales(1, 1, 2, 2)

    def p8_stageB(self):
        nc, block, d = self.nc, self.block, self.dram
        ld, vec, scl, stc = self.ld, self.vec, self.scl, self.stc
        ctx = ExitStack()
        sb = lambda *a: ctx.enter_context(nc.sbuf_tensor(*a))
        GCH = sb("GCH", [128, 2 * T], F32)
        GQt = sb("GQ", [128, 2 * T], F32)
        SGt = sb("SG", [128, 2 * T], F32)
        G2t = sb("G2", [128, 2 * T], F32)
        TMPC = sb("TMPC8", [128, 2], F32)
        gq_ring = Ring()
        for ic in range(NIC):
            par = ic % 2
            gch = GCH[:, par * T:(par + 1) * T]
            GQ = GQt[:, par * T:(par + 1) * T]
            SG = SGt[:, par * T:(par + 1) * T]
            G2 = G2t[:, par * T:(par + 1) * T]
            vprev = getattr(self, "_p8_vec", {}).get(ic - 2)
            gchs = self.sem(f"gch{par}")
            @block.sync
            def _(sync):
                if vprev is not None:
                    sync.wait_ge(vec.h, vprev)
                gchs.inc(sync.dma_start(out=gch, in_=d["gate_h"][ic * 128:(ic + 1) * 128, :]), 16)
            ldn = gchs.n
            st_prev = getattr(self, "_p8_st", {}).get(ic - 2)
            gqf = gq_ring.old(2)
            @block.scalar
            def _(scalar):
                scalar.wait_ge(gchs.h, ldn)
                if gqf is not None:
                    scalar.wait_ge(vec.h, gqf)
                scalar.wait_ge(vec.h, self.p0_vec)
                scl.inc(scalar.activation(GQ, gch, AF.Identity,
                                          bias=self.MAGC[:, 0:1], scale=self.SCLB[:, 5:6]))
            scl0 = scl.n
            @block.vector
            def _(vector):
                vector.wait_ge(scl.h, scl0)
                vector.tensor_scalar(GQ, GQ, MAGIC, 32767.0, ALU.subtract, ALU.min)
                vec.inc(vector.tensor_scalar(GQ, GQ, -32768.0, self.SCLB[:, 3:4], ALU.max, ALU.mult))
            vecn1 = vec.n
            @block.scalar
            def _(scalar):
                scalar.wait_ge(vec.h, vecn1)
                scalar.activation(SG, GQ, AF.Sigmoid)
                scl.inc(scalar.activation(SG, SG, AF.Identity, bias=self.MAGC[:, 0:1], scale=65536.0))
            self._p8_scl = scl.n
            scln = scl.n
            @block.vector
            def _(vector):
                vector.wait_ge(scl.h, scln)
                if st_prev is not None:
                    vector.wait_ge(self.ssem("st_g2").h, st_prev)
                vector.tensor_scalar(SG, SG, MAGIC, 65535.0, ALU.subtract, ALU.min)
                vector.tensor_scalar(SG, SG, 0.0, float(2.0 ** -16), ALU.max, ALU.mult)
                vector.drain()
                vector.tensor_tensor(G2, GQ, SG, ALU.mult)
                vector.drain()
                self._absmax_into(vector, G2, self.GMX[:, 3:4], TMPC[:, 0:1])
                vec.inc(vector.tensor_copy(TMPC[:, 1:2], TMPC[:, 0:1]))
            gq_ring.push(vec.n)
            if not hasattr(self, "_p8_vec"):
                self._p8_vec = {}
            self._p8_vec[ic] = vec.n
            vecn2 = vec.n
            stg = self.ssem("st_g2")
            @block.gpsimd
            def _(gpsimd):
                gpsimd.wait_ge(vec.h, vecn2)
                stg.inc(gpsimd.dma_start(out=d["g2_h"][ic * 128:(ic + 1) * 128, :], in_=G2), 16)
            if not hasattr(self, "_p8_st"):
                self._p8_st = {}
            self._p8_st[ic] = stg.n
        self.w("vector"); self.w("scalar"); self.w("gpsimd")
        self.barrier()
        if "g2" in self.taps:
            @block.gpsimd
            def _(gpsimd):
                self.stc.inc(gpsimd.dma_start(out=self.taps["g2"], in_=d["g2_h"]), 16)
            self.w("gpsimd")
            self.barrier()
        ctx.close()
        self._allreduce_scales(2, 3, 1, 6)

    def p9_stageC(self):
        nc, block, d = self.nc, self.block, self.dram
        ld, vec, scl, stc = self.ld, self.vec, self.scl, self.stc
        ctx = ExitStack()
        sb = lambda *a: ctx.enter_context(nc.sbuf_tensor(*a))
        ACH = sb("ACH", [128, 2 * T], F32)
        BCH = sb("BCH", [128, 2 * T], F32)
        AQt = sb("AQ", [128, 2 * T], F32)
        BQt = sb("BQ9", [128, 2 * T], F32)
        TMPC = sb("TMPC9", [128, 2], F32)
        aq_ring = Ring()
        for ic in range(NIC):
            par = ic % 2
            ach = ACH[:, par * T:(par + 1) * T]
            bch = BCH[:, par * T:(par + 1) * T]
            AQ = AQt[:, par * T:(par + 1) * T]
            BQ = BQt[:, par * T:(par + 1) * T]
            vprev = getattr(self, "_p9_vec", {}).get(ic - 2)
            achs = self.sem(f"ach{par}")
            @block.sync
            def _(sync):
                if vprev is not None:
                    sync.wait_ge(vec.h, vprev)
                achs.inc(sync.dma_start(out=ach, in_=d["g2_h"][ic * 128:(ic + 1) * 128, :]), 16)
                achs.inc(sync.dma_start(out=bch, in_=d["up_h"][ic * 128:(ic + 1) * 128, :]), 16)
            ldn = achs.n
            st_prev = getattr(self, "_p9_st", {}).get(ic - 2)
            aqf = aq_ring.old(2)
            @block.scalar
            def _(scalar):
                scalar.wait_ge(achs.h, ldn)
                if aqf is not None:
                    scalar.wait_ge(vec.h, aqf)
                if st_prev is not None:
                    scalar.wait_ge(self.ssem("st_o").h, st_prev)
                scalar.activation(AQ, ach, AF.Identity, bias=self.MAGC[:, 0:1], scale=self.SCLB[:, 7:8])
                scl.inc(scalar.activation(BQ, bch, AF.Identity, bias=self.MAGC[:, 0:1], scale=self.SCLB[:, 4:5]))
            scl0 = scl.n
            @block.vector
            def _(vector):
                vector.wait_ge(scl.h, scl0)
                if st_prev is not None:
                    vector.wait_ge(self.ssem("st_o").h, st_prev)
                vector.tensor_scalar(AQ, AQ, MAGIC, 32767.0, ALU.subtract, ALU.min)
                vector.tensor_scalar(AQ, AQ, -32768.0, self.SCLB[:, 6:7], ALU.max, ALU.mult)
                vector.tensor_scalar(BQ, BQ, MAGIC, 32767.0, ALU.subtract, ALU.min)
                vector.tensor_scalar(BQ, BQ, -32768.0, self.SCLB[:, 2:3], ALU.max, ALU.mult)
                vector.drain()
                vector.tensor_tensor(AQ, AQ, BQ, ALU.mult)
                vector.drain()
                self._absmax_into(vector, AQ, self.GMX[:, 4:5], TMPC[:, 0:1])
                vec.inc(vector.tensor_copy(TMPC[:, 1:2], TMPC[:, 0:1]))
            aq_ring.push(vec.n)
            if not hasattr(self, "_p9_vec"):
                self._p9_vec = {}
            self._p9_vec[ic] = vec.n
            vecn = vec.n
            sto = self.ssem("st_o")
            @block.gpsimd
            def _(gpsimd):
                gpsimd.wait_ge(vec.h, vecn)
                sto.inc(gpsimd.dma_start(out=d["o_h"][ic * 128:(ic + 1) * 128, :], in_=AQ), 16)
            if not hasattr(self, "_p9_st"):
                self._p9_st = {}
            self._p9_st[ic] = sto.n
        self.w("vector"); self.w("gpsimd")
        self.barrier()
        if "o" in self.taps:
            @block.gpsimd
            def _(gpsimd):
                self.stc.inc(gpsimd.dma_start(out=self.taps["o"], in_=d["o_h"]), 16)
            self.w("gpsimd")
            self.barrier()
        ctx.close()
        self._allreduce_scales(3, 4, 1, 8)

    def p10_down(self):
        nc, block, d = self.nc, self.block, self.dram
        ld, vec, scl, pe, stc = self.ld, self.vec, self.scl, self.pe, self.stc
        ctx = ExitStack()
        sb = lambda *a: ctx.enter_context(nc.sbuf_tensor(*a))
        OQB = sb("OQB", [128, 2 * 512], F32)
        OQD = sb("OQD", [128, 2 * 512], F32)
        WDB = sb("WDB", [128, 2 * 1024], F32)
        YC = sb("YC10", [128, D], F32)
        DB = sb("DB10", [128, D], F32)
        JUNK = sb("JUNK10", [128, D], F32)
        COLS = sb("COLS10", [128, 8], F32)
        OUTC = sb("OUTC", [128, D], F32)
        DP = ctx.enter_context(nc.psum_tensor("DP10", [128, 4096], F32))
        for tcg in range(4):
            for ic in range(NIC):
                par = ic % 2
                oqb = OQB[:, par * 512:(par + 1) * 512]
                oqd = OQD[:, par * 512:(par + 1) * 512]
                wdb = WDB[:, par * 1024:(par + 1) * 1024]
                pe_prev = getattr(self, "_p10_pe", {}).get((tcg, ic - 2))
                oqbs = self.sem(f"oqb{par}")
                @block.sync
                def _(sync):
                    if pe_prev is not None:
                        sync.wait_ge(pe.h, pe_prev)
                    oqbs.inc(sync.dma_start(out=oqb, in_=d["o_h"][ic * 128:(ic + 1) * 128,
                                                                  tcg * 512:(tcg + 1) * 512]), 16)
                    oqbs.inc(sync.dma_start(out=wdb, in_=d["wdT_h"][ic * 128:(ic + 1) * 128, :]), 16)
                ldn = oqbs.n
                @block.vector
                def _(vector):
                    vector.wait_ge(oqbs.h, ldn)
                    if pe_prev is not None:
                        vector.wait_ge(pe.h, pe_prev)
                    self._qdq3(vector, oqd, oqb, self.SCLB[:, 8:9], self.SCLB[:, 9:10])
                    vector.drain()
                    vec.inc(vector.tensor_copy(self.MCOL[:, 1:2], self.MCOL[:, 0:1]))
                vecn = vec.n
                vec_evac = getattr(self, "_p10_evac", {}).get(tcg - 1)
                @block.tensor
                def _(tensor):
                    tensor.wait_ge(vec.h, vecn)
                    if ic == 0 and vec_evac is not None:
                        tensor.wait_ge(vec.h, vec_evac)
                    ins = None
                    for tcl in range(4):
                        for hh in range(2):
                            ins = tensor.matmul(DP[:, (tcl * 2 + hh) * 512:(tcl * 2 + hh + 1) * 512],
                                                oqd[:, tcl * 128:(tcl + 1) * 128],
                                                wdb[:, hh * 512:(hh + 1) * 512],
                                                start=(ic == 0), stop=(ic == NIC - 1))
                    pe.inc(ins)
                if not hasattr(self, "_p10_pe"):
                    self._p10_pe = {}
                self._p10_pe[(tcg, ic)] = pe.n
            pen = pe.n
            for tcl in range(4):
                tc = tcg * 4 + tcl
                p10s = self.sem("p10x")
                @block.sync
                def _(sync):
                    sync.wait_ge(vec.h, getattr(self, "_p10_xc", 0))
                    p10s.inc(sync.dma_start(out=self.CHK[:, 0:D],
                                            in_=d["x2_h"][tc * 128:(tc + 1) * 128, :]), 16)
                ldn_x = p10s.n
                @block.vector
                def _(vector):
                    vector.wait_ge(pe.h, pen)
                    vector.wait_ge(p10s.h, ldn_x)
                    for hh in range(2):
                        vector.tensor_tensor(YC[:, hh * 512:(hh + 1) * 512],
                                             DP[:, (tcl * 2 + hh) * 512:(tcl * 2 + hh + 1) * 512],
                                             self.CHK[:, hh * 512:(hh + 1) * 512], ALU.add)
                    vector.drain()
                    vector.tensor_reduce(COLS[:, 0:1], YC[:], AX.X, ALU.add)
                    vector.drain()
                    vector.tensor_scalar_mul(COLS[:, 1:2], COLS[:, 0:1], float(1.0 / D))
                    vector.drain()
                    vec.inc(vector.tensor_scalar_sub(DB[:], YC[:], COLS[:, 1:2]))
                self._p10_xc = vec.n
                vecn1 = vec.n
                @block.scalar
                def _(scalar):
                    scalar.wait_ge(vec.h, vecn1)
                    scl.inc(scalar.activation(JUNK[:], DB[:], AF.Square, accum_out=COLS[:, 2:3]))
                scln1 = scl.n
                @block.vector
                def _(vector):
                    vector.wait_ge(scl.h, scln1)
                    vec.inc(vector.tensor_scalar(COLS[:, 3:4], COLS[:, 2:3], float(1.0 / D), float(EPS), ALU.mult, ALU.add))
                vecn2 = vec.n
                @block.scalar
                def _(scalar):
                    scalar.wait_ge(vec.h, vecn2)
                    scl.inc(scalar.sqrt(COLS[:, 4:5], COLS[:, 3:4]))
                scln2 = scl.n
                st_prev = getattr(self, "_p10_st", None)
                @block.vector
                def _(vector):
                    vector.wait_ge(scl.h, scln2)
                    vector.reciprocal(COLS[:, 5:6], COLS[:, 4:5])
                    vector.drain()
                    if st_prev is not None:
                        vector.wait_ge(self.ssem("st_out").h, st_prev)
                    vector.tensor_scalar(OUTC[:], DB[:], COLS[:, 5:6], None, ALU.mult)
                    vector.drain()
                    vec.inc(vector.tensor_copy(COLS[:, 7:8], COLS[:, 6:7]))
                vecn3 = vec.n
                stq = self.ssem("st_out")
                @block.gpsimd
                def _(gpsimd):
                    gpsimd.wait_ge(vec.h, vecn3)
                    stq.inc(gpsimd.dma_start(out=d["out"][tc * 128:(tc + 1) * 128, :], in_=OUTC[:]), 16)
                self._p10_st = stq.n
            if not hasattr(self, "_p10_evac"):
                self._p10_evac = {}
            self._p10_evac[tcg] = vec.n
        self.w("vector"); self.w("scalar"); self.w("gpsimd")
        self.barrier()
        ctx.close()

    def finish(self):
        block = self.block
        stn = self.stc.n
        store_waits = [(ss.h, ss.n) for ss in getattr(self, "_store_sems", set()) if ss.n]
        @block.gpsimd
        def _(gpsimd):
            if stn:
                gpsimd.wait_ge(self.stc.h, stn)
            for h, n in store_waits:
                gpsimd.wait_ge(h, n)


def host_inputs(L0_attn_x, L0_attn_ln_w, L0_attn_ln_b, L0_attn_Wq, L0_attn_bq,
                L0_attn_Wk, L0_attn_bk, L0_attn_Wv, L0_attn_bv, L0_attn_Wo, L0_attn_bo,
                L0_ffn_w_gate, L0_ffn_w_up, L0_ffn_w_down):
    f = np.float32
    perm = np.empty(D, np.int64)
    for h in range(NH):
        for i in range(32):
            perm[h * 64 + i] = h * 64 + 2 * i
            perm[h * 64 + 32 + i] = h * 64 + 2 * i + 1
    Wq = np.asarray(L0_attn_Wq, f)[perm, :]
    Wk = np.asarray(L0_attn_Wk, f)[perm, :]
    com = {
        "wqT": np.ascontiguousarray(Wq.T),
        "wkT": np.ascontiguousarray(Wk.T),
        "wvT": np.ascontiguousarray(np.asarray(L0_attn_Wv, f).T),
        "woT": np.ascontiguousarray(np.asarray(L0_attn_Wo, f).T),
        "bqc": np.ascontiguousarray(np.asarray(L0_attn_bq, f)[perm].reshape(NDC, 128).T),
        "bkc": np.ascontiguousarray(np.asarray(L0_attn_bk, f)[perm].reshape(NDC, 128).T),
        "bvc": np.ascontiguousarray(np.asarray(L0_attn_bv, f).reshape(NDC, 128).T),
        "bob": np.ascontiguousarray(np.broadcast_to(np.asarray(L0_attn_bo, f), (128, D))),
        "lnwb": np.ascontiguousarray(np.broadcast_to(np.asarray(L0_attn_ln_w, f), (128, D))),
        "lnbb": np.ascontiguousarray(np.broadcast_to(np.asarray(L0_attn_ln_b, f), (128, D))),
        "idn": np.eye(128, dtype=f),
        "onesr": np.ones((1, 128), f),
        "wup": np.ascontiguousarray(np.asarray(L0_ffn_w_up, f)),
        "wgate": np.ascontiguousarray(np.asarray(L0_ffn_w_gate, f)),
        "wdown": np.ascontiguousarray(np.asarray(L0_ffn_w_down, f)),
    }
    o2 = np.zeros((128, 2), f)
    o2[0:64, 0] = 1.0
    o2[64:128, 1] = 1.0
    com["ones2"] = o2
    inv = (1.0 / (f(10000.0) ** (np.arange(0, HD, 2, dtype=f) / f(HD)))).astype(f)
    t = np.arange(T, dtype=f)
    fr = np.outer(t, inv).astype(f)
    c = np.cos(fr).astype(f).T
    s = np.sin(fr).astype(f).T
    com["c2t"] = np.ascontiguousarray(np.concatenate([c, c, c, c], 0))
    com["s2t"] = np.ascontiguousarray(np.concatenate([-s, s, -s, s], 0))
    x = np.asarray(L0_attn_x, f)
    return [dict(com, xin=np.ascontiguousarray(x[b])) for b in range(B)]


def build(dbg=(), upto=99):
    kb = KB(dbg)
    kb.p0_consts()
    kb.p1_ln1()
    if upto >= 2:
        kb.p2_v()
    if upto >= 3:
        if upto >= 6:
            kb._p3_interleave_dequant = True
            kb._p6_init()
        kb.p3_attn()
    if upto >= 4:
        if upto >= 6:
            kb._p4_interleave_tr = True
            kb._p6t_init()
        kb.p4_wo()
    if upto >= 6:
        kb.p6_dequant()
    if upto >= 7:
        kb.p7_upgate()
    if upto >= 8:
        kb.p8_stageB()
    if upto >= 9:
        kb.p9_stageC()
    if upto >= 10:
        kb.p10_down()
    kb.finish()
    return kb


def kernel(**inputs):
    kb = build()
    in_maps = host_inputs(**inputs)
    res = run_bass_kernel_spmd(kb.nc, in_maps, core_ids=list(range(B)))
    return np.stack([np.asarray(r["out"]) for r in res.results], axis=0)
